# version 05 (final)
# speedup vs baseline: 1.4675x; 1.0512x over previous
import numpy as np
from contextlib import ExitStack

import concourse.bass as bass
import concourse.mybir as mybir
from concourse.bass import IndirectOffsetOnAxis

F32 = mybir.dt.float32
BF16 = mybir.dt.bfloat16
U32 = mybir.dt.uint32
AF = mybir.ActivationFunctionType
ALU = mybir.AluOpType
AX = mybir.AxisListType

NEG = -1.0e30


class Geo:
    def __init__(self, n_batch=2, cpb=4, n_slots=16, d=1024):
        self.n_batch = n_batch
        self.cpb = cpb
        self.n_cores = n_batch * cpb
        self.n_slots = n_slots
        self.groups = n_slots // 4
        self.T_local = n_slots * 128
        self.n_tiles = cpb * n_slots
        self.S = self.n_tiles * 128
        self.d = d
        self.dchunks = d // 128

    def chunks_of_group(self, m):
        return 4 * self.cpb * m + 4 * self.cpb


class Sem:
    __slots__ = ("h", "n", "name")

    def __init__(self, h, name):
        self.h = h
        self.n = 0
        self.name = name

    def tick(self, k=1):
        self.n += k
        return self.n


class K:
    ENGS = ("sync", "gpsimd", "vector", "scalar", "tensor")

    def __init__(self, name="k"):
        self.nc = bass.Bass("TRN2", target_bir_lowering=False, debug=False)
        self.ctx = ExitStack()
        self.ops = {e: [] for e in self.ENGS}
        self._consts = {}
        self._cpool = None
        self._cpool_n = 0
        self.name = name
        self.vt = self.sem("vt")
        self.st = self.sem("st")
        self.gt = self.sem("gt")
        self.tt = self.sem("tt")

    def sem(self, name):
        return Sem(self.ctx.enter_context(self.nc.semaphore(name)), name)

    def sb(self, name, shape, dt):
        return self.ctx.enter_context(self.nc.sbuf_tensor(name, shape, dt))

    def psum(self, name, shape, dt=F32):
        return self.ctx.enter_context(self.nc.psum_tensor(name, shape, dt))

    def inp(self, name, shape, dt):
        return self.nc.dram_tensor(name, shape, dt, kind="ExternalInput").ap()

    def outp(self, name, shape, dt):
        return self.nc.dram_tensor(name, shape, dt, kind="ExternalOutput").ap()

    def dram(self, name, shape, dt, addr_space="Local"):
        return self.nc.dram_tensor(name, shape, dt, addr_space=addr_space).ap()

    def const(self, val):
        if self._cpool is None:
            self._cpool = self.sb("cpool", [128, 16], F32)
        val = float(val)
        if val not in self._consts:
            i = self._cpool_n
            assert i < 16
            self._cpool_n += 1
            cell = self._cpool[:, i:i + 1]
            self.nc.const_aps.register(F32, val, cell)
            self._consts[val] = cell
        return val

    def emit(self, eng, build, waits=(), inc=None, inck=1):
        wl = []
        for w in waits:
            if isinstance(w, tuple):
                s, v = w
            else:
                s, v = w, w.n
            if v > 0:
                wl.append((s.h, v))
        n = None
        if inc is not None:
            n = inc.tick(inck)
        self.ops[eng].append((wl, build, (inc.h if inc else None), inck))
        return n

    def v(self, build, waits=(), ):
        return self.emit("vector", build, waits=list(waits) + [self.vt], inc=self.vt)

    def s(self, build, waits=()):
        return self.emit("scalar", build, waits=list(waits) + [self.st], inc=self.st)

    def g(self, build, waits=()):
        return self.emit("gpsimd", build, waits=list(waits) + [self.gt], inc=self.gt)

    def t(self, build, waits=(), milestone=False):
        return self.emit("tensor", build, waits=waits,
                         inc=(self.tt if milestone else None))

    def dma(self, eng, out, in_, sem, waits=()):
        return self.emit(eng, lambda e, o=out, i=in_: e.dma_start(out=o, in_=i),
                         waits=waits, inc=sem, inck=16)

    def close(self):
        self.ctx.close()

    def finalize(self):
        nc = self.nc
        if self._cpool is not None:
            cs = self.sem("const_init")
            new = []
            for val, cell in self._consts.items():
                new.append(([], (lambda e, c=cell, v=val: e.memset(c, v)), None, 1))
            if new:
                new[-1] = (new[-1][0], new[-1][1], cs.h, 1)
            self.ops["gpsimd"] = new + self.ops["gpsimd"]
            for eng in ("scalar", "vector", "tensor"):
                if self.ops[eng]:
                    wl, build, inch, inck = self.ops[eng][0]
                    self.ops[eng][0] = (wl + [(cs.h, 1)], build, inch, inck)

        block = self.ctx.enter_context(nc.Block())
        for eng in self.ENGS:
            lst = self.ops[eng]
            if not lst:
                continue

            def runner(e, lst=lst):
                for wl, build, inch, inck in lst:
                    for sh, v in wl:
                        e.wait_ge(sh, v)
                    inst = build(e)
                    if inch is not None and inst is not None:
                        inst.then_inc(inch, inck)
            getattr(block, eng)(runner)
        return nc



def host_causal_masks(cc):
    s = np.arange(128)[:, None]
    q = np.arange(128)[None, :]
    out = np.zeros((4, 128, 128), np.float32)
    for p in range(4):
        out[p] = np.where(128 * (p - cc) + s > q, NEG, 0.0)
    return out


def host_rope_head64(S):
    inv = 1.0 / (10000.0 ** (np.arange(32, dtype=np.float64) * 2.0 / 64.0))
    t = np.arange(S, dtype=np.float64)
    ang = inv[:, None] * t[None, :]
    c = np.cos(ang).astype(np.float32)
    s = np.sin(ang).astype(np.float32)
    return (np.concatenate([c, c, c, c], 0), np.concatenate([s, s, s, s], 0))


def host_rope_d1024(S):
    inv = 1.0 / (10000.0 ** (np.arange(512, dtype=np.float64) / 512.0))
    t = np.arange(S, dtype=np.float64)
    ang = inv[:, None] * t[None, :]
    return (np.cos(ang).astype(np.float32), np.sin(ang).astype(np.float32))


def host_swap64():
    n = 128
    P = np.zeros((n, n), np.float32)
    for b0 in range(0, n, 64):
        P[b0:b0 + 32, b0 + 32:b0 + 64] = -np.eye(32)
        P[b0 + 32:b0 + 64, b0:b0 + 32] = np.eye(32)
    return P.T.copy()


def host_cosine_pos(S, dim=64):
    EPS = float(np.finfo(np.float32).eps)
    idx = np.arange(S, dtype=np.float32)
    sc = 1.0 / (10000.0 ** (np.arange(0, dim, 2, dtype=np.float32) / dim) + EPS)
    ang = idx[:, None] * sc[None, :]
    pos = np.stack([np.sin(ang), np.cos(ang)], axis=-1).reshape(S, dim)
    return pos.astype(np.float32)


def shard_tiles(x, geo: Geo, core):
    b, cc = divmod(core, geo.cpb)
    xb = x[b].reshape(geo.n_tiles, 128, -1)
    return np.ascontiguousarray(xb[cc::geo.cpb])


def unshard_tiles(parts, geo: Geo):
    d = parts[0].size // (geo.n_slots * 128)
    out = np.empty((geo.n_batch, geo.n_tiles, 128, d), parts[0].dtype)
    for core, p in enumerate(parts):
        b, cc = divmod(core, geo.cpb)
        out[b, cc::geo.cpb] = p.reshape(geo.n_slots, 128, d)
    return out.reshape(geo.n_batch, geo.S, d)



class Common:

    def __init__(self, k: K, d=1024):
        self.d = d
        self.ident = k.sb("ident", [128, 128], BF16)
        self.identf = k.sb("identf", [128, 128], F32)
        self.iod = k.sb("iod", [128, 128], F32)
        self.ones1 = k.sb("ones1", [1, 128], F32)
        self.ones1b = k.sb("ones1b", [1, 128], BF16)
        self.lsum = k.sb("ln_sum", [128, 1], F32)
        self.lssq = k.sb("ln_ssq", [128, 1], F32)
        self.lmean = k.sb("ln_mean", [128, 1], F32)
        self.lvar = k.sb("ln_var", [128, 1], F32)
        self.lrstd = k.sb("ln_rstd", [128, 1], F32)
        self.lnmr = k.sb("ln_nmr", [128, 1], F32)
        self.sqd = k.sb("ln_sqd", [128, d], F32)

    def emit_init(self, k: K):
        k.g(lambda e: e.iota(out=self.iod[:], pattern=[[-1, 128]], base=0,
                             channel_multiplier=1, allow_small_or_imprecise_dtypes=True))
        k.g(lambda e: e.memset(self.ones1[:], 1.0))
        k.g(lambda e: e.memset(self.ones1b[:], 1.0))
        k.v(lambda e: e.tensor_scalar(out=self.ident[:], in0=self.iod[:], scalar1=0.0,
                                      scalar2=None, op0=ALU.is_equal), waits=[k.gt])
        k.v(lambda e: e.tensor_scalar(out=self.identf[:], in0=self.iod[:], scalar1=0.0,
                                      scalar2=None, op0=ALU.is_equal), waits=[k.gt])


def emit_ln(k: K, cm: Common, x_ap, xn_ap, d, eps=1e-5, x_waits=(),
            gain_b=None, bias_b=None, out_f32=None):
    xw = list(x_waits)
    k.const(eps)
    k.v(lambda e: e.tensor_reduce(out=cm.lsum[:], in_=x_ap, axis=AX.X, op=ALU.add),
        waits=xw)
    k.s(lambda e: e.activation(out=cm.sqd[:, :d], in_=x_ap, func=AF.Square,
                               accum_out=cm.lssq[:]), waits=xw + [k.vt])
    k.v(lambda e: e.tensor_scalar_mul(cm.lmean[:], cm.lsum[:], 1.0 / d))
    k.v(lambda e: e.tensor_scalar_mul(cm.lvar[:], cm.lssq[:], 1.0 / d), waits=[k.st])
    k.v(lambda e: e.tensor_tensor(out=cm.lnmr[:], in0=cm.lmean[:], in1=cm.lmean[:],
                                  op=ALU.mult))
    k.v(lambda e: e.tensor_sub(cm.lvar[:], cm.lvar[:], cm.lnmr[:]))
    k.s(lambda e: e.activation(out=cm.lrstd[:], in_=cm.lvar[:], func=AF.Sqrt, bias=eps),
        waits=[k.vt])
    k.v(lambda e: e.reciprocal(cm.lrstd[:], cm.lrstd[:]), waits=[k.st])
    k.v(lambda e: e.scalar_tensor_tensor(out=cm.lnmr[:], in0=cm.lmean[:], scalar=-1.0,
                                         in1=cm.lrstd[:], op0=ALU.mult, op1=ALU.mult))
    if gain_b is None:
        tgt = xn_ap if out_f32 is None else out_f32
        k.v(lambda e: e.scalar_tensor_tensor(out=tgt, in0=x_ap, scalar=cm.lmean[:],
                                             in1=cm.lrstd[:].to_broadcast([128, d]), op0=ALU.subtract,
                                             op1=ALU.mult))
        if out_f32 is not None and xn_ap is not None:
            k.v(lambda e: e.tensor_copy(out=xn_ap, in_=out_f32))
        return "vt"
    else:
        k.v(lambda e: e.scalar_tensor_tensor(out=cm.sqd[:, :d], in0=x_ap,
                                             scalar=cm.lmean[:],
                                             in1=cm.lrstd[:].to_broadcast([128, d]),
                                             op0=ALU.subtract, op1=ALU.mult))
        k.v(lambda e: e.tensor_mul(cm.sqd[:, :d], cm.sqd[:, :d], gain_b[:, :]))
        k.v(lambda e: e.tensor_tensor(out=xn_ap, in0=cm.sqd[:, :d], in1=bias_b[:, :],
                                      op=ALU.add))
        if out_f32 is not None:
            k.v(lambda e: e.tensor_copy(out=out_f32, in_=xn_ap))
        return "vt"


def emit_bcast_row(k: K, cm: Common, row_ap, ps_ap, sb_ap, nrows, waits=()):
    k.t(lambda e: e.matmul(ps_ap, cm.ones1[0:1, 0:nrows], row_ap, start=True, stop=True),
        waits=waits, milestone=True)
    k.v(lambda e: e.tensor_copy(out=sb_ap, in_=ps_ap), waits=[k.tt])


def emit_transpose(k: K, cm: Common, src_bf16, dst_bf16, ps_bf16, waits=()):
    k.t(lambda e: e.transpose(out=ps_bf16, in_=src_bf16, identity=cm.ident[:]),
        waits=waits, milestone=True)
    k.v(lambda e: e.tensor_copy(out=dst_bf16, in_=ps_bf16), waits=[k.tt])



class AttnCfg:
    def __init__(self, nq=16, nkv=16, dh=64, causal=True, scale=0.125,
                 mode="none", inner_ln=False, fused_qkv=False):
        self.nq = nq
        self.nkv = nkv
        self.dh = dh
        self.causal = causal
        self.scale = scale
        self.mode = mode
        self.inner_ln = inner_ln
        self.fused_qkv = fused_qkv
        self.dq = nq * dh
        self.dkv = nkv * dh


def build_attn_prep(geo: Geo, cfg: AttnCfg, name="prep"):
    k = K(name)
    d = geo.d
    nch = geo.dchunks
    Tl = geo.T_local

    x_in = k.inp("x", [geo.n_slots, 128, d], F32)
    if cfg.fused_qkv:
        w_qkv = k.inp("w_qkv", [d, 3 * d], F32)
    else:
        w_q = k.inp("w_q", [d, cfg.dq], F32)
        w_k = k.inp("w_k", [d, cfg.dkv], F32)
        w_v = k.inp("w_v", [d, cfg.dkv], F32)
    if cfg.inner_ln:
        ln_g = k.inp("ln_g", [d], F32)
        ln_b = k.inp("ln_b", [d], F32)
    if cfg.mode == "rope64":
        ropeC = k.inp("ropeC", [128, Tl], F32)
        ropeS = k.inp("ropeS", [128, Tl], F32)
        swap_in = k.inp("swapT", [128, 128], F32)
    if cfg.mode == "pos":
        posT = k.inp("posT", [128, Tl], F32)

    qt_out = k.outp("qt", [cfg.dq, Tl], BF16)
    kt_out = k.outp("kt", [cfg.nkv * 64, Tl], BF16)
    v_out = k.outp("v", [cfg.nkv, 128, geo.n_slots * 65], BF16)

    cm = Common(k, d)
    cm.emit_init(k)

    wsem = k.sem("wld")
    wq_sb = k.sb("wq_sb", [128, nch * cfg.dq], BF16)
    wk_sb = k.sb("wk_sb", [128, nch * cfg.dkv], BF16)
    wv_sb = k.sb("wv_sb", [128, nch * cfg.dkv], BF16)
    for c in range(nch):
        rs = slice(128 * c, 128 * (c + 1))
        if cfg.fused_qkv:
            k.dma("gpsimd", wq_sb[:, c * cfg.dq:(c + 1) * cfg.dq], w_qkv[rs, 0:d], wsem)
            k.dma("gpsimd", wk_sb[:, c * cfg.dkv:(c + 1) * cfg.dkv], w_qkv[rs, d:2 * d], wsem)
            k.dma("gpsimd", wv_sb[:, c * cfg.dkv:(c + 1) * cfg.dkv], w_qkv[rs, 2 * d:3 * d], wsem)
        else:
            k.dma("gpsimd", wq_sb[:, c * cfg.dq:(c + 1) * cfg.dq], w_q[rs, :], wsem)
            k.dma("gpsimd", wk_sb[:, c * cfg.dkv:(c + 1) * cfg.dkv], w_k[rs, :], wsem)
            k.dma("gpsimd", wv_sb[:, c * cfg.dkv:(c + 1) * cfg.dkv], w_v[rs, :], wsem)
    wld_total = wsem.n

    gain_b = bias_b = None
    if cfg.inner_ln:
        gsem = k.sem("gld")
        g_row = k.sb("g_row", [1, d], F32)
        b_row = k.sb("b_row", [1, d], F32)
        k.dma("sync", g_row[:], ln_g[None, :], gsem)
        k.dma("sync", b_row[:], ln_b[None, :], gsem)
        gld_total = gsem.n
        gain_b = k.sb("gain_b", [128, d], BF16)
        bias_b = k.sb("bias_b", [128, d], BF16)
        psb = k.psum("ps_bc", [128, 512])
        for half in range((d + 511) // 512):
            hw = min(512, d - half * 512)
            sl = slice(half * 512, half * 512 + hw)
            k.t(lambda e, sl=sl, hw=hw: e.matmul(psb[:, 0:hw], cm.ones1[:], g_row[0:1, sl],
                                                 start=True, stop=True),
                waits=[(gsem, gld_total), k.vt], milestone=True)
            k.v(lambda e, sl=sl, hw=hw: e.tensor_copy(out=gain_b[:, sl], in_=psb[:, 0:hw]),
                waits=[k.tt])
            k.t(lambda e, sl=sl, hw=hw: e.matmul(psb[:, 0:hw], cm.ones1[:], b_row[0:1, sl],
                                                 start=True, stop=True),
                waits=[k.vt], milestone=True)
            k.v(lambda e, sl=sl, hw=hw: e.tensor_copy(out=bias_b[:, sl], in_=psb[:, 0:hw]),
                waits=[k.tt])

    rsem = k.sem("rld")
    rope_total = 0
    if cfg.mode == "rope64":
        ropeC_sb = k.sb("ropeC_sb", [128, Tl], F32)
        ropeS_sb = k.sb("ropeS_sb", [128, Tl], F32)
        swap_sb = k.sb("swap_sb", [128, 128], BF16)
        k.dma("sync", ropeC_sb[:], ropeC[:, :], rsem)
        k.dma("sync", ropeS_sb[:], ropeS[:, :], rsem)
        k.dma("gpsimd", swap_sb[:], swap_in[:, :], wsem)
        rope_total = rsem.n
    if cfg.mode == "pos":
        posT_sb = k.sb("posT_sb", [128, Tl], F32)
        k.dma("sync", posT_sb[:], posT[:, :], rsem)
        rope_total = rsem.n

    xs = [k.sem("xs0"), k.sem("xs1")]
    x_sb = [k.sb(f"x_sb{i}", [128, d], F32) for i in range(2)]
    xin_sb = k.sb("xin_sb", [128, d], F32) if cfg.inner_ln else None
    xn_sb = k.sb("xn_sb", [128, d], BF16)
    xnT = k.sb("xnT", [128, nch * 512], BF16)
    ps_tr = k.psum("ps_tr", [128, 128], BF16)
    ps_p = [k.psum(f"ps_p{i}", [128, 512]) for i in range(2)]
    ps_r = k.psum("ps_r", [128, 512])
    ps_v = k.psum("ps_v", [128, 512])
    stg = [k.sb(f"stg{i}", [128, 512], BF16) for i in range(2)]
    qs = [k.sem("qs0"), k.sem("qs1")]
    t1 = k.sb("rope_t1", [128, 512], BF16)
    t2 = k.sb("rope_t2", [128, 512], BF16)
    v_sb = [k.sb(f"v_sb{i}", [128, cfg.nkv * 65], BF16) for i in range(2)]
    vq = [k.sem("vq0"), k.sem("vq1")]
    stg_i = [0]

    k.dma("sync", x_sb[0][:], x_in[0, :, :], xs[0])

    for m in range(geo.groups):
        for kk in range(4):
            ti = 4 * m + kk
            buf = ti % 2
            if ti + 1 < geo.n_slots:
                k.dma("sync", x_sb[(ti + 1) % 2][:], x_in[ti + 1, :, :], xs[(ti + 1) % 2],
                      waits=[k.vt, k.st])
            ld_waits = [(xs[buf], 16 * (ti // 2 + 1))]
            if cfg.inner_ln:
                emit_ln(k, cm, x_sb[buf][:], None, d, x_waits=ld_waits,
                        out_f32=xin_sb[:])
                ln_done = emit_ln(k, cm, xin_sb[:], xn_sb[:], d, x_waits=[k.vt],
                                  gain_b=gain_b, bias_b=bias_b)
            else:
                ln_done = emit_ln(k, cm, x_sb[buf][:], xn_sb[:], d, x_waits=ld_waits)
            ln_sem = k.vt if ln_done == "vt" else k.st
            for c in range(nch):
                k.t(lambda e, c=c: e.transpose(
                    out=ps_tr[:], in_=xn_sb[:, 128 * c:128 * (c + 1)],
                    identity=cm.ident[:]),
                    waits=[(ln_sem, ln_sem.n), k.vt], milestone=True)
                k.v(lambda e, c=c, kk=kk: e.tensor_copy(
                    out=xnT[:, c * 512 + kk * 128: c * 512 + kk * 128 + 128],
                    in_=ps_tr[:]), waits=[k.tt])
            vbuf = ti % 2
            for vc in range((cfg.dkv + 511) // 512):
                fmv = min(512, cfg.dkv - vc * 512)
                for c in range(nch):
                    k.t(lambda e, c=c, kk=kk, vc=vc, fmv=fmv: e.matmul(
                        ps_v[:, 0:fmv],
                        xnT[:, c * 512 + kk * 128: c * 512 + kk * 128 + 128],
                        wv_sb[:, c * cfg.dkv + vc * 512: c * cfg.dkv + vc * 512 + fmv],
                        start=(c == 0), stop=(c == nch - 1)),
                        waits=([wsem, k.vt,
                                (vq[vbuf], vq[vbuf].n)] if c == 0 else []),
                        milestone=(c == nch - 1))
                nh_v = fmv // 64
                h0 = vc * 8
                k.v(lambda e, vc=vc, nh_v=nh_v, h0=h0, vbuf=vbuf: e.tensor_copy(
                    out=v_sb[vbuf][:, 65 * h0:65 * (h0 + nh_v)].rearrange(
                        "p (h c) -> p h c", c=65)[:, :, 0:64],
                    in_=ps_v[:, 0:nh_v * 64].rearrange("p (h c) -> p h c", c=64)),
                    waits=[k.tt])
            k.v(lambda e, vbuf=vbuf: e.memset(
                v_sb[vbuf][:].rearrange("p (h c) -> p h c", c=65)[:, :, 64:65], 1.0))
            k.dma("sync", v_out[:, :, ti * 65:(ti + 1) * 65].rearrange("h p c -> p h c"),
                  v_sb[vbuf][:].rearrange("p (h c) -> p h c", c=65), vq[vbuf],
                  waits=[k.vt])

        def proj(dst_out, w_sb, wW, oc, kind, m=m):
            pb = ps_p[oc % 2]
            for c in range(nch):
                k.t(lambda e, c=c, oc=oc, pb=pb, w_sb=w_sb, wW=wW: e.matmul(
                    pb[:], w_sb[:, c * wW + 128 * oc: c * wW + 128 * (oc + 1)],
                    xnT[:, c * 512:(c + 1) * 512],
                    start=(c == 0), stop=(c == nch - 1)),
                    waits=([k.vt, wsem] if c == 0 else []),
                    milestone=(c == nch - 1))
            tt_mm = k.tt.n
            si = stg_i[0] % 2
            stg_i[0] += 1
            sg = stg[si]
            sq = qs[si]
            co = slice(m * 512, (m + 1) * 512)
            if cfg.mode == "rope64" and kind in ("q", "k"):
                k.v(lambda e, pb=pb, co=co: e.tensor_tensor(
                    out=t1[:], in0=pb[:], in1=ropeS_sb[:, co], op=ALU.mult),
                    waits=[(k.tt, tt_mm), (rsem, rope_total)])
                k.t(lambda e: e.matmul(ps_r[:], swap_sb[:], t1[:], start=True, stop=True),
                    waits=[k.vt, (wsem, wsem.n)], milestone=True)
                k.v(lambda e, pb=pb, co=co: e.tensor_tensor(
                    out=t2[:], in0=pb[:], in1=ropeC_sb[:, co], op=ALU.mult))
                k.v(lambda e, sg=sg: e.tensor_tensor(out=sg[:], in0=t2[:], in1=ps_r[:],
                                                     op=ALU.add),
                    waits=[k.tt, (sq, sq.n)])
            elif cfg.mode == "pos" and kind in ("q", "k"):
                k.v(lambda e, pb=pb, co=co, sg=sg: e.tensor_tensor(
                    out=sg[:], in0=pb[:], in1=posT_sb[:, co], op=ALU.add),
                    waits=[(k.tt, tt_mm), (rsem, rope_total), (sq, sq.n)])
            else:
                k.v(lambda e, pb=pb, sg=sg: e.tensor_copy(out=sg[:], in_=pb[:]),
                    waits=[(k.tt, tt_mm), (sq, sq.n)])
            k.dma("sync", dst_out[128 * oc:128 * (oc + 1), m * 512:(m + 1) * 512],
                  sg[:], sq, waits=[k.vt])

        for oc in range(cfg.dq // 128):
            proj(qt_out, wq_sb, cfg.dq, oc, "q")
        for oc in range(cfg.dkv // 128):
            proj(kt_out, wk_sb, cfg.dkv, oc, "k")

    k.emit("sync", lambda e: e.nop(),
           waits=[(qs[0], qs[0].n), (qs[1], qs[1].n), (vq[0], vq[0].n), (vq[1], vq[1].n)])
    return k


def build_attn_main(geo: Geo, cfg: AttnCfg, name="attnB", residual_ln=False,
                    gate=False):
    k = K(name)
    d = geo.d
    nch = geo.dchunks
    Tl = geo.T_local
    S = geo.S
    njall = S // 128
    cpb = geo.cpb
    hpkv = cfg.nq // cfg.nkv

    x_in = k.inp("x", [geo.n_slots, 128, d], F32)
    qt_in = k.inp("qt", [cfg.dq, Tl], BF16)
    kt_in = k.inp("kt", [cfg.nkv * 64, S], BF16)
    v_in = k.inp("v", [cfg.nkv, 128, geo.n_tiles * 65], BF16)
    wo_in = k.inp("wo", [d, d], F32)
    mk_in = k.inp("masks", [cpb, 128, 128], F32)
    y_out = k.outp("y", [geo.n_slots, 128, d], F32)
    if gate:
        g2_in = k.inp("ln2_g", [d], F32)
        b2_in = k.inp("ln2_b", [d], F32)
        ag_in = k.inp("agent_gate", [d, 10], F32)
        ag_out = k.outp("aglast", [1, 10], F32)

    cm = Common(k, d)
    cm.emit_init(k)
    k.const(cfg.scale)

    wsem = k.sem("wld")
    wo_sb = k.sb("wo_sb", [128, nch * d], BF16)
    for c in range(nch):
        k.dma("gpsimd", wo_sb[:, c * d:(c + 1) * d], wo_in[128 * c:128 * (c + 1), :], wsem)
    wld_total = wsem.n

    qsem = k.sem("qld")
    qt_sb = k.sb("qt_sb", [64, cfg.nq * Tl], BF16)
    for h in range(cfg.nq):
        k.dma("sync", qt_sb[:, h * Tl:(h + 1) * Tl], qt_in[64 * h:64 * (h + 1), :], qsem)
    msem = k.sem("mld")
    mask_sb = k.sb("mask_sb", [128, cpb * 128], F32)
    if cfg.causal:
        for p in range(cpb):
            k.dma("sync", mask_sb[:, 128 * p:128 * (p + 1)], mk_in[p, :, :], msem)
    mld_total = msem.n
    qld_total = qsem.n

    kvsem = [k.sem("kv0"), k.sem("kv1")]
    kt_sb = [k.sb(f"kt_sb{i}", [64, S], BF16) for i in range(2)]
    v_sb = [k.sb(f"v_sb{i}", [128, geo.n_tiles * 65], BF16) for i in range(2)]

    def load_kv(kv, waits=()):
        b = kv % 2
        k.dma("gpsimd", kt_sb[b][:], kt_in[64 * kv:64 * (kv + 1), :], kvsem[b],
              waits=waits)
        k.dma("gpsimd", v_sb[b][:], v_in[kv, :, :], kvsem[b], waits=waits)

    load_kv(0)
    tt_end_prev = 0

    ps_s = [k.psum(f"ps_s{i}", [128, 512]) for i in range(2)]
    ps_o = k.psum("ps_o", [128, 512])
    ps_b = k.psum("ps_b", [128, 512])
    exp_sb = [k.sb(f"exp_sb{i}", [128, 512], BF16) for i in range(3)]
    rcp_sb = k.sb("rcp_sb", [1, 512], F32)
    bb_sb = k.sb("bb_sb", [64, 512], F32)
    outT = k.sb("outT", [128, max(nch * Tl, 4 * d + 256 + 2 * nch * 128 + 128)], BF16)

    for kv in range(cfg.nkv):
        b = kv % 2
        if kv + 1 < cfg.nkv:
            load_kv(kv + 1, waits=[(k.tt, tt_end_prev)])
        kv_w = (kvsem[b], 32 * (kv // 2 + 1))
        for hq in range(hpkv):
            h = kv * hpkv + hq
            qrow = 64 * (h % 2)
            qcol0 = h * Tl
            for m in range(geo.groups):
                jmax = geo.chunks_of_group(m) if cfg.causal else njall
                qk_t = {}
                exp_t = {}
                av_t = {}
                geom = {}

                def emit_qk(j, b=b, qrow=qrow, qcol0=qcol0, m=m):
                    r = j - 4 * cpb * m if cfg.causal else -1
                    if cfg.causal and r >= 0:
                        kmin = r // cpb
                        pat = r % cpb
                    else:
                        kmin, pat = 0, -1
                    coff = 128 * kmin
                    fm = 512 - coff
                    jb = j % 2
                    eb = j % 3
                    geom[j] = (coff, fm, eb)
                    w = [kv_w, (qsem, qld_total)]
                    if j - 2 >= 0:
                        w.append((k.st, exp_t[j - 2]))
                    else:
                        w.append((k.vt, k.vt.n))
                    qk_t[j] = k.emit(
                        "tensor",
                        lambda e, b=b, j=j, jb=jb, coff=coff, fm=fm, qrow=qrow,
                        qcol0=qcol0, m=m: e.matmul(
                            ps_s[jb][:, coff:coff + fm],
                            kt_sb[b][:, 128 * j:128 * (j + 1)],
                            qt_sb[:, qcol0 + m * 512 + coff: qcol0 + m * 512 + coff + fm],
                            start=True, stop=True),
                        waits=w, inc=k.tt)
                    mvt = None
                    if cfg.causal and r >= 0 and kmin < 4:
                        mvt = k.v(lambda e, jb=jb, kmin=kmin, pat=pat: e.tensor_tensor(
                            out=ps_s[jb][:, 128 * kmin:128 * (kmin + 1)],
                            in0=ps_s[jb][:, 128 * kmin:128 * (kmin + 1)],
                            in1=mask_sb[:, 128 * pat:128 * (pat + 1)], op=ALU.add),
                            waits=[(k.tt, qk_t[j]), (msem, mld_total)])
                    w = [(k.tt, qk_t[j])]
                    if mvt is not None:
                        w.append((k.vt, mvt))
                    if j - 3 in av_t:
                        w.append((k.tt, av_t[j - 3]))
                    exp_t[j] = k.s(lambda e, jb=jb, eb=eb, coff=coff, fm=fm: e.activation(
                        out=exp_sb[eb][:, coff:coff + fm], in_=ps_s[jb][:, coff:coff + fm],
                        func=AF.Exp, scale=cfg.scale), waits=w)

                def emit_av(j, b=b, jmax=jmax):
                    coff, fm, eb = geom[j]
                    w = [(k.st, exp_t[j])]
                    if j == 0:
                        w.append((k.vt, k.vt.n))
                    av_t[j] = k.emit(
                        "tensor",
                        lambda e, b=b, j=j, eb=eb, coff=coff, fm=fm, last=(j == jmax - 1):
                        e.matmul(ps_o[0:65, coff:coff + fm],
                                 v_sb[b][:, 65 * j:65 * (j + 1)],
                                 exp_sb[eb][:, coff:coff + fm],
                                 start=(j == 0), stop=last),
                        waits=w, inc=k.tt)

                LAG = 2
                for j in range(jmax + LAG):
                    if j < jmax:
                        emit_qk(j)
                    if j - LAG >= 0:
                        emit_av(j - LAG)
                k.v(lambda e: e.reciprocal(rcp_sb[:], ps_o[64:65, 0:512]),
                    waits=[(k.tt, av_t[jmax - 1])])
                k.t(lambda e: e.matmul(ps_b[0:64, :], cm.ones1[0:1, 0:64], rcp_sb[:],
                                       start=True, stop=True),
                    waits=[k.vt], milestone=True)
                k.v(lambda e: e.tensor_copy(out=bb_sb[:], in_=ps_b[0:64, :]),
                    waits=[k.tt])
                oc0 = (h // 2) * Tl
                k.v(lambda e, qrow=qrow, oc0=oc0, m=m: e.tensor_tensor(
                    out=outT[qrow:qrow + 64, oc0 + m * 512: oc0 + (m + 1) * 512],
                    in0=ps_o[0:64, 0:512], in1=bb_sb[:], op=ALU.mult))
        tt_end_prev = k.tt.n

    if residual_ln:
        xin_sb = k.sb("xin_sb", [128, d], F32)
    xs = [k.sem("xs0"), k.sem("xs1")]
    x_sb = [k.sb(f"x_sb{i}", [128, d], F32) for i in range(2)]
    yq = [k.sem("yq0"), k.sem("yq1")]
    y_sb = [k.sb(f"y_sb{i}", [128, d], F32) for i in range(2)]
    ps_y = [k.psum(f"ps_y{i}", [128, 512]) for i in range(2)]

    k.dma("sync", x_sb[0][:], x_in[0, :, :], xs[0])
    attn_done = k.vt.n
    for ti in range(geo.n_slots):
        m, kk = divmod(ti, 4)
        buf = ti % 2
        if ti + 1 < geo.n_slots:
            k.dma("sync", x_sb[(ti + 1) % 2][:], x_in[ti + 1, :, :], xs[(ti + 1) % 2],
                  waits=[k.vt, k.st])
        if residual_ln:
            rl_sem = k.vt if emit_ln(k, cm, x_sb[buf][:], None, d,
                                     x_waits=[(xs[buf], 16 * (ti // 2 + 1))],
                                     out_f32=xin_sb[:]) == "vt" else k.st
        nhalf = (d + 511) // 512
        for half in range(nhalf):
            hw = min(512, d - half * 512)
            pb = ps_y[half % 2]
            for c in range(nch):
                k.t(lambda e, c=c, m=m, kk=kk, half=half, pb=pb, hw=hw: e.matmul(
                    pb[:, 0:hw], outT[:, c * Tl + m * 512 + kk * 128: c * Tl + m * 512 + kk * 128 + 128],
                    wo_sb[:, c * d + half * 512: c * d + half * 512 + hw],
                    start=(c == 0), stop=(c == nch - 1)),
                    waits=([(k.vt, attn_done), (wsem, wld_total),
                            (k.vt, k.vt.n)] if c == 0 else []),
                    milestone=(c == nch - 1))
            tmm = k.tt.n
            res_src = xin_sb if residual_ln else x_sb[buf]
            k.v(lambda e, half=half, buf=buf, pb=pb, hw=hw, res_src=res_src:
                e.tensor_tensor(
                out=y_sb[buf][:, half * 512: half * 512 + hw],
                in0=res_src[:, half * 512: half * 512 + hw],
                in1=pb[:, 0:hw], op=ALU.add),
                waits=[(k.tt, tmm), (xs[buf], 16 * (ti // 2 + 1)),
                       (yq[buf], yq[buf].n)] + ([(rl_sem, rl_sem.n)] if residual_ln else []))
        k.dma("sync", y_out[ti, :, :], y_sb[buf][:], yq[buf], waits=[k.vt])

    if gate:
        gsem = k.sem("ggld")
        g_row = k.sb("gl_g", [1, d], F32)
        b_row = k.sb("gl_b", [1, d], F32)
        k.dma("sync", g_row[:], g2_in[None, :], gsem)
        k.dma("sync", b_row[:], b2_in[None, :], gsem)
        gld = gsem.n
        ag_all = k.sb("ag_all", [128, 2 * nch * 10], BF16)
        ag2sem = k.sem("ag2ld")
        agf2sem = k.sem("agf2ld")
        agf2 = k.sb("agf2", [128, 10], F32)
        for c in range(nch):
            k.dma("gpsimd", ag_all[:, c * 10:(c + 1) * 10], ag_in[128 * c:128 * (c + 1), :], ag2sem)
        ag2 = ag2sem.n
        for c in range(nch):
            k.dma("sync", agf2[:], ag_in[128 * c:128 * (c + 1), :], agf2sem, waits=[k.vt])
            k.v(lambda e, c=c: e.tensor_tensor(
                out=ag_all[:, (nch + c) * 10:(nch + c + 1) * 10], in0=agf2[:],
                in1=ag_all[:, c * 10:(c + 1) * 10], op=ALU.subtract),
                waits=[(agf2sem, agf2sem.n), (ag2sem, ag2)])
        ag2v = k.vt.n
        gain_g = outT[:, 0:d]
        bias_g = outT[:, d:2 * d]
        ps_g = ps_b
        for half in range((d + 511) // 512):
            hw = min(512, d - half * 512)
            sl = slice(half * 512, half * 512 + hw)
            k.t(lambda e, sl=sl, hw=hw: e.matmul(ps_g[:, 0:hw], cm.ones1[:],
                                                 g_row[0:1, sl], start=True, stop=True),
                waits=[(gsem, gld), k.vt], milestone=True)
            k.v(lambda e, sl=sl, hw=hw: e.tensor_copy(out=gain_g[:, sl.start:sl.stop], in_=ps_g[:, 0:hw]),
                waits=[k.tt])
            k.t(lambda e, sl=sl, hw=hw: e.matmul(ps_g[:, 0:hw], cm.ones1[:],
                                                 b_row[0:1, sl], start=True, stop=True),
                waits=[k.vt], milestone=True)
            k.v(lambda e, sl=sl, hw=hw: e.tensor_copy(out=bias_g[:, sl.start:sl.stop], in_=ps_g[:, 0:hw]),
                waits=[k.tt])
        lb = (geo.n_slots - 1) % 2
        xn2g = outT[:, 2 * d:3 * d]
        xn2glo = outT[:, 3 * d + 256:4 * d + 256]
        emit_ln(k, cm, y_sb[lb][:], xn2g, d, x_waits=[k.vt],
                gain_b=None, bias_b=None)
        k.v(lambda e: e.tensor_mul(cm.sqd[:, :d], xn2g, gain_g))
        k.v(lambda e: e.tensor_tensor(out=cm.sqd[:, :d], in0=cm.sqd[:, :d],
                                      in1=bias_g, op=ALU.add))
        k.v(lambda e: e.tensor_copy(out=xn2g, in_=cm.sqd[:, :d]))
        k.v(lambda e: e.tensor_tensor(out=xn2glo, in0=cm.sqd[:, :d], in1=xn2g,
                                      op=ALU.subtract))
        ps_ag = k.psum("ps_ag", [128, 10])
        ps_tr_g = k.psum("ps_tr_g", [128, 128], BF16)
        gstg = outT[:, 3 * d:3 * d + 128]
        gstg2 = outT[:, 3 * d + 128:3 * d + 256]
        hiT0 = 4 * d + 256
        loT0 = 4 * d + 256 + nch * 128
        for c in range(nch):
            k.t(lambda e, c=c: e.transpose(out=ps_tr_g[:],
                                           in_=outT[:, 2 * d + 128 * c: 2 * d + 128 * (c + 1)],
                                           identity=cm.ident[:]), waits=[k.vt],
                milestone=True)
            k.v(lambda e, c=c: e.tensor_copy(out=outT[:, hiT0 + 128 * c: hiT0 + 128 * (c + 1)],
                                             in_=ps_tr_g[:]), waits=[k.tt])
            k.t(lambda e, c=c: e.transpose(out=ps_tr_g[:],
                                           in_=outT[:, 3 * d + 256 + 128 * c: 3 * d + 256 + 128 * (c + 1)],
                                           identity=cm.ident[:]), waits=[k.vt],
                milestone=True)
            k.v(lambda e, c=c: e.tensor_copy(out=outT[:, loT0 + 128 * c: loT0 + 128 * (c + 1)],
                                             in_=ps_tr_g[:]), waits=[k.tt])
        nmm_g = 0
        for c in range(nch):
            for (hi_side, tblo) in ((True, 0), (True, 1), (False, 0)):
                nmm_g += 1
                k.t(lambda e, c=c, hi_side=hi_side, tblo=tblo,
                    st_=(nmm_g == 1), sp_=(nmm_g == 3 * nch): e.matmul(
                    ps_ag[:], outT[:, (hiT0 if hi_side else loT0) + 128 * c:
                                   (hiT0 if hi_side else loT0) + 128 * (c + 1)],
                    ag_all[:, (tblo * nch + c) * 10:(tblo * nch + c + 1) * 10],
                    start=st_, stop=sp_),
                    waits=[k.vt, (k.vt, ag2v)], milestone=(nmm_g == 3 * nch))
        agfull = k.sb("agfull", [128, 10], F32)
        k.v(lambda e: e.tensor_copy(out=agfull[:], in_=ps_ag[:]), waits=[k.tt])
        gq2 = k.sem("gq2")
        k.dma("sync", ag_out[:, :], agfull[127:128, :], gq2, waits=[k.vt])
        k.emit("sync", lambda e: e.nop(), waits=[(gq2, gq2.n)])

    k.emit("sync", lambda e: e.nop(), waits=[(yq[0], yq[0].n), (yq[1], yq[1].n)])
    return k



def build_l1ffn(geo: Geo, name="l1ffn", n_exp=8, F=4096):
    k = K(name)
    d = geo.d
    nch = geo.dchunks
    Tl = geo.T_local
    nslots = geo.n_slots
    nfc = F // 128

    x_in = k.inp("x", [nslots, 128, d], F32)
    cen_in = k.inp("cen", [n_exp, d], F32)
    w1t_in = k.inp("w1t", [n_exp, d, F], BF16)
    w2t_in = k.inp("w2t", [n_exp, F, d], BF16)
    b1p_in = k.inp("b1p", [n_exp, 128, nfc], F32)
    b2_in = k.inp("b2", [n_exp, d], F32)
    y_out = k.outp("y", [nslots, 128, d], F32)

    cm = Common(k, d)
    cm.emit_init(k)
    csem = k.sem("cld")
    cen_f = k.sb("cen_f", [n_exp, d], F32)
    cen_hi = k.sb("cen_hi", [n_exp, d], BF16)
    cen_lo = k.sb("cen_lo", [n_exp, d], BF16)
    cen_sb = k.sb("cen_sb", [128, 2 * nch * n_exp], BF16)
    k.dma("sync", cen_f[:], cen_in[:, :], csem)
    cld0 = csem.n
    ps_tr2 = k.psum("ps_tr", [128, 128], BF16)
    k.v(lambda e: e.tensor_copy(out=cen_hi[:], in_=cen_f[:]), waits=[(csem, cld0)])
    k.v(lambda e: e.tensor_tensor(out=cen_lo[:], in0=cen_f[:], in1=cen_hi[:],
                                  op=ALU.subtract))
    for part in range(2):
        src = cen_hi if part == 0 else cen_lo
        for c in range(nch):
            k.t(lambda e, c=c, src=src: e.transpose(
                out=ps_tr2[:, 0:n_exp], in_=src[:, 128 * c:128 * (c + 1)],
                identity=cm.ident[0:n_exp, 0:n_exp]),
                waits=[k.vt], milestone=True)
            k.v(lambda e, c=c, part=part: e.tensor_copy(
                out=cen_sb[:, (part * nch + c) * n_exp:(part * nch + c + 1) * n_exp],
                in_=ps_tr2[:, 0:n_exp]), waits=[k.tt])
    cen_ready = k.vt.n
    b2sem = k.sem("b2ld")
    b2_sb = k.sb("b2_sb", [1, d], BF16)
    k.dma("gpsimd", b2_sb[:], b2_in[0:1, :], b2sem)

    feats_hi = k.sb("feats_hi", [128, d], BF16)
    nrmT = k.sb("nrmT", [128, nch * Tl], BF16)
    acc = k.sb("acc", [128, nslots * d], BF16)
    masks = k.sb("masks", [128, nslots * n_exp], F32)
    alpha = k.sb("alpha", [128, nslots], F32)
    mrow = k.sb("mrow", [128, 1], F32)
    meq = k.sb("meq", [128, n_exp], F32)
    mcum = k.sb("mcum", [128, n_exp], F32)
    nrm_sb = k.sb("nrm_sb", [128, d], BF16)
    ps_tr = ps_tr2
    ps_sc = k.psum("ps_sc", [128, n_exp])
    ftr_sb = k.sb("ftr_sb", [128, 128], BF16)
    flo_sb = k.sb("flo_sb", [128, 128], BF16)
    feats_f = k.sb("feats_f", [128, d], F32)
    feats_lo = k.sb("feats_lo", [128, d], BF16)
    zeros8 = k.sb("zeros8", [128, 8], F32)

    xs = [k.sem("xs0"), k.sem("xs1")]
    x_sb = [k.sb(f"x_sb{i}", [128, d], F32) for i in range(2)]

    k.v(lambda e: e.memset(zeros8[:], 0.0))
    k.dma("sync", x_sb[0][:], x_in[0, :, :], xs[0])
    for ti in range(nslots):
        buf = ti % 2
        if ti + 1 < nslots:
            k.dma("sync", x_sb[(ti + 1) % 2][:], x_in[ti + 1, :, :], xs[(ti + 1) % 2],
                  waits=[k.vt, k.st])
        ldw = [(xs[buf], 16 * (ti // 2 + 1))]
        emit_ln(k, cm, x_sb[buf][:], None, d, x_waits=ldw, out_f32=feats_f[:])
        k.v(lambda e: e.tensor_copy(out=feats_hi[:], in_=feats_f[:]),
            waits=[(k.tt, k.tt.n), (k.st, k.st.n)])
        k.v(lambda e: e.tensor_tensor(out=feats_lo[:], in0=feats_f[:],
                                      in1=feats_hi[:], op=ALU.subtract))
        fvt = k.vt.n
        emit_ln(k, cm, feats_hi[:], nrm_sb[:], d, x_waits=[k.vt])
        m, kk = divmod(ti, 4)
        for c in range(nch):
            k.t(lambda e, c=c: e.transpose(out=ps_tr[:], in_=nrm_sb[:, 128 * c:128 * (c + 1)],
                                           identity=cm.ident[:]),
                waits=[k.vt], milestone=True)
            k.v(lambda e, c=c, ti=ti: e.tensor_copy(
                out=nrmT[:, c * Tl + ti * 128: c * Tl + ti * 128 + 128],
                in_=ps_tr[:]), waits=[k.tt])
        nmm = 3 * nch
        imm = 0
        for c in range(nch):
            k.t(lambda e, c=c: e.transpose(
                out=ps_tr[:], in_=feats_hi[:, 128 * c: 128 * (c + 1)],
                identity=cm.ident[:]), waits=[(k.vt, fvt), k.vt], milestone=True)
            k.v(lambda e: e.tensor_copy(out=ftr_sb[:], in_=ps_tr[:]),
                waits=[k.tt])
            k.t(lambda e, c=c: e.transpose(
                out=ps_tr[:], in_=feats_lo[:, 128 * c: 128 * (c + 1)],
                identity=cm.ident[:]), waits=[k.vt], milestone=True)
            k.v(lambda e: e.tensor_copy(out=flo_sb[:], in_=ps_tr[:]),
                waits=[k.tt])
            for (aa, bb) in ((ftr_sb, 0), (ftr_sb, 1), (flo_sb, 0)):
                imm += 1
                k.t(lambda e, c=c, aa=aa, bb=bb: e.matmul(
                    ps_sc[:], aa[:],
                    cen_sb[:, (bb * nch + c) * n_exp:(bb * nch + c + 1) * n_exp],
                    start=(c == 0 and aa is ftr_sb and bb == 0),
                    stop=(c == nch - 1 and aa is flo_sb)),
                    waits=[k.vt, (k.vt, cen_ready)],
                    milestone=(c == nch - 1 and aa is flo_sb))
        k.v(lambda e: e.tensor_reduce(out=mrow[:], in_=ps_sc[:], axis=AX.X, op=ALU.max),
            waits=[k.tt])
        k.s(lambda e, ti=ti: e.activation(out=alpha[:, ti:ti + 1], in_=mrow[:],
                                          func=AF.Sigmoid), waits=[k.vt])
        k.v(lambda e: e.tensor_scalar(out=meq[:], in0=ps_sc[:], scalar1=mrow[:],
                                      scalar2=None, op0=ALU.is_ge))
        k.v(lambda e: e.tensor_tensor_scan(out=mcum[:], data0=meq[:], data1=zeros8[:],
                                           initial=0.0, op0=ALU.add, op1=ALU.add))
        k.v(lambda e: e.tensor_scalar(out=mcum[:], in0=mcum[:], scalar1=1.0,
                                      scalar2=None, op0=ALU.is_le))
        k.v(lambda e, ti=ti: e.tensor_tensor(out=masks[:, ti * n_exp:(ti + 1) * n_exp],
                                             in0=meq[:], in1=mcum[:], op=ALU.mult))
        k.v(lambda e, ti=ti: e.memset(acc[:, ti * d:(ti + 1) * d], 0.0))

    state_done = k.vt.n

    w1sem = k.sem("w1s")
    w1r = k.sb("w1r", [128, nch * F], BF16)
    w2sem = [k.sem("w2s0"), k.sem("w2s1")]
    w2_sb = [k.sb(f"w2_sb{i}", [128, 512], BF16) for i in range(2)]
    b1sem = k.sem("b1ld")
    b1_sb = k.sb("b1_sb", [128, n_exp * nfc], F32)
    for e in range(n_exp):
        k.dma("sync", b1_sb[:, e * nfc:(e + 1) * nfc], b1p_in[e, :, :], b1sem)
    b1_total = b1sem.n
    relu_sb = k.sb("relu_sb", [128, nfc * 512], BF16)
    ps_h = [k.psum("ps_h0", [128, 512]), k.psum("ps_h1", [128, 512])]
    ps_y2 = [k.psum(f"ps_y2{i}", [128, 512]) for i in range(4)]
    ymask = k.sb("ymask", [128, 512], BF16)

    nfp = F // 256

    def load_w1(e, waits):
        for c in range(nch):
            k.dma("sync", w1r[:, c * F:(c + 1) * F],
                  w1t_in[e, 128 * c:128 * (c + 1), :],
                  w1sem, waits=(list(waits) if c == 0 else []))

    nhalf = (d + 511) // 512
    cons2 = [(e, g, half, fc) for e in range(n_exp) for g in range(geo.groups)
             for half in range(nhalf) for fc in range(nfc)]

    def load_w2(i, waits):
        e, g, half, fc = cons2[i]
        hw = min(512, d - half * 512)
        b = i % 2
        k.dma("gpsimd" if b == 0 else "sync", w2_sb[b][:, 0:hw],
              w2t_in[e, 128 * fc:128 * (fc + 1), half * 512: half * 512 + hw],
              w2sem[b], waits=waits)

    load_w1(0, [])
    load_w2(0, [])
    load_w2(1, [])
    i2 = 0
    for e in range(n_exp):
        w1_tgt = 16 * nch * (e + 1)
        if e > 0:
            k.dma("gpsimd", b2_sb[:], b2_in[e:e + 1, :], b2sem,
                  waits=[(k.tt, k.tt.n)])
        b2_tgt = 16 * (e + 1)
        for g in range(geo.groups):
            rel_st = {}
            for fp in range(nfp):
                for sub in range(2):
                    fc = fp * 2 + sub
                    hb = fc % 2
                    for c in range(nch):
                        w = []
                        if c == 0:
                            w = [(k.vt, state_done),
                                 (k.st, rel_st.get(fc - 2, k.st.n))]
                            if g == 0 and fc == 0:
                                w.append((w1sem, w1_tgt))
                        k.t(lambda e2, c=c, g=g, fp=fp, sub=sub, hb=hb: e2.matmul(
                            ps_h[hb][:], w1r[:, c * F + fp * 256 + sub * 128:
                                             c * F + fp * 256 + (sub + 1) * 128],
                            nrmT[:, c * Tl + g * 512: c * Tl + (g + 1) * 512],
                            start=(c == 0), stop=(c == nch - 1)),
                            waits=w, milestone=(c == nch - 1))
                    rel_st[fc] = k.s(lambda e2, fc=fc, e=e, hb=hb: e2.activation(
                        out=relu_sb[:, fc * 512:(fc + 1) * 512], in_=ps_h[hb][:],
                        func=AF.Relu, bias=b1_sb[:, e * nfc + fc: e * nfc + fc + 1]),
                        waits=[(k.tt, k.tt.n), (b1sem, b1_total)])
            if g == geo.groups - 1 and e + 1 < n_exp:
                load_w1(e + 1, [(k.tt, k.tt.n)])
            relu_done_st = k.st.n
            for half in range(nhalf):
                hw = min(512, d - half * 512)
                for fc in range(nfc):
                    b = i2 % 2
                    w_w = (w2sem[b], 16 * (i2 // 2 + 1))
                    for kk in range(4):
                        k.t(lambda e2, fc=fc, kk=kk, b=b, hw=hw: e2.matmul(
                            ps_y2[kk][:, 0:hw],
                            relu_sb[:, fc * 512 + kk * 128: fc * 512 + (kk + 1) * 128],
                            w2_sb[b][:, 0:hw],
                            start=(fc == 0), stop=False),
                            waits=([w_w, (k.st, relu_done_st), (k.vt, k.vt.n)]
                                   if (kk == 0 and fc == 0) else
                                   ([w_w] if kk == 0 else [])),
                            milestone=(kk == 3))
                    if i2 + 2 < len(cons2):
                        load_w2(i2 + 2, [(k.tt, k.tt.n)])
                    i2 += 1
                for kk in range(4):
                    ti = 4 * g + kk
                    k.t(lambda e2, e=e, half=half, kk=kk, hw=hw: e2.matmul(
                        ps_y2[kk][:, 0:hw], cm.ones1b[0:1, 0:128],
                        b2_sb[:, half * 512: half * 512 + hw],
                        start=False, stop=True),
                        waits=[(b2sem, b2_tgt)], milestone=True)
                    tmm = k.tt.n
                    k.s(lambda e2, ti=ti, e=e, kk=kk, hw=hw: e2.mul(
                        ymask[:, 0:hw], ps_y2[kk][:, 0:hw],
                        masks[:, ti * n_exp + e: ti * n_exp + e + 1]),
                        waits=[(k.tt, tmm), (k.vt, k.vt.n)])
                    k.v(lambda e2, ti=ti, half=half, hw=hw: e2.tensor_tensor(
                        out=acc[:, ti * d + half * 512: ti * d + half * 512 + hw],
                        in0=acc[:, ti * d + half * 512: ti * d + half * 512 + hw],
                        in1=ymask[:, 0:hw], op=ALU.add), waits=[k.st])

    xs2 = [k.sem("xt0"), k.sem("xt1")]
    x2_sb = x_sb
    yq = [k.sem("yq0"), k.sem("yq1")]
    y_sb = [k.sb(f"y_sb{i}", [128, d], F32) for i in range(2)]
    k.dma("sync", x2_sb[0][:], x_in[0, :, :], xs2[0], waits=[k.vt, k.st])
    moe_done = k.vt.n
    for ti in range(nslots):
        buf = ti % 2
        if ti + 1 < nslots:
            k.dma("sync", x2_sb[(ti + 1) % 2][:], x_in[ti + 1, :, :], xs2[(ti + 1) % 2],
                  waits=[k.vt, k.st])
        emit_ln(k, cm, x2_sb[buf][:], None, d,
                x_waits=[(xs2[buf], 16 * (ti // 2 + 1)), (k.vt, moe_done)],
                out_f32=feats_f[:])
        k.v(lambda e, buf=buf: e.tensor_tensor(
            out=y_sb[buf][:], in0=x2_sb[buf][:], in1=feats_f[:],
            op=ALU.add),
            waits=[(yq[buf], yq[buf].n), k.vt])
        k.v(lambda e, ti=ti, buf=buf: e.scalar_tensor_tensor(
            out=y_sb[buf][:], in0=acc[:, ti * d:(ti + 1) * d],
            scalar=alpha[:, ti:ti + 1], in1=y_sb[buf][:], op0=ALU.mult, op1=ALU.add))
        k.dma("sync", y_out[ti, :, :], y_sb[buf][:], yq[buf], waits=[k.vt])

    k.emit("sync", lambda e: e.nop(), waits=[(yq[0], yq[0].n), (yq[1], yq[1].n)])
    return k



def build_prepass(geo: Geo, name="prepass", F=4096, d=None):
    k = K(name)
    d = d or geo.d
    nch = d // 128
    nfc = F // 128
    F2 = 2 * d

    w1_in = k.inp("w1", [F, d], F32)
    w2_in = k.inp("w2", [d, F], F32)
    g_in = k.inp("gamma", [d], F32)
    b_in = k.inp("beta", [d], F32)
    b1_in = k.inp("b1", [F], F32)
    l0a_in = k.inp("l0a", [5, d, F2], F32)
    l0b_in = k.inp("l0b", [3, F2, d], F32)

    w1t_out = k.outp("w1t", [d, F], BF16)
    w2t_out = k.outp("w2t", [F, d], BF16)
    b1p_out = k.outp("b1p", [128, nfc], F32)
    l0a_out = k.outp("l0a_o", [5, d, F2], BF16)
    l0b_out = k.outp("l0b_o", [3, F2, d], BF16)

    cm = Common(k, d)
    cm.emit_init(k)

    ssem = k.sem("sld")
    g8 = k.sb("g8", [8, 128], BF16)
    b1_32 = k.sb("b1_32", [F // 128, 128], F32)
    beta_row = k.sb("beta_row", [1, d], F32)
    gsem2 = k.sem("gld2")
    k.dma("gpsimd", g8[:], g_in.rearrange("(a b) -> a b", b=128), gsem2)
    k.dma("sync", b1_32[:], b1_in.rearrange("(a b) -> a b", b=128), ssem)
    k.dma("sync", beta_row[:], b_in[None, :], ssem)
    sld = ssem.n
    gld2 = gsem2.n
    gscale = k.sb("gscale", [128, 8], F32)
    b1T = k.sb("b1T", [128, F // 128], F32)
    beta_b = k.sb("beta_b", [128, d], F32)
    ps_small = k.psum("ps_small", [128, 512])
    ps_smallb = k.psum("ps_smallb", [128, 128], BF16)
    k.t(lambda e: e.transpose(out=ps_smallb[:, 0:8], in_=g8[:],
                              identity=cm.ident[0:8, 0:8]),
        waits=[(gsem2, gld2), k.vt], milestone=True)
    k.v(lambda e: e.tensor_copy(out=gscale[:], in_=ps_smallb[:, 0:8]), waits=[k.tt])
    ps_f = k.psum("ps_f", [128, 128])
    k.t(lambda e: e.transpose(out=ps_f[:, 0:F // 128], in_=b1_32[:],
                              identity=cm.identf[0:F // 128, 0:F // 128]),
        waits=[(ssem, sld), k.vt], milestone=True)
    k.v(lambda e: e.tensor_copy(out=b1T[:], in_=ps_f[:, 0:F // 128]), waits=[k.tt])
    for half in range((d + 511) // 512):
        hw = min(512, d - half * 512)
        k.t(lambda e, half=half, hw=hw: e.matmul(
            ps_small[:, 0:hw], cm.ones1[:], beta_row[0:1, half * 512:half * 512 + hw],
            start=True, stop=True), waits=[(ssem, sld), k.vt], milestone=True)
        k.v(lambda e, half=half, hw=hw: e.tensor_copy(
            out=beta_b[:, half * 512:half * 512 + hw], in_=ps_small[:, 0:hw]),
            waits=[k.tt])

    wsem = [k.sem("wl0"), k.sem("wl1")]
    row_sb = [k.sb(f"row_sb{i}", [128, max(F2, F)], BF16) for i in range(2)]
    junk = k.sb("junk", [128, max(F2, F)], BF16)
    b1pc = k.sb("b1pc", [128, 1], F32)
    b1pa = k.sb("b1pa", [128, nfc], F32)
    stage = k.sb("stage", [128, nch * F], BF16)
    ps_t = k.psum("ps_t", [128, 128], BF16)
    oq = k.sem("oq")

    k.dma("gpsimd", row_sb[0][:, 0:d], w1_in[0:128, :], wsem[0])
    for ft in range(F // 128):
        bb = ft % 2
        if ft + 1 < F // 128:
            k.dma("gpsimd", row_sb[(ft + 1) % 2][:, 0:d], w1_in[128 * (ft + 1):128 * (ft + 2), :],
                  wsem[(ft + 1) % 2], waits=[k.vt, (k.tt, k.tt.n)])
        ww = (wsem[bb], 16 * (ft // 2 + 1))
        k.v(lambda e, bb=bb, ft=ft: e.scalar_tensor_tensor(
            out=junk[:, 0:d], in0=row_sb[bb][:, 0:d], scalar=1.0, in1=beta_b[:],
            op0=ALU.mult, op1=ALU.mult, accum_out=b1pc[:]), waits=[ww])
        k.v(lambda e, ft=ft: e.tensor_copy(out=b1pa[:, ft:ft + 1], in_=b1pc[:]))
        for c in range(nch):
            k.t(lambda e, bb=bb, c=c: e.transpose(
                out=ps_t[:], in_=row_sb[bb][:, 128 * c:128 * (c + 1)],
                identity=cm.ident[:]), waits=[ww, k.vt, (k.st, k.st.n)], milestone=True)
            k.s(lambda e, c=c, ft=ft: e.mul(
                stage[:, c * F + 128 * ft: c * F + 128 * (ft + 1)], ps_t[:],
                gscale[:, c:c + 1]), waits=[k.tt, (ssem, sld), k.vt])
    k.v(lambda e: e.tensor_tensor(out=b1pa[:], in0=b1pa[:], in1=b1T[:], op=ALU.add),
        waits=[k.st])
    k.dma("sync", b1p_out[:, :], b1pa[:], oq, waits=[k.vt])
    for c in range(nch):
        k.dma("sync", w1t_out[128 * c:128 * (c + 1), :], stage[:, c * F:(c + 1) * F], oq,
              waits=[k.st, k.vt])
    w1_stores = oq.n

    k.dma("gpsimd", row_sb[0][:, 0:F], w2_in[0:128, :], wsem[0],
          waits=[(oq, w1_stores), (k.tt, k.tt.n), k.vt])
    for dt_ in range(nch):
        bb = dt_ % 2
        if dt_ + 1 < nch:
            k.dma("gpsimd", row_sb[(dt_ + 1) % 2][:, 0:F], w2_in[128 * (dt_ + 1):128 * (dt_ + 2), :],
                  wsem[(dt_ + 1) % 2], waits=[(k.tt, k.tt.n)])
        ww = (wsem[bb], 16 * (F // 128 // 2) + 16 * (dt_ // 2 + 1))
        for fc in range(nfc):
            k.t(lambda e, bb=bb, fc=fc: e.transpose(
                out=ps_t[:], in_=row_sb[bb][:, 128 * fc:128 * (fc + 1)],
                identity=cm.ident[:]), waits=[ww, k.vt], milestone=True)
            k.v(lambda e, fc=fc, dt_=dt_: e.tensor_copy(
                out=stage[:, fc * d + 128 * dt_: fc * d + 128 * (dt_ + 1)], in_=ps_t[:]),
                waits=[k.tt])
    for fc in range(nfc):
        k.dma("sync", w2t_out[128 * fc:128 * (fc + 1), :], stage[:, fc * d:(fc + 1) * d], oq,
              waits=[k.vt])
    w2_stores = oq.n

    csem = [k.sem("cl0"), k.sem("cl1")]
    cq = [k.sem("cq0"), k.sem("cq1")]
    cast_sb = [k.sb(f"cast_sb{i}", [128, F2], BF16) for i in range(2)]
    jobs = []
    for j in range(5):
        for rt in range(d // 128):
            jobs.append((l0a_in[j, 128 * rt:128 * (rt + 1), :],
                         l0a_out[j, 128 * rt:128 * (rt + 1), :], F2))
    for j in range(3):
        for rt in range(F2 // 128):
            jobs.append((l0b_in[j, 128 * rt:128 * (rt + 1), :],
                         l0b_out[j, 128 * rt:128 * (rt + 1), :], d))
    for i, (src, dst, w) in enumerate(jobs):
        bb = i % 2
        k.dma("gpsimd", cast_sb[bb][:, 0:w], src, csem[bb],
              waits=[(cq[bb], cq[bb].n)])
        k.dma("sync", dst, cast_sb[bb][:, 0:w], cq[bb],
              waits=[(csem[bb], 16 * (i // 2 + 1))])

    k.emit("sync", lambda e: e.nop(),
           waits=[(oq, oq.n), (cq[0], cq[0].n), (cq[1], cq[1].n)])
    return k



def build_l0moe(geo: Geo, name="l0moe", n_ag=10, n_exp=20, Fm=2048):
    k = K(name)
    d = geo.d
    nch = geo.dchunks
    Tl = geo.T_local
    nslots = geo.n_slots
    nfc = Fm // 128

    xm_in = k.inp("xmid", [nslots, 128, d], F32)
    xo_in = k.inp("xorig", [nslots, 128, d], F32)
    gr_in = k.inp("gate_row", [1, n_ag], F32)
    g2_in = k.inp("ln2_g", [d], F32)
    b2_in = k.inp("ln2_b", [d], F32)
    ag_in = k.inp("agent_gate", [d, n_ag], F32)
    eg_in = k.inp("expert_gate", [d, n_exp], F32)
    re_in = k.inp("role_emb", [n_ag, d], F32)
    w1c_in = k.inp("w1c", [n_exp, d, Fm], BF16)
    w3c_in = k.inp("w3c", [n_exp, d, Fm], BF16)
    w2c_in = k.inp("w2c", [n_exp, Fm, d], BF16)
    y_out = k.outp("y", [nslots, 128, d], F32)
    dbg_awk = k.outp("dbg_awk", [128, 2 * nslots], F32)
    dbg_aew = k.outp("dbg_aew", [128, 2 * nslots], F32)

    cm = Common(k, d)
    cm.emit_init(k)
    k.const(1e-6)

    ssem = k.sem("sld")
    gr_sb = k.sb("gr_sb", [1, n_ag], F32)
    g_row = k.sb("g_row", [1, d], F32)
    b_row = k.sb("b_row", [1, d], F32)
    k.dma("sync", gr_sb[:], gr_in[:, :], ssem)
    k.dma("sync", g_row[:], g2_in[None, :], ssem)
    k.dma("sync", b_row[:], b2_in[None, :], ssem)
    sld = ssem.n
    gsem = k.sem("gld")
    ag_all = k.sb("ag_all", [128, 2 * nch * n_ag], BF16)
    ag_f = k.sb("ag_f", [128, n_ag], F32)
    agsem = k.sem("agld")
    eg_all = k.sb("eg_all", [128, nch * n_exp], BF16)
    re_sb = k.sb("re_sb", [n_ag, d], BF16)
    for c in range(nch):
        k.dma("gpsimd", ag_all[:, c * n_ag:(c + 1) * n_ag],
              ag_in[128 * c:128 * (c + 1), :], gsem)
        k.dma("gpsimd", eg_all[:, c * n_exp:(c + 1) * n_exp],
              eg_in[128 * c:128 * (c + 1), :], gsem)
    k.dma("gpsimd", re_sb[:], re_in[:, :], gsem)
    gld = gsem.n
    for c in range(nch):
        k.dma("sync", ag_f[:], ag_in[128 * c:128 * (c + 1), :], agsem,
              waits=[k.vt])
        k.v(lambda e, c=c: e.tensor_tensor(
            out=ag_all[:, (nch + c) * n_ag:(nch + c + 1) * n_ag], in0=ag_f[:],
            in1=ag_all[:, c * n_ag:(c + 1) * n_ag], op=ALU.subtract),
            waits=[(agsem, agsem.n), (gsem, gld)])

    gain_g = k.sb("gain_g", [128, d], F32)
    bias_g = k.sb("bias_g", [128, d], F32)
    ps_all = k.psum("ps_all", [128, 512])
    for half in range((d + 511) // 512):
        hw = min(512, d - half * 512)
        sl = slice(half * 512, half * 512 + hw)
        k.t(lambda e, sl=sl, hw=hw: e.matmul(ps_all[:, 0:hw], cm.ones1[:], g_row[0:1, sl],
                                             start=True, stop=True),
            waits=[(ssem, sld), k.vt], milestone=True)
        k.v(lambda e, sl=sl, hw=hw: e.tensor_copy(out=gain_g[:, sl], in_=ps_all[:, 0:hw]),
            waits=[k.tt])
        k.t(lambda e, sl=sl, hw=hw: e.matmul(ps_all[:, 0:hw], cm.ones1[:], b_row[0:1, sl],
                                             start=True, stop=True),
            waits=[k.vt], milestone=True)
        k.v(lambda e, sl=sl, hw=hw: e.tensor_copy(out=bias_g[:, sl], in_=ps_all[:, 0:hw]),
            waits=[k.tt])

    m1 = k.sb("m1", [1, 1], F32)
    m2 = k.sb("m2", [1, 1], F32)
    oh1 = k.sb("oh1", [1, n_ag], F32)
    oh2 = k.sb("oh2", [1, n_ag], F32)
    tmp10 = k.sb("tmp10", [1, n_ag], F32)
    e1s = k.sb("e1s", [1, 1], F32)
    e2s = k.sb("e2s", [1, 1], F32)
    iota10 = k.sb("iota10", [1, n_ag], F32)
    k.g(lambda e: e.iota(out=iota10[:], pattern=[[1, n_ag]], base=0,
                         channel_multiplier=0, allow_small_or_imprecise_dtypes=True))
    k.v(lambda e: e.tensor_reduce(out=m1[:], in_=gr_sb[:], axis=AX.X, op=ALU.max),
        waits=[(ssem, sld)])
    k.v(lambda e: e.tensor_scalar(out=oh1[:], in0=gr_sb[:], scalar1=m1[:],
                                  scalar2=None, op0=ALU.is_ge))
    zz = k.sb("zz10", [1, n_ag], F32)
    k.v(lambda e: e.memset(zz[:], 0.0))
    k.v(lambda e: e.tensor_tensor_scan(out=tmp10[:], data0=oh1[:], data1=zz[:],
                                       initial=0.0, op0=ALU.add, op1=ALU.add))
    k.v(lambda e: e.tensor_scalar(out=tmp10[:], in0=tmp10[:], scalar1=1.0,
                                  scalar2=None, op0=ALU.is_le))
    k.v(lambda e: e.tensor_mul(oh1[:], oh1[:], tmp10[:]))
    k.v(lambda e: e.scalar_tensor_tensor(out=tmp10[:], in0=oh1[:], scalar=NEG,
                                         in1=gr_sb[:], op0=ALU.mult, op1=ALU.add))
    k.v(lambda e: e.tensor_reduce(out=m2[:], in_=tmp10[:], axis=AX.X, op=ALU.max))
    k.v(lambda e: e.tensor_scalar(out=oh2[:], in0=tmp10[:], scalar1=m2[:],
                                  scalar2=None, op0=ALU.is_ge))
    k.v(lambda e: e.tensor_tensor_scan(out=tmp10[:], data0=oh2[:], data1=zz[:],
                                       initial=0.0, op0=ALU.add, op1=ALU.add))
    k.v(lambda e: e.tensor_scalar(out=tmp10[:], in0=tmp10[:], scalar1=1.0,
                                  scalar2=None, op0=ALU.is_le))
    k.v(lambda e: e.tensor_mul(oh2[:], oh2[:], tmp10[:]))
    k.v(lambda e: e.tensor_mul(tmp10[:], iota10[:], oh1[:]), waits=[k.gt])
    k.v(lambda e: e.tensor_reduce(out=e1s[:], in_=tmp10[:], axis=AX.X, op=ALU.add))
    k.v(lambda e: e.tensor_mul(tmp10[:], iota10[:], oh2[:]))
    k.v(lambda e: e.tensor_reduce(out=e2s[:], in_=tmp10[:], axis=AX.X, op=ALU.add))
    gate_vt = k.vt.n

    ebc = k.sb("ebc", [128, 2], F32)
    k.t(lambda e: e.matmul(ps_all[:, 0:1], cm.ones1[:], e1s[:], start=True, stop=True),
        waits=[(k.vt, gate_vt)], milestone=True)
    k.t(lambda e: e.matmul(ps_all[:, 1:2], cm.ones1[:], e2s[:], start=True, stop=True),
        milestone=True)
    k.v(lambda e: e.tensor_copy(out=ebc[:], in_=ps_all[:, 0:2]), waits=[k.tt])
    iop = k.sb("iop", [128, 1], F32)
    k.g(lambda e: e.iota(out=iop[:], pattern=[[0, 1]], base=0, channel_multiplier=1,
                         allow_small_or_imprecise_dtypes=True))
    idx13 = k.sb("idx13", [128, 4 * nch], U32)
    idx2 = k.sb("idx2", [128, 4 * nfc], U32)
    idxf = k.sb("idxf", [128, 1], F32)
    emb_vt = k.vt.n
    for bm in range(4):
        br, mm = divmod(bm, 2)
        for c in range(nch):
            k.v(lambda e, bm=bm, br=br, mm=mm, c=c: e.scalar_tensor_tensor(
                out=idxf[:], in0=ebc[:, br:br + 1], scalar=2.0 * d, in1=iop[:],
                op0=ALU.mult, op1=ALU.add), waits=[k.gt])
            k.v(lambda e, bm=bm, mm=mm, c=c: e.tensor_scalar_add(
                idxf[:], idxf[:], float(mm * d + 128 * c)))
            k.v(lambda e, bm=bm, c=c: e.tensor_copy(
                out=idx13[:, bm * nch + c: bm * nch + c + 1], in_=idxf[:]))
        for fc in range(nfc):
            k.v(lambda e, bm=bm, br=br, mm=mm, fc=fc: e.scalar_tensor_tensor(
                out=idxf[:], in0=ebc[:, br:br + 1], scalar=2.0 * Fm, in1=iop[:],
                op0=ALU.mult, op1=ALU.add))
            k.v(lambda e, bm=bm, mm=mm, fc=fc: e.tensor_scalar_add(
                idxf[:], idxf[:], float(mm * Fm + 128 * fc)))
            k.v(lambda e, bm=bm, fc=fc: e.tensor_copy(
                out=idx2[:, bm * nfc + fc: bm * nfc + fc + 1], in_=idxf[:]))
    idx_vt = k.vt.n

    sel = k.sb("sel", [1, 4 * n_exp], F32)
    iota20 = k.sb("iota20", [1, n_exp], F32)
    k.g(lambda e: e.iota(out=iota20[:], pattern=[[1, n_exp]], base=0,
                         channel_multiplier=0, allow_small_or_imprecise_dtypes=True))
    self_idx = k.sb("selidx", [1, 1], F32)
    for bm in range(4):
        br, mm = divmod(bm, 2)
        src = e1s if br == 0 else e2s
        k.v(lambda e, src=src, mm=mm: e.scalar_tensor_tensor(
            out=self_idx[:], in0=src[:], scalar=2.0, in1=src[:],
            op0=ALU.mult, op1=ALU.bypass), waits=[(k.vt, gate_vt), k.gt])
        k.v(lambda e, mm=mm: e.tensor_scalar_add(self_idx[:], self_idx[:], float(mm)))
        k.v(lambda e, bm=bm: e.tensor_scalar(
            out=sel[:, bm * n_exp:(bm + 1) * n_exp], in0=iota20[:],
            scalar1=self_idx[:], scalar2=None, op0=ALU.is_equal))
    sel_vt = k.vt.n

    xs = [k.sem("xs0"), k.sem("xs1")]
    x_sb = [k.sb(f"x_sb{i}", [128, d], F32) for i in range(2)]
    xn2 = k.sb("xn2", [128, d], BF16)
    xn2lo = k.sb("xn2lo", [128, d], BF16)
    xn2loT = k.sb("xn2loT", [128, None], BF16) if False else k.sb("xn2loT", [128, nch * 128], BF16)
    xn2T = k.sb("xn2T", [128, nch * Tl], BF16)
    ps_tr = k.psum("ps_tr", [128, 128], BF16)
    aw = k.sb("aw", [128, n_ag], F32)
    awt = k.sb("awt", [128, n_ag], F32)
    awm = k.sb("awm", [128, 1], F32)
    ohb = k.sb("ohb", [128, 2 * nslots * n_ag], BF16)
    awk = k.sb("awk", [128, 2 * nslots], F32)
    aew = k.sb("aew", [128, 2 * nslots], F32)
    awk_aew = k.sb("awk_aew", [128, 2 * nslots], F32)
    awk_aew1 = k.sb("awk_aew1", [128, 2 * nslots], F32)
    elog = k.sb("elog", [128, n_exp], F32)
    selb = k.sb("selb", [128, n_exp], F32)
    l0l1 = k.sb("l0l1", [128, 2], F32)
    zz20 = k.sb("zz20", [128, n_ag], F32)
    sc20 = k.sb("sc20", [128, n_ag], F32)
    k.v(lambda e: e.memset(zz20[:], 0.0))

    selb_all = k.sb("selb_all", [128, 4 * n_exp], F32)
    for bm in range(4):
        k.t(lambda e, bm=bm: e.matmul(ps_all[:, 32:32 + n_exp], cm.ones1[:],
                                      sel[0:1, bm * n_exp:(bm + 1) * n_exp],
                                      start=True, stop=True),
            waits=[(k.vt, sel_vt), k.vt], milestone=True)
        k.v(lambda e, bm=bm: e.tensor_copy(out=selb_all[:, bm * n_exp:(bm + 1) * n_exp],
                                           in_=ps_all[:, 32:32 + n_exp]), waits=[k.tt])

    k.dma("sync", x_sb[0][:], xm_in[0, :, :], xs[0])
    for ti in range(nslots):
        buf = ti % 2
        if ti + 1 < nslots:
            k.dma("sync", x_sb[(ti + 1) % 2][:], xm_in[ti + 1, :, :], xs[(ti + 1) % 2],
                  waits=[k.vt, k.st])
        ldw = [(xs[buf], 16 * (ti // 2 + 1))]
        emit_ln(k, cm, x_sb[buf][:], xn2[:], d, x_waits=ldw)
        k.v(lambda e: e.tensor_mul(cm.sqd[:, :d], xn2[:], gain_g[:]))
        k.v(lambda e: e.tensor_tensor(out=cm.sqd[:, :d], in0=cm.sqd[:, :d],
                                      in1=bias_g[:], op=ALU.add))
        k.v(lambda e: e.tensor_copy(out=xn2[:], in_=cm.sqd[:, :d]))
        k.v(lambda e: e.tensor_tensor(out=xn2lo[:], in0=cm.sqd[:, :d], in1=xn2[:],
                                      op=ALU.subtract))
        for c in range(nch):
            k.t(lambda e, c=c: e.transpose(out=ps_tr[:], in_=xn2[:, 128 * c:128 * (c + 1)],
                                           identity=cm.ident[:]), waits=[k.vt],
                milestone=True)
            k.v(lambda e, c=c, ti=ti: e.tensor_copy(
                out=xn2T[:, c * Tl + ti * 128: c * Tl + ti * 128 + 128], in_=ps_tr[:]),
                waits=[k.tt])
        for c in range(nch):
            k.t(lambda e, c=c: e.transpose(out=ps_tr[:], in_=xn2lo[:, 128 * c:128 * (c + 1)],
                                           identity=cm.ident[:]), waits=[k.vt],
                milestone=True)
            k.v(lambda e, c=c: e.tensor_copy(out=xn2loT[:, 128 * c:128 * (c + 1)],
                                             in_=ps_tr[:]), waits=[k.tt])
        nmm_ag = 0
        for c in range(nch):
            for (hi_side, tblo) in ((True, 0), (True, 1), (False, 0)):
                nmm_ag += 1
                k.t(lambda e, c=c, ti=ti, hi_side=hi_side, tblo=tblo,
                    st_=(nmm_ag == 1), sp_=(nmm_ag == 3 * nch): e.matmul(
                    ps_all[:, 0:n_ag],
                    (xn2T[:, c * Tl + ti * 128: c * Tl + ti * 128 + 128]
                     if hi_side else xn2loT[:, 128 * c:128 * (c + 1)]),
                    ag_all[:, (tblo * nch + c) * n_ag:(tblo * nch + c + 1) * n_ag],
                    start=st_, stop=sp_),
                    waits=[k.vt, (gsem, gld)], milestone=(nmm_ag == 3 * nch))
        k.s(lambda e: e.activation(out=aw[:], in_=ps_all[:, 0:n_ag], func=AF.Exp),
            waits=[(k.tt, k.tt.n)])
        k.v(lambda e: e.tensor_reduce(out=awm[:], in_=aw[:], axis=AX.X, op=ALU.add),
            waits=[k.st])
        k.v(lambda e: e.reciprocal(awm[:], awm[:]))
        k.v(lambda e: e.tensor_scalar_mul(aw[:], aw[:], awm[:]))
        for br in range(2):
            k.v(lambda e: e.tensor_reduce(out=awm[:], in_=aw[:], axis=AX.X, op=ALU.max))
            k.v(lambda e: e.tensor_scalar(out=awt[:], in0=aw[:], scalar1=awm[:],
                                          scalar2=None, op0=ALU.is_ge))
            k.v(lambda e: e.tensor_tensor_scan(out=sc20[:], data0=awt[:], data1=zz20[:],
                                               initial=0.0, op0=ALU.add, op1=ALU.add))
            k.v(lambda e: e.tensor_scalar(out=sc20[:], in0=sc20[:], scalar1=1.0,
                                          scalar2=None, op0=ALU.is_le))
            k.v(lambda e: e.tensor_mul(awt[:], awt[:], sc20[:]))
            k.v(lambda e, ti=ti, br=br: e.tensor_copy(
                out=ohb[:, (ti * 2 + br) * n_ag:(ti * 2 + br + 1) * n_ag], in_=awt[:]))
            k.v(lambda e, ti=ti, br=br: e.tensor_copy(out=awk[:, ti * 2 + br: ti * 2 + br + 1],
                                                      in_=awm[:]))
            k.v(lambda e: e.scalar_tensor_tensor(out=aw[:], in0=awt[:], scalar=NEG,
                                                 in1=aw[:], op0=ALU.mult, op1=ALU.add))
        k.v(lambda e, ti=ti: e.tensor_tensor(out=awm[:], in0=awk[:, ti * 2:ti * 2 + 1],
                                             in1=awk[:, ti * 2 + 1:ti * 2 + 2], op=ALU.add))
        k.v(lambda e: e.tensor_scalar_add(awm[:], awm[:], 1e-6))
        k.v(lambda e: e.reciprocal(awm[:], awm[:]))
        k.v(lambda e, ti=ti: e.tensor_scalar_mul(awk[:, ti * 2:ti * 2 + 2],
                                                 awk[:, ti * 2:ti * 2 + 2], awm[:]))
        for c in range(nch):
            k.t(lambda e, c=c, ti=ti: e.matmul(
                ps_all[:, 0:n_exp], xn2T[:, c * Tl + ti * 128: c * Tl + ti * 128 + 128],
                eg_all[:, c * n_exp:(c + 1) * n_exp],
                start=(c == 0), stop=(c == nch - 1)),
                waits=[k.vt, (gsem, gld)], milestone=(c == nch - 1))
        k.v(lambda e: e.tensor_copy(out=elog[:], in_=ps_all[:, 0:n_exp]),
            waits=[(k.tt, k.tt.n)])
        for br in range(2):
            for mm in range(2):
                bm = br * 2 + mm
                k.v(lambda e, bm=bm: e.tensor_mul(selb[:], elog[:],
                                                  selb_all[:, bm * n_exp:(bm + 1) * n_exp]))
                k.v(lambda e, mm=mm: e.tensor_reduce(out=l0l1[:, mm:mm + 1], in_=selb[:],
                                                     axis=AX.X, op=ALU.add))
            k.v(lambda e: e.tensor_sub(l0l1[:, 0:1], l0l1[:, 0:1], l0l1[:, 1:2]))
            k.s(lambda e, ti=ti, br=br: e.activation(out=aew[:, ti * 2 + br: ti * 2 + br + 1],
                                                     in_=l0l1[:, 0:1], func=AF.Sigmoid),
                waits=[k.vt])
            co = ti * 2 + br
            k.v(lambda e, co=co: e.tensor_tensor(
                out=awk_aew[:, co:co + 1], in0=awk[:, co:co + 1],
                in1=aew[:, co:co + 1], op=ALU.mult), waits=[k.st])
            k.v(lambda e, co=co: e.tensor_sub(
                awk_aew1[:, co:co + 1], awk[:, co:co + 1], awk_aew[:, co:co + 1]))
    state_vt = k.vt.n
    state_st = k.st.n

    w13sem = k.sem("w13s")
    w13_sb = k.sb("w13_sb", [128, nch * Fm], BF16)
    w33_sb = k.sb("w33_sb", [128, nch * Fm], BF16)
    NW2 = 4
    w2sem = [k.sem(f"w2s{i}") for i in range(NW2)]
    w2_sb = [k.sb(f"w2m_sb{i}", [128, 512], BF16) for i in range(NW2)]
    hbuf = k.sb("hbuf", [128, nfc * 512], BF16)
    silu_sb = k.sb("silu_sb", [128, 512], BF16)
    acc = k.sb("acc", [128, nslots * d], BF16)
    ps_h1 = k.psum("ps_h1", [128, 512])
    ps_h3 = k.psum("ps_h3", [128, 512])
    ps_eo = [k.psum(f"ps_eo{i}", [128, 512]) for i in range(4)]
    ps_ae = ps_all
    ps_trh = ps_tr
    ohT_sb = k.sb("ohT_sb", [n_ag, 128], BF16)
    t_ev = [k.sb(f"t_ev{i}", [128, 512], F32) for i in range(2)]
    k.v(lambda e: e.memset(acc[:], 0.0))

    nfp = Fm // 256
    IndA = IndirectOffsetOnAxis

    def load_w13(bm, waits):
        for c in range(nch):
            k.emit("gpsimd", lambda e, c=c, bm=bm: e.indirect_dma_start(
                out=w13_sb[:, c * Fm:(c + 1) * Fm], out_offset=None,
                in_=w1c_in.rearrange("e d f -> (e d) f"),
                in_offset=IndA(ap=idx13[:, bm * nch + c: bm * nch + c + 1], axis=0),
                element_offset=0),
                waits=(list(waits) if c == 0 else []) + [(k.vt, idx_vt)],
                inc=w13sem, inck=16)
            k.emit("gpsimd", lambda e, c=c, bm=bm: e.indirect_dma_start(
                out=w33_sb[:, c * Fm:(c + 1) * Fm], out_offset=None,
                in_=w3c_in.rearrange("e d f -> (e d) f"),
                in_offset=IndA(ap=idx13[:, bm * nch + c: bm * nch + c + 1], axis=0),
                element_offset=0),
                waits=[], inc=w13sem, inck=16)

    nhalf = (d + 511) // 512

    def load_w2(i, waits):
        bm = i // (geo.groups * nhalf * nfc)
        rem = i % (geo.groups * nhalf * nfc)
        fc = rem % nfc
        half = (rem // nfc) % nhalf
        b = i % NW2
        hw = min(512, d - half * 512)
        k.emit("gpsimd",
               lambda e, b=b, fc=fc, bm=bm, half=half, hw=hw: e.indirect_dma_start(
                   out=w2_sb[b][:, 0:hw], out_offset=None,
                   in_=w2c_in.rearrange("e f dd -> (e f) dd"),
                   in_offset=IndA(ap=idx2[:, bm * nfc + fc: bm * nfc + fc + 1], axis=0),
                   element_offset=half * 512),
               waits=waits + [(k.vt, idx_vt)], inc=w2sem[b], inck=16)

    n2 = 4 * geo.groups * nhalf * nfc
    load_w13(0, [])
    for i in range(NW2):
        load_w2(i, [])
    i2 = 0
    for bm in range(4):
        br, mm = divmod(bm, 2)
        w13_tgt = 16 * 2 * nch * (bm + 1)
        for g in range(geo.groups):
            for fp in range(nfp):
                for sub in range(2):
                    fc = fp * 2 + sub
                    for c in range(nch):
                        w = []
                        if c == 0:
                            w = [(k.vt, state_vt), (k.st, k.st.n), (k.vt, k.vt.n)]
                            if g == 0 and fp == 0 and sub == 0:
                                w.append((w13sem, w13_tgt))
                        k.t(lambda e2, c=c, g=g, fp=fp, sub=sub: e2.matmul(
                            ps_h1[:], w13_sb[:, c * Fm + fp * 256 + sub * 128:
                                             c * Fm + fp * 256 + (sub + 1) * 128],
                            xn2T[:, c * Tl + g * 512: c * Tl + (g + 1) * 512],
                            start=(c == 0), stop=(c == nch - 1)),
                            waits=w, milestone=(c == nch - 1))
                    for c in range(nch):
                        k.t(lambda e2, c=c, g=g, fp=fp, sub=sub: e2.matmul(
                            ps_h3[:], w33_sb[:, c * Fm + fp * 256 + sub * 128:
                                             c * Fm + fp * 256 + (sub + 1) * 128],
                            xn2T[:, c * Tl + g * 512: c * Tl + (g + 1) * 512],
                            start=(c == 0), stop=(c == nch - 1)),
                            waits=[], milestone=(c == nch - 1))
                    tmm = k.tt.n
                    k.s(lambda e2: e2.activation(out=silu_sb[:], in_=ps_h1[:],
                                                 func=AF.Silu), waits=[(k.tt, tmm)])
                    k.v(lambda e2, fc=fc: e2.tensor_tensor(
                        out=hbuf[:, fc * 512:(fc + 1) * 512], in0=silu_sb[:],
                        in1=ps_h3[:], op=ALU.mult), waits=[k.st])
            if g == geo.groups - 1 and bm + 1 < 4:
                load_w13(bm + 1, [(k.tt, k.tt.n)])
            h_done = k.vt.n
            for half in range(nhalf):
                hw = min(512, d - half * 512)
                for fc in range(nfc):
                    b = i2 % NW2
                    w_w = (w2sem[b], 16 * (i2 // NW2 + 1))
                    for kk in range(4):
                        k.t(lambda e2, fc=fc, kk=kk, b=b, hw=hw: e2.matmul(
                            ps_eo[kk][:, 0:hw],
                            hbuf[:, fc * 512 + kk * 128: fc * 512 + (kk + 1) * 128],
                            w2_sb[b][:, 0:hw],
                            start=(fc == 0), stop=(fc == nfc - 1)),
                            waits=([w_w, (k.vt, h_done), (k.st, k.st.n)]
                                   if (kk == 0 and fc == 0)
                                   else ([w_w] if kk == 0 else [])),
                            milestone=(kk == 3))
                    if i2 + NW2 < n2:
                        load_w2(i2 + NW2, [(k.tt, k.tt.n)])
                    i2 += 1
                eo_tt = k.tt.n
                wcol = awk_aew if mm == 0 else awk_aew1
                for kk in range(4):
                    ti = 4 * g + kk
                    sl = slice(ti * d + half * 512, ti * d + half * 512 + hw)
                    k.t(lambda e2, ti=ti, br=br: e2.transpose(
                        out=ps_trh[0:n_ag, :],
                        in_=ohb[:, (ti * 2 + br) * n_ag:(ti * 2 + br + 1) * n_ag],
                        identity=cm.ident[:]),
                        waits=[(k.vt, state_vt), (k.vt, k.vt.n)], milestone=True)
                    k.v(lambda e2: e2.tensor_copy(out=ohT_sb[0:n_ag, :],
                                                  in_=ps_trh[0:n_ag, :]),
                        waits=[k.tt])
                    k.t(lambda e2, half=half, hw=hw: e2.matmul(
                        ps_ae[:, 0:hw], ohT_sb[0:n_ag, :],
                        re_sb[:, half * 512: half * 512 + hw],
                        start=True, stop=True),
                        waits=[k.vt, (gsem, gld)], milestone=True)
                    k.s(lambda e2, kk=kk, ti=ti, hw=hw, br=br, wcol=wcol: e2.mul(
                        t_ev[0][:, 0:hw], ps_eo[kk][:, 0:hw],
                        wcol[:, ti * 2 + br: ti * 2 + br + 1]),
                        waits=[(k.tt, eo_tt), (k.vt, k.vt.n)])
                    k.v(lambda e2, hw=hw: e2.tensor_tensor(
                        out=t_ev[1][:, 0:hw], in0=t_ev[0][:, 0:hw],
                        in1=ps_ae[:, 0:hw], op=ALU.mult), waits=[k.st, k.tt])
                    k.v(lambda e2, hw=hw: e2.scalar_tensor_tensor(
                        out=t_ev[1][:, 0:hw], in0=t_ev[1][:, 0:hw], scalar=0.1,
                        in1=t_ev[0][:, 0:hw], op0=ALU.mult, op1=ALU.add))
                    k.v(lambda e2, sl=sl, hw=hw: e2.tensor_tensor(
                        out=acc[:, sl], in0=acc[:, sl], in1=t_ev[1][:, 0:hw],
                        op=ALU.add))

    dbq = k.sem("dbq")
    k.dma("sync", dbg_awk[:, :], awk[:], dbq, waits=[k.vt, k.st])
    k.dma("sync", dbg_aew[:, :], aew[:], dbq, waits=[k.vt, k.st])
    xs2 = [k.sem("xt0"), k.sem("xt1")]
    xo2 = [k.sem("xo0"), k.sem("xo1")]
    x2_sb = x_sb
    xo_sb = [cm.sqd[:, 0:d], k.sb("xo_sb1", [128, d], F32)]
    yq = [k.sem("yq0"), k.sem("yq1")]
    y_sb = [gain_g, bias_g]
    k.dma("sync", x2_sb[0][:], xm_in[0, :, :], xs2[0], waits=[k.vt, k.st])
    k.dma("sync", xo_sb[0][:], xo_in[0, :, :], xo2[0], waits=[k.vt, k.st])
    moe_done = k.vt.n
    for ti in range(nslots):
        buf = ti % 2
        if ti + 1 < nslots:
            k.dma("sync", x2_sb[(ti + 1) % 2][:], xm_in[ti + 1, :, :], xs2[(ti + 1) % 2],
                  waits=[k.vt])
            k.dma("sync", xo_sb[(ti + 1) % 2][:], xo_in[ti + 1, :, :], xo2[(ti + 1) % 2],
                  waits=[k.vt])
        k.v(lambda e, ti=ti, buf=buf: e.tensor_tensor(
            out=y_sb[buf][:], in0=x2_sb[buf][:], in1=acc[:, ti * d:(ti + 1) * d],
            op=ALU.add),
            waits=[(xs2[buf], 16 * (ti // 2 + 1)), (k.vt, moe_done),
                   (yq[buf], yq[buf].n), k.vt])
        k.v(lambda e, buf=buf: e.tensor_tensor(
            out=y_sb[buf][:], in0=y_sb[buf][:], in1=xo_sb[buf][:], op=ALU.add),
            waits=[(xo2[buf], 16 * (ti // 2 + 1))])
        k.dma("sync", y_out[ti, :, :], y_sb[buf][:], yq[buf], waits=[k.vt])
    k.emit("sync", lambda e: e.nop(), waits=[(yq[0], yq[0].n), (yq[1], yq[1].n)])
    return k



def build_attnb_prep(geo: Geo, name="abprep"):
    k = K(name)
    d = geo.d
    nch = geo.dchunks
    half = nch // 2
    Tl = geo.T_local

    x_in = k.inp("x", [geo.n_slots, 128, d], F32)
    wi_in = k.inp("wi", [d, d], F32)
    wp_in = k.inp("wp", [d, d], F32)
    rc_in = k.inp("ropeC", [128, half * Tl], F32)
    rs_in = k.inp("ropeS", [128, half * Tl], F32)
    it_out = k.outp("it", [d, Tl], BF16)
    pt_out = k.outp("pt", [d, Tl], BF16)
    pn_out = k.outp("pn", [geo.n_slots, 128, d], BF16)
    in_out = k.outp("inn", [geo.n_slots, 128, d], BF16)

    cm = Common(k, d)
    cm.emit_init(k)

    wsem = k.sem("wld")
    wi_sb = k.sb("wi_sb", [128, nch * d], BF16)
    wp_sb = k.sb("wp_sb", [128, nch * d], BF16)
    for c in range(nch):
        k.dma("gpsimd", wi_sb[:, c * d:(c + 1) * d], wi_in[128 * c:128 * (c + 1), :], wsem)
        k.dma("gpsimd", wp_sb[:, c * d:(c + 1) * d], wp_in[128 * c:128 * (c + 1), :], wsem)
    rsem = k.sem("rld")
    rc_sb = k.sb("rc_sb", [128, half * Tl], F32)
    rs_sb = k.sb("rs_sb", [128, half * Tl], F32)
    k.dma("sync", rc_sb[:], rc_in[:, :], rsem)
    k.dma("sync", rs_sb[:], rs_in[:, :], rsem)
    rld = rsem.n

    xs = [k.sem("xs0"), k.sem("xs1")]
    x_sb = [k.sb(f"x_sb{i}", [128, d], F32) for i in range(2)]
    xn_sb = k.sb("xn_sb", [128, d], BF16)
    xnT = k.sb("xnT", [128, nch * 512], BF16)
    raw = k.sb("raw", [128, nch * 512], BF16)
    rop = k.sb("rop", [128, nch * 512], BF16)
    tok = k.sb("tok", [128, d], BF16)
    ssq = k.sb("ssq", [128, 1], F32)
    rn = k.sb("rn", [128, 1], F32)
    sqd2 = k.sb("sqd2", [128, max(d, 512)], F32)
    ps_tr = k.psum("ps_tr", [128, 128], BF16)
    ps_p = [k.psum(f"ps_p{i}", [128, 512]) for i in range(2)]
    oq = k.sem("oq")
    oq2 = k.sem("oq2")

    k.dma("sync", x_sb[0][:], x_in[0, :, :], xs[0])
    for m in range(geo.groups):
        for kk in range(4):
            ti = 4 * m + kk
            buf = ti % 2
            if ti + 1 < geo.n_slots:
                k.dma("sync", x_sb[(ti + 1) % 2][:], x_in[ti + 1, :, :], xs[(ti + 1) % 2],
                      waits=[k.vt, k.st])
            emit_ln(k, cm, x_sb[buf][:], xn_sb[:], d,
                    x_waits=[(xs[buf], 16 * (ti // 2 + 1))])
            for c in range(nch):
                k.t(lambda e, c=c: e.transpose(out=ps_tr[:], in_=xn_sb[:, 128 * c:128 * (c + 1)],
                                               identity=cm.ident[:]),
                    waits=[k.vt], milestone=True)
                k.v(lambda e, c=c, kk=kk: e.tensor_copy(
                    out=xnT[:, c * 512 + kk * 128: c * 512 + kk * 128 + 128],
                    in_=ps_tr[:]), waits=[k.tt])
        for wname, w_sb, t_dst, tm_dst in (("i", wi_sb, it_out, in_out),
                                           ("p", wp_sb, pt_out, pn_out)):
            for oc in range(nch):
                pb = ps_p[oc % 2]
                for c in range(nch):
                    k.t(lambda e, c=c, oc=oc, pb=pb, w_sb=w_sb: e.matmul(
                        pb[:], w_sb[:, c * d + 128 * oc: c * d + 128 * (oc + 1)],
                        xnT[:, c * 512:(c + 1) * 512],
                        start=(c == 0), stop=(c == nch - 1)),
                        waits=([k.vt, wsem] if c == 0 else []),
                        milestone=(c == nch - 1))
                k.v(lambda e, oc=oc, pb=pb: e.tensor_copy(
                    out=raw[:, oc * 512:(oc + 1) * 512], in_=pb[:]),
                    waits=[(k.tt, k.tt.n)])
            for c in range(nch):
                cf = c if c < half else c - half
                partner = c + half if c < half else c - half
                sgn = ALU.subtract if c < half else ALU.add
                co = slice(cf * Tl + m * 512, cf * Tl + (m + 1) * 512) if False else \
                    slice(cf * Tl // 1 + m * 512, cf * Tl // 1 + (m + 1) * 512)
                k.v(lambda e, c=c, cf=cf: e.tensor_tensor(
                    out=rop[:, c * 512:(c + 1) * 512],
                    in0=raw[:, c * 512:(c + 1) * 512],
                    in1=rc_sb[:, cf * Tl + m * 512: cf * Tl + (m + 1) * 512],
                    op=ALU.mult), waits=[(rsem, rld), (oq, oq.n)])
                k.v(lambda e, c=c, cf=cf, partner=partner: e.tensor_tensor(
                    out=sqd2[:, 0:512],
                    in0=raw[:, partner * 512:(partner + 1) * 512],
                    in1=rs_sb[:, cf * Tl + m * 512: cf * Tl + (m + 1) * 512],
                    op=ALU.mult))
                k.v(lambda e, c=c, sgn=sgn: e.tensor_tensor(
                    out=rop[:, c * 512:(c + 1) * 512],
                    in0=rop[:, c * 512:(c + 1) * 512], in1=sqd2[:, 0:512], op=sgn))
            rvt = k.vt.n
            for c in range(nch):
                k.dma("sync", t_dst[128 * c:128 * (c + 1), m * 512:(m + 1) * 512],
                      rop[:, c * 512:(c + 1) * 512], oq, waits=[(k.vt, rvt)])
            for kk in range(4):
                ti = 4 * m + kk
                for c in range(nch):
                    k.t(lambda e, c=c, kk=kk: e.transpose(
                        out=ps_tr[:], in_=rop[:, c * 512 + kk * 128: c * 512 + kk * 128 + 128],
                        identity=cm.ident[:]),
                        waits=[(k.vt, k.vt.n), (k.st, k.st.n)], milestone=True)
                    k.v(lambda e, c=c: e.tensor_copy(out=tok[:, 128 * c:128 * (c + 1)],
                                                     in_=ps_tr[:]),
                        waits=[k.tt, (oq2, oq2.n)])
                k.s(lambda e: e.activation(out=sqd2[:, 0:d], in_=tok[:], func=AF.Square,
                                           accum_out=ssq[:]), waits=[k.vt])
                k.s(lambda e: e.activation(out=rn[:], in_=ssq[:], func=AF.Sqrt))
                k.v(lambda e: e.tensor_scalar_max(rn[:], rn[:], 1e-12), waits=[k.st])
                k.v(lambda e: e.reciprocal(rn[:], rn[:]))
                k.v(lambda e: e.tensor_scalar_mul(tok[:], tok[:], rn[:]))
                k.dma("sync", tm_dst[ti, :, :], tok[:], oq2, waits=[k.vt])

    k.emit("sync", lambda e: e.nop(), waits=[(oq, oq.n), (oq2, oq2.n)])
    return k


def build_attnb_topk(geo: Geo, name="abtopk", K_top=15):
    k = K(name)
    d = geo.d
    nch = geo.dchunks
    Tl = geo.T_local
    S = geo.S
    cpb = geo.cpb

    it_in = k.inp("it", [d, Tl], BF16)
    pt_in = k.inp("pt", [d, S], BF16)
    mk_in = k.inp("masksq", [cpb, 128, 128], F32)
    keep_in = k.inp("keep", [geo.n_slots, 128, 16], F32)
    idx_out = k.outp("idxg", [geo.n_slots, 128, 16], U32)

    cm = Common(k, d)
    cm.emit_init(k)

    qsem = k.sem("qld")
    it_sb = k.sb("it_sb", [128, nch * Tl], BF16)
    for c in range(nch):
        k.dma("sync", it_sb[:, c * Tl:(c + 1) * Tl], it_in[128 * c:128 * (c + 1), :], qsem)
    qld = qsem.n
    msem = k.sem("mld")
    mask_sb = k.sb("mask_sb", [128, cpb * 128], F32)
    for p in range(cpb):
        k.dma("sync", mask_sb[:, 128 * p:128 * (p + 1)], mk_in[p, :, :], msem)
    mld = msem.n
    ksem = k.sem("kld")
    keep_sb = k.sb("keep_sb", [128, 16], F32)

    ptsem = [k.sem("pt0"), k.sem("pt1")]
    pt_sb = [k.sb(f"pt_sb{i}", [128, nch * 512], BF16) for i in range(2)]
    logit = k.sb("logit", [128, S], F32)
    m8 = k.sb("m8", [128, 8], F32)
    i8 = k.sb("i8", [128, 16], U32)
    idxf = k.sb("idxf", [128, 16], F32)
    rep = k.sb("rep", [128, S], F32)
    ps_s = [k.psum(f"ps_s{i}", [128, 512]) for i in range(2)]
    oq = k.sem("oq")

    sched = []
    for ti in range(geo.n_slots):
        nj = (cpb * (ti + 1) * 128 + 511) // 512
        for j5 in range(nj):
            sched.append((ti, j5))

    def load_pt(i, waits):
        ti, j5 = sched[i]
        b = i % 2
        for c in range(nch):
            k.dma("gpsimd", pt_sb[b][:, c * 512:(c + 1) * 512],
                  pt_in[128 * c:128 * (c + 1), 512 * j5:512 * (j5 + 1)], ptsem[b],
                  waits=waits)

    load_pt(0, [])
    load_pt(1, [])
    ii = 0
    for ti in range(geo.n_slots):
        Wt = cpb * (ti + 1) * 128
        nj = Wt // 512
        k.dma("sync", keep_sb[:], keep_in[ti, :, :], ksem, waits=[k.vt])
        kld_ti = ksem.n
        for j5 in range(nj):
            b = ii % 2
            pb = ps_s[j5 % 2]
            w_w = (ptsem[b], 16 * nch * (ii // 2 + 1))
            for c in range(nch):
                k.t(lambda e, c=c, ti=ti, j5=j5, b=b, pb=pb: e.matmul(
                    pb[:], it_sb[:, c * Tl + ti * 128: c * Tl + ti * 128 + 128],
                    pt_sb[b][:, c * 512:(c + 1) * 512],
                    start=(c == 0), stop=(c == nch - 1)),
                    waits=([w_w, (qsem, qld), (k.vt, k.vt.n)] if c == 0 else []),
                    milestone=(c == nch - 1))
            if ii + 2 < len(sched):
                load_pt(ii + 2, [(k.tt, k.tt.n)])
            ii += 1
            k.v(lambda e, j5=j5, pb=pb: e.tensor_copy(
                out=logit[:, 512 * j5:512 * (j5 + 1)], in_=pb[:]),
                waits=[(k.tt, k.tt.n)])
        for p in range(cpb):
            j = cpb * ti + p
            k.v(lambda e, j=j, p=p: e.tensor_tensor(
                out=logit[:, 128 * j:128 * (j + 1)], in0=logit[:, 128 * j:128 * (j + 1)],
                in1=mask_sb[:, 128 * p:128 * (p + 1)], op=ALU.add),
                waits=[(msem, mld)])
        k.v(lambda e, Wt=Wt: e.max(out=m8[:], in_=logit[:, 0:Wt]),
            waits=[(oq, oq.n)])
        k.v(lambda e, Wt=Wt: e.max_index(out=i8[:, 0:8], in_max=m8[:],
                                         in_values=logit[:, 0:Wt]))
        k.v(lambda e, Wt=Wt: e.match_replace(out=rep[:, 0:Wt], in_to_replace=m8[:],
                                             in_values=logit[:, 0:Wt], imm_value=NEG))
        k.v(lambda e, Wt=Wt: e.max(out=m8[:], in_=rep[:, 0:Wt]))
        k.v(lambda e, Wt=Wt: e.max_index(out=i8[:, 8:16], in_max=m8[:],
                                         in_values=rep[:, 0:Wt]))
        k.v(lambda e: e.tensor_copy(out=idxf[:], in_=i8[:]))
        k.v(lambda e: e.tensor_mul(idxf[:], idxf[:], keep_sb[:]),
            waits=[(ksem, kld_ti)])
        k.v(lambda e: e.scalar_tensor_tensor(out=idxf[:], in0=keep_sb[:],
                                             scalar=-float(S), in1=idxf[:],
                                             op0=ALU.mult, op1=ALU.add))
        k.v(lambda e: e.tensor_scalar_add(idxf[:], idxf[:], float(S)))
        k.v(lambda e: e.tensor_copy(out=i8[:], in_=idxf[:]))
        k.dma("sync", idx_out[ti, :, :], i8[:], oq, waits=[k.vt])

    k.emit("sync", lambda e: e.nop(), waits=[(oq, oq.n)])
    return k


def build_attnb_main(geo: Geo, name="abmain", K_top=15):
    k = K(name)
    d = geo.d
    nch = geo.dchunks
    Tl = geo.T_local
    S = geo.S

    x_in = k.inp("x", [geo.n_slots, 128, d], F32)
    gt_in = k.inp("gtab", [S + 1, d], BF16)
    in_in = k.inp("inn", [geo.n_slots, 128, d], BF16)
    idx_in = k.inp("idxg", [geo.n_slots, 128, 16], U32)
    tg_in = k.inp("tg", [geo.n_slots, 128, 1], F32)
    keep_in = k.inp("keep", [geo.n_slots, 128, 16], F32)
    kneg_in = k.inp("kneg", [geo.n_slots, 128, 16], F32)
    w1_in = k.inp("w1", [17, 64], F32)
    w2_in = k.inp("w2", [64, 64], F32)
    wc_in = k.inp("wc", [64, 2], F32)
    wm_in = k.inp("wm", [64, 1], F32)
    b1_in = k.inp("b1", [64], F32)
    b2_in = k.inp("b2", [64], F32)
    bc_in = k.inp("bc", [2], F32)
    bm_in = k.inp("bm", [1], F32)
    pal_in = k.inp("palette", [d, 256], F32)
    wo_in = k.inp("wo", [d, d], F32)
    y_out = k.outp("y", [geo.n_slots, 128, d], F32)

    cm = Common(k, d)
    cm.emit_init(k)
    k.const(1.0)
    k.const(-1.0)

    ssem = k.sem("sld")
    w1_sb = k.sb("w1_sb", [17, 64], BF16)
    w2_sb = k.sb("w2_sb", [64, 64], BF16)
    wcm_sb = k.sb("wcm_sb", [64, 3], BF16)
    k.dma("gpsimd", w1_sb[:], w1_in[:, :], ssem)
    k.dma("gpsimd", w2_sb[:], w2_in[:, :], ssem)
    k.dma("gpsimd", wcm_sb[:, 0:2], wc_in[:, :], ssem)
    k.dma("gpsimd", wcm_sb[:, 2:3], wm_in[:, :], ssem)
    b_sem = k.sem("bld")
    b1c = k.sb("b1c", [64, 1], F32)
    b2c = k.sb("b2c", [64, 1], F32)
    bcm = k.sb("bcm", [3, 1], F32)
    k.dma("sync", b1c[:], b1_in[:, None], b_sem)
    k.dma("sync", b2c[:], b2_in[:, None], b_sem)
    k.dma("sync", bcm[0:2, :], bc_in[:, None], b_sem)
    k.dma("sync", bcm[2:3, :], bm_in[:, None], b_sem)
    sld = ssem.n
    bld = b_sem.n

    wsem = k.sem("wld")
    pal_sb = k.sb("pal_sb", [128, nch * 256], BF16)
    wo_sb = k.sb("wo_sb", [128, nch * d], BF16)
    for c in range(nch):
        k.dma("gpsimd", pal_sb[:, c * 256:(c + 1) * 256], pal_in[128 * c:128 * (c + 1), :], wsem)
        k.dma("gpsimd", wo_sb[:, c * d:(c + 1) * d], wo_in[128 * c:128 * (c + 1), :], wsem)
    wld = wsem.n
    wp_sb = k.sb("wp_sb", [128, 2 * d], BF16)
    ps_w = k.psum("ps_w", [128, 512])
    for r in range(2):
        for fh in range((d + 511) // 512):
            hw = min(512, d - fh * 512)
            for c in range(nch):
                k.t(lambda e, r=r, fh=fh, c=c, hw=hw: e.matmul(
                    ps_w[:, 0:hw], pal_sb[:, c * 256 + 128 * r: c * 256 + 128 * (r + 1)],
                    wo_sb[:, c * d + fh * 512: c * d + fh * 512 + hw],
                    start=(c == 0), stop=(c == nch - 1)),
                    waits=([(wsem, wld), k.vt] if c == 0 else []),
                    milestone=(c == nch - 1))
            k.v(lambda e, r=r, fh=fh, hw=hw: e.tensor_copy(
                out=wp_sb[:, r * d + fh * 512: r * d + fh * 512 + hw], in_=ps_w[:, 0:hw]),
                waits=[k.tt])
    wp_vt = k.vt.n

    gsem = [k.sem("g0"), k.sem("g1")]
    gdq = k.sem("gdq")
    isem = k.sem("ild")
    xsem = [k.sem("xs0"), k.sem("xs1")]
    idq = k.sem("idq")
    A_sb = k.sb("A_sb", [128, 17 * d], BF16)
    idx_sb = k.sb("idx_sb", [128, 16], U32)
    idxf = k.sb("idxf", [128, 16], F32)
    tg_sb = k.sb("tg_sb", [128, 1], F32)
    keep_sb = k.sb("keep_sb", [128, 16], F32)
    kneg_sb = k.sb("kneg_sb", [128, 16], F32)
    ATall = k.sb("ATall", [128, nch * 2048], BF16)
    relT = k.sb("relT", [17, 2048], BF16)
    dl = k.sb("dl", [128, 16], F32)
    dlrow = k.sb("dlrow", [1, 2048], F32)
    h1 = k.sb("h1", [64, 2048], BF16)
    h2 = k.sb("h2", [64, 2048], BF16)
    zmix = k.sb("zmix", [3, 2048], F32)
    gx = k.sb("gx", [128, 16], F32)
    gy = k.sb("gy", [128, 16], F32)
    wsm = k.sb("wsm", [128, 16], F32)
    wrow = k.sb("wrow", [128, 1], F32)
    fxr = k.sb("fxr", [1, 2048], F32)
    fyr = k.sb("fyr", [1, 2048], F32)
    wxr = k.sb("wxr", [1, 2048], F32)
    bc16 = k.sb("bc16", [16, 2048], F32)
    wxT = k.sb("wxT", [16, 2048], BF16)
    wyT = k.sb("wyT", [16, 2048], BF16)
    iot16 = k.sb("iot16", [16, 1], F32)
    cmbT = k.sb("cmbT", [128, 2 * 2048], BF16)
    cmb_t = k.sb("cmb_t", [128, 512], BF16)
    cmb = k.sb("cmb", [128, 2 * 128], BF16)
    cmb_f = k.sb("cmb_f", [128, 128], F32)
    x_sb = [k.sb(f"x_sb{i}", [128, d], F32) for i in range(2)]
    y_sb = [k.sb(f"y_sb{i}", [128, d], F32) for i in range(2)]
    yq = [k.sem("yq0"), k.sem("yq1")]
    ps_tr = k.psum("ps_tr", [128, 128], BF16)
    ps_tr_b = k.psum("ps_tr_b", [128, 128], BF16)
    ps_rel = [k.psum(f"ps_rel{i}", [128, 512]) for i in range(2)]
    ps_mlp = k.psum("ps_mlp", [128, 512])
    ps_c = k.psum("ps_c", [128, 512])
    ps_y = k.psum("ps_y", [128, 512])
    ytmp = k.sb("ytmp", [128, 128], BF16)
    flat_d = k.dram("flat_d", [4, 128, 16], F32)
    zrow_d = k.dram("zrow_d", [3, 2048], F32)
    repy_sb = k.sb("repy_sb", [16, 2 * 128], BF16)
    repx_sb = k.sb("repx_sb", [16, 128], BF16)
    repi = k.sb("repi", [16, 128], F32)

    k.g(lambda e: e.iota(out=iot16[:], pattern=[[0, 1]], base=0, channel_multiplier=1,
                         allow_small_or_imprecise_dtypes=True))
    k.g(lambda e: e.iota(out=repi[:], pattern=[[0, 8], [1, 16]], base=0,
                         channel_multiplier=-1, allow_small_or_imprecise_dtypes=True))
    k.v(lambda e: e.tensor_scalar(out=repx_sb[:], in0=repi[:], scalar1=0.0,
                                  scalar2=None, op0=ALU.is_equal), waits=[k.gt])
    for hf in range(2):
        k.g(lambda e, hf=hf: e.iota(out=repi[:], pattern=[[1, 8], [0, 16]], base=8 * hf,
                                    channel_multiplier=-1,
                                    allow_small_or_imprecise_dtypes=True), waits=[k.vt])
        k.v(lambda e, hf=hf: e.tensor_scalar(out=repy_sb[:, hf * 128:(hf + 1) * 128],
                                             in0=repi[:], scalar1=0.0, scalar2=None,
                                             op0=ALU.is_equal), waits=[k.gt])

    for ti in range(geo.n_slots):
        k.dma("sync", idx_sb[:], idx_in[ti, :, :], idq, waits=[k.vt, (k.gt, k.gt.n)])
        k.dma("sync", tg_sb[:], tg_in[ti, :, :], idq)
        k.dma("sync", keep_sb[:], keep_in[ti, :, :], idq)
        k.dma("sync", kneg_sb[:], kneg_in[ti, :, :], idq)
        k.dma("sync", x_sb[ti % 2][:], x_in[ti, :, :], xsem[ti % 2], waits=[k.vt])
        idq_ti = idq.n
        gvt = k.gt.n
        for kslot in range(16):
            b = kslot % 2
            k.emit("gpsimd", lambda e, kslot=kslot: e.indirect_dma_start(
                out=A_sb[:, kslot * d:(kslot + 1) * d], out_offset=None,
                in_=gt_in[:, :],
                in_offset=IndirectOffsetOnAxis(ap=idx_sb[:, kslot:kslot + 1], axis=0)),
                waits=[(idq, idq_ti), (k.tt, k.tt.n)], inc=gsem[b], inck=16)
        k.dma("sync", A_sb[:, 16 * d:17 * d], in_in[ti, :, :], isem,
              waits=[(k.tt, k.tt.n)])
        g_t = (gsem[0], gsem[0].n)
        g_t2 = (gsem[1], gsem[1].n)
        i_t = (isem, isem.n)
        tb_banks = [ps_tr, ps_tr_b]
        tb_tok = [None, None]
        tb_i = 0
        for kslot in range(17):
            if kslot == 15:
                continue
            srcw = [g_t, g_t2] if kslot < 16 else [i_t]
            slotcol = kslot if kslot < 15 else 15
            for c in range(nch):
                bk = tb_i % 2
                pb = tb_banks[bk]
                w = list(srcw)
                if tb_tok[bk] is not None:
                    w.append(tb_tok[bk])
                else:
                    w.append((k.vt, k.vt.n))
                k.t(lambda e, kslot=kslot, c=c, pb=pb: e.transpose(
                    out=pb[:], in_=A_sb[:, kslot * d + 128 * c: kslot * d + 128 * (c + 1)],
                    identity=cm.ident[:]),
                    waits=w, milestone=True)
                dst = (ATall[:].rearrange("p (ch tok sl) -> p ch tok sl", ch=nch, sl=16)
                       [:, c, :, slotcol])
                if tb_i % 2 == 0:
                    t = k.v(lambda e, dst=dst, pb=pb: e.tensor_copy(out=dst, in_=pb[:]),
                            waits=[k.tt])
                    tb_tok[bk] = (k.vt, t)
                else:
                    t = k.s(lambda e, dst=dst, pb=pb: e.activation(
                        out=dst, in_=pb[:], func=AF.Copy), waits=[(k.tt, k.tt.n)])
                    tb_tok[bk] = (k.st, t)
                tb_i += 1
        at_vt = k.vt.n
        for half32 in range(4):
            pb = ps_rel[half32 % 2]
            for t32 in range(32):
                tok = half32 * 32 + t32
                for c in range(nch):
                    k.t(lambda e, tok=tok, t32=t32, c=c, pb=pb: e.matmul(
                        pb[0:16, t32 * 16:(t32 + 1) * 16],
                        ATall[:, c * 2048 + 16 * tok: c * 2048 + 16 * (tok + 1)],
                        ATall[:, c * 2048 + 16 * tok: c * 2048 + 16 * (tok + 1)],
                        start=(c == 0), stop=(c == nch - 1)),
                        waits=([(k.vt, k.vt.n), (k.st, k.st.n)]
                               if (t32 == 0 and c == 0) else []),
                        milestone=(tok % 8 == 7 and c == nch - 1))
            k.v(lambda e, half32=half32, pb=pb: e.tensor_scalar(
                out=relT[0:16, half32 * 512:(half32 + 1) * 512], in0=pb[0:16, :],
                scalar1=1.0, scalar2=-1.0, op0=ALU.min, op1=ALU.max),
                waits=[(k.tt, k.tt.n)])
        k.v(lambda e: e.tensor_copy(out=idxf[:], in_=idx_sb[:]), waits=[(idq, idq_ti)])
        k.v(lambda e: e.scalar_tensor_tensor(out=dl[:], in0=idxf[:], scalar=-1.0,
                                             in1=tg_sb[:].to_broadcast([128, 16]),
                                             op0=ALU.mult, op1=ALU.add))
        k.v(lambda e: e.tensor_relu(dl[:], dl[:]))
        k.v(lambda e: e.tensor_scalar_mul(dl[:], dl[:], 1.0 / S))
        k.v(lambda e: e.tensor_mul(dl[:], dl[:], keep_sb[:]))
        k.dma("sync", flat_d[3, :, :], dl[:], idq, waits=[k.vt])
        dls = idq.n
        k.dma("sync", dlrow[:], flat_d[3].rearrange("p c -> () (p c)"), idq,
              waits=[(idq, dls)])
        k.emit("gpsimd", lambda e: e.dma_start(out=relT[16:17, :], in_=dlrow[:]),
               waits=[(idq, idq.n)], inc=gdq, inck=16)
        dl_t = gdq.n
        rel_vt = k.vt.n
        for fm in range(4):
            k.t(lambda e, fm=fm: e.matmul(ps_mlp[0:64, :], w1_sb[:],
                                          relT[:, fm * 512:(fm + 1) * 512],
                                          start=True, stop=True),
                waits=[(k.vt, rel_vt), (gdq, dl_t), (ssem, sld), (k.st, k.st.n)],
                milestone=True)
            k.s(lambda e, fm=fm: e.activation(out=h1[:, fm * 512:(fm + 1) * 512],
                                              in_=ps_mlp[0:64, :], func=AF.Gelu,
                                              bias=b1c[:]),
                waits=[(k.tt, k.tt.n), (b_sem, bld)])
        for fm in range(4):
            k.t(lambda e, fm=fm: e.matmul(ps_mlp[0:64, :], w2_sb[:],
                                          h1[:, fm * 512:(fm + 1) * 512],
                                          start=True, stop=True),
                waits=[(k.st, k.st.n)], milestone=True)
            k.s(lambda e, fm=fm: e.activation(out=h2[:, fm * 512:(fm + 1) * 512],
                                              in_=ps_mlp[0:64, :], func=AF.Gelu,
                                              bias=b2c[:]),
                waits=[(k.tt, k.tt.n)])
        for fm in range(4):
            k.t(lambda e, fm=fm: e.matmul(ps_mlp[0:3, :], wcm_sb[:],
                                          h2[:, fm * 512:(fm + 1) * 512],
                                          start=True, stop=True),
                waits=[(k.st, k.st.n), (k.vt, k.vt.n)], milestone=True)
            k.v(lambda e, fm=fm: e.tensor_copy(out=zmix[:, fm * 512:(fm + 1) * 512],
                                               in_=ps_mlp[0:3, :]), waits=[k.tt])
        k.v(lambda e: e.tensor_scalar(out=zmix[:], in0=zmix[:], scalar1=bcm[:],
                                      scalar2=None, op0=ALU.add))
        z_vt = k.vt.n
        k.dma("sync", zrow_d[0, :], zmix[0:1, :], idq, waits=[(k.vt, z_vt)])
        k.dma("sync", zrow_d[1, :], zmix[1:2, :], idq)
        k.dma("sync", zrow_d[2, :], zmix[2:3, :], idq)
        zst = idq.n
        k.dma("sync", gx[:], zrow_d[0].rearrange("(p c) -> p c", c=16), idq,
              waits=[(idq, zst)])
        k.dma("sync", gy[:], zrow_d[1].rearrange("(p c) -> p c", c=16), idq)
        k.dma("sync", wsm[:], zrow_d[2].rearrange("(p c) -> p c", c=16), idq)
        mix_t = idq.n
        k.v(lambda e: e.tensor_tensor(out=wsm[:], in0=wsm[:], in1=kneg_sb[:], op=ALU.add),
            waits=[(idq, mix_t)])
        k.s(lambda e: e.activation(out=wsm[:], in_=wsm[:], func=AF.Exp), waits=[k.vt])
        k.v(lambda e: e.tensor_reduce(out=wrow[:], in_=wsm[:], axis=AX.X, op=ALU.add),
            waits=[k.st])
        k.v(lambda e: e.reciprocal(wrow[:], wrow[:]))
        k.v(lambda e: e.tensor_scalar_mul(wsm[:], wsm[:], wrow[:]))
        k.s(lambda e: e.activation(out=gx[:], in_=gx[:], func=AF.Tanh), waits=[k.vt])
        k.s(lambda e: e.activation(out=gy[:], in_=gy[:], func=AF.Tanh))
        k.v(lambda e: e.tensor_scalar(out=gx[:], in0=gx[:], scalar1=1.0, scalar2=7.5,
                                      op0=ALU.add, op1=ALU.mult), waits=[k.st])
        k.v(lambda e: e.tensor_scalar(out=gx[:], in0=gx[:], scalar1=0.0, scalar2=15.0,
                                      op0=ALU.max, op1=ALU.min))
        k.v(lambda e: e.tensor_scalar(out=gy[:], in0=gy[:], scalar1=1.0, scalar2=7.5,
                                      op0=ALU.add, op1=ALU.mult))
        k.v(lambda e: e.tensor_scalar(out=gy[:], in0=gy[:], scalar1=0.0, scalar2=15.0,
                                      op0=ALU.max, op1=ALU.min))
        k.dma("sync", flat_d[0, :, :], gx[:], idq, waits=[k.vt])
        k.dma("sync", flat_d[1, :, :], gy[:], idq)
        k.dma("sync", flat_d[2, :, :], wsm[:], idq)
        fst = idq.n
        k.dma("sync", fxr[:], flat_d[0].rearrange("p c -> () (p c)"), idq,
              waits=[(idq, fst), (k.tt, k.tt.n)])
        k.dma("sync", fyr[:], flat_d[1].rearrange("p c -> () (p c)"), idq)
        k.dma("sync", wxr[:], flat_d[2].rearrange("p c -> () (p c)"), idq)
        f_t = idq.n

        for fm in range(4):
            sl = slice(fm * 512, (fm + 1) * 512)
            k.t(lambda e, sl=sl: e.matmul(ps_c[0:16, :], cm.ones1[0:1, 0:16],
                                          fxr[0:1, sl], start=True, stop=True),
                waits=[(idq, f_t), (k.vt, k.vt.n), (k.st, k.st.n)], milestone=True)
            k.v(lambda e: e.tensor_scalar(out=bc16[:, 0:512], in0=ps_c[0:16, :],
                                          scalar1=iot16[:, 0:1], scalar2=None,
                                          op0=ALU.subtract), waits=[k.tt])
            k.s(lambda e: e.activation(out=bc16[:, 0:512], in_=bc16[0:16, 0:512],
                                       func=AF.Abs), waits=[k.vt])
            k.s(lambda e, sl=sl: e.activation(out=wxT[0:16, sl], in_=bc16[0:16, 0:512],
                                              func=AF.Relu, bias=1.0, scale=-1.0))
            k.t(lambda e, sl=sl: e.matmul(ps_c[0:16, :], cm.ones1[0:1, 0:16],
                                          fyr[0:1, sl], start=True, stop=True),
                waits=[(k.st, k.st.n)], milestone=True)
            k.v(lambda e: e.tensor_scalar(out=bc16[:, 0:512], in0=ps_c[0:16, :],
                                          scalar1=iot16[:, 0:1], scalar2=None,
                                          op0=ALU.subtract), waits=[k.tt])
            k.s(lambda e: e.activation(out=bc16[:, 0:512], in_=bc16[0:16, 0:512],
                                       func=AF.Abs), waits=[k.vt])
            k.s(lambda e, sl=sl: e.activation(out=wyT[0:16, sl], in_=bc16[0:16, 0:512],
                                              func=AF.Relu, bias=1.0, scale=-1.0))
            k.t(lambda e, sl=sl: e.matmul(ps_c[0:16, :], cm.ones1[0:1, 0:16],
                                          wxr[0:1, sl], start=True, stop=True),
                waits=[(k.st, k.st.n)], milestone=True)
            k.v(lambda e, sl=sl: e.tensor_tensor(out=wyT[0:16, sl], in0=wyT[0:16, sl],
                                                 in1=ps_c[0:16, :], op=ALU.mult),
                waits=[k.tt, (k.st, k.st.n)])
        wxy_vt = k.vt.n

        for hf in range(2):
            for fm in range(4):
                sl = slice(fm * 512, (fm + 1) * 512)
                k.t(lambda e, hf=hf, sl=sl: e.matmul(
                    ps_c[:], repy_sb[:, hf * 128:(hf + 1) * 128], wyT[0:16, sl],
                    start=True, stop=True),
                    waits=[(k.vt, k.vt.n), (k.gt, k.gt.n)], milestone=True)
                k.v(lambda e: e.tensor_copy(out=cmb_t[:], in_=ps_c[:]),
                    waits=[k.tt])
                k.t(lambda e, sl=sl: e.matmul(
                    ps_c[:], repx_sb[:], wxT[0:16, sl], start=True, stop=True),
                    waits=[(k.vt, k.vt.n)], milestone=True)
                k.v(lambda e, hf=hf, sl=sl: e.tensor_tensor(
                    out=cmbT[:, hf * 2048 + sl.start: hf * 2048 + sl.stop],
                    in0=cmb_t[:], in1=ps_c[:], op=ALU.mult), waits=[k.tt])
            k.v(lambda e, hf=hf: e.tensor_reduce(
                out=cmb_f[:],
                in_=cmbT[:, hf * 2048:(hf + 1) * 2048].rearrange(
                    "p (t k) -> p t k", k=16),
                axis=AX.X, op=ALU.add))
            k.v(lambda e, hf=hf: e.tensor_copy(out=cmb[:, hf * 128:(hf + 1) * 128],
                                               in_=cmb_f[:]))
        cmb_vt = k.vt.n

        buf = ti % 2
        for oc in range(nch):
            for r in range(2):
                k.t(lambda e, oc=oc, r=r: e.matmul(
                    ps_y[:, 0:128], wp_sb[:, r * d + 128 * oc: r * d + 128 * (oc + 1)],
                    cmb[:, r * 128:(r + 1) * 128],
                    start=(r == 0), stop=(r == 1)),
                    waits=([(k.vt, cmb_vt), (k.vt, wp_vt), (k.st, k.st.n)]
                           if r == 0 else []),
                    milestone=(r == 1))
            k.v(lambda e: e.tensor_copy(out=ytmp[:], in_=ps_y[:, 0:128]),
                waits=[(k.tt, k.tt.n)])
            k.t(lambda e: e.transpose(out=ps_tr[:], in_=ytmp[:], identity=cm.ident[:]),
                waits=[k.vt], milestone=True)
            k.v(lambda e, oc=oc, buf=buf: e.tensor_tensor(
                out=y_sb[buf][:, 128 * oc:128 * (oc + 1)],
                in0=x_sb[buf][:, 128 * oc:128 * (oc + 1)], in1=ps_tr[:], op=ALU.add),
                waits=[k.tt, (xsem[buf], 16 * (ti // 2 + 1)), (yq[buf], yq[buf].n)])
        k.dma("sync", y_out[ti, :, :], y_sb[buf][:], yq[buf], waits=[k.vt])

    k.emit("sync", lambda e: e.nop(), waits=[(yq[0], yq[0].n), (yq[1], yq[1].n)])
    return k


from concourse.bass_utils import run_bass_kernel_spmd
import ml_dtypes

GEO = Geo(n_batch=2, cpb=4, n_slots=16, d=1024)
NC = list(range(8))


def _run(kb, in_maps, out_names):
    nc = kb.finalize()
    res = run_bass_kernel_spmd(nc, in_maps, core_ids=NC)
    out = [{nm: np.asarray(r[nm]) for nm in out_names} for r in res.results]
    kb.close()
    return out


def _shard_cols(tab, geo, core):
    b, cc = divmod(core, geo.cpb)
    R = tab.shape[0]
    t = tab.reshape(R, geo.n_tiles, 128)
    return np.ascontiguousarray(t[:, cc::geo.cpb].reshape(R, geo.T_local))


def _tg_of(geo, core):
    b, cc = divmod(core, geo.cpb)
    g = np.arange(geo.n_slots) * geo.cpb + cc
    return (g[:, None] * 128 + np.arange(128)[None, :]).astype(np.float32)


def _masks_sq(geo, cc, transpose_q):
    a = np.arange(128)[:, None]
    b = np.arange(128)[None, :]
    out = np.zeros((geo.cpb, 128, 128), np.float32)
    for p in range(geo.cpb):
        if transpose_q:
            out[p] = np.where(128 * (p - cc) + b > a, NEG, 0.0)
        else:
            out[p] = np.where(128 * (p - cc) + a > b, NEG, 0.0)
    return out


def _assemble_ktv(geo, resA, nkv):
    S = geo.S
    kt_g = np.zeros((geo.n_batch, nkv * 64, S), ml_dtypes.bfloat16)
    v_g = np.zeros((geo.n_batch, nkv, 128, geo.n_tiles * 65), ml_dtypes.bfloat16)
    for core in range(geo.n_cores):
        b, cc = divmod(core, geo.cpb)
        kt_g[b].reshape(nkv * 64, geo.n_tiles, 128)[:, cc::geo.cpb] = \
            resA[core]["kt"].reshape(nkv * 64, geo.n_slots, 128)
        v_g[b].reshape(nkv, 128, geo.n_tiles, 65)[:, :, cc::geo.cpb] = \
            resA[core]["v"].reshape(nkv, 128, geo.n_slots, 65)
    return kt_g, v_g


def _do_attn(geo, x, cfg, wmap, extras, gate=False, residual_ln=False):
    kb = build_attn_prep(geo, cfg, "prep")
    im = []
    for core in range(geo.n_cores):
        d = {"x": shard_tiles(x, geo, core)}
        d.update(wmap)
        for nm, fn in extras.get("prep", {}).items():
            d[nm] = fn(core)
        im.append(d)
    resA = _run(kb, im, ["qt", "kt", "v"])
    kt_g, v_g = _assemble_ktv(geo, resA, cfg.nkv)

    kb2 = build_attn_main(geo, cfg, "attnB", residual_ln=residual_ln, gate=gate)
    im2 = []
    for core in range(geo.n_cores):
        b, cc = divmod(core, geo.cpb)
        d = {"x": shard_tiles(x, geo, core), "qt": resA[core]["qt"],
             "kt": kt_g[b], "v": v_g[b], "wo": extras["wo"],
             "masks": _masks_sq(geo, cc, transpose_q=False)}
        if gate:
            d["ln2_g"] = extras["ln2_g"]
            d["ln2_b"] = extras["ln2_b"]
            d["agent_gate"] = extras["agent_gate"]
        im2.append(d)
    outs = ["y"] + (["aglast"] if gate else [])
    resB = _run(kb2, im2, outs)
    y = unshard_tiles([r["y"] for r in resB], geo)
    ag = resB[geo.cpb - 1]["aglast"].reshape(1, 10) if gate else None
    return y, ag


def _do_l0ffn(geo, x, ins, pre):
    cfg = AttnCfg(nq=16, nkv=4, dh=64, causal=True, scale=1.0 / 8.0,
                  mode="rope64", inner_ln=True)
    Ct, St = host_rope_head64(geo.S)
    swapT = host_swap64()
    wmap = {"w_q": ins["L0_ffn_wq"], "w_k": ins["L0_ffn_wk"], "w_v": ins["L0_ffn_wv"],
            "ln_g": ins["L0_ffn_ln1_g"], "ln_b": ins["L0_ffn_ln1_b"]}
    extras = {"wo": ins["L0_ffn_wo"],
              "ln2_g": ins["L0_ffn_ln2_g"], "ln2_b": ins["L0_ffn_ln2_b"],
              "agent_gate": ins["L0_ffn_agent_gate"],
              "prep": {"ropeC": lambda c: _shard_cols(Ct, geo, c),
                       "ropeS": lambda c: _shard_cols(St, geo, c),
                       "swapT": lambda c: swapT}}
    xmid, gate_row = _do_attn(geo, x, cfg, wmap, extras, gate=True, residual_ln=True)
    kb3 = build_l0moe(geo, "l0moe", n_ag=10, n_exp=20, Fm=2048)
    im3 = []
    for core in range(geo.n_cores):
        im3.append({"xmid": shard_tiles(xmid, geo, core),
                    "xorig": shard_tiles(x, geo, core),
                    "gate_row": gate_row,
                    "ln2_g": ins["L0_ffn_ln2_g"], "ln2_b": ins["L0_ffn_ln2_b"],
                    "agent_gate": ins["L0_ffn_agent_gate"],
                    "expert_gate": ins["L0_ffn_expert_gate"],
                    "role_emb": ins["L0_ffn_role_emb"],
                    "w1c": pre["l0w1c"], "w3c": pre["l0w3c"], "w2c": pre["l0w2c"]})
    resC = _run(kb3, im3, ["y"])
    return unshard_tiles([r["y"] for r in resC], geo)


def _do_l1ffn(geo, x, ins, pre):
    kb = build_l1ffn(geo, "l1ffn", n_exp=8, F=4096)
    im = []
    for core in range(geo.n_cores):
        im.append({"x": shard_tiles(x, geo, core),
                   "cen": ins["L1_ffn_centroids"],
                   "w1t": pre["l1w1t"], "w2t": pre["l1w2t"],
                   "b1p": pre["l1b1p"], "b2": ins["L1_ffn_b2"]})
    res = _run(kb, im, ["y"])
    return unshard_tiles([r["y"] for r in res], geo)


def _do_attnb(geo, x, ins):
    S = geo.S
    d = geo.d
    half = d // 2
    inv = 1.0 / 10000.0 ** (np.arange(half, dtype=np.float64) / half)
    ang = inv[:, None] * np.arange(S, dtype=np.float64)[None, :]
    Cf = np.cos(ang).astype(np.float32)
    Sf = np.sin(ang).astype(np.float32)

    def shard_rope(tab, core):
        b, cc = divmod(core, geo.cpb)
        nhc = half // 128
        t = tab.reshape(nhc, 128, geo.n_tiles, 128)[:, :, cc::geo.cpb]
        return np.ascontiguousarray(
            t.transpose(1, 0, 2, 3).reshape(128, nhc * geo.T_local))

    kb = build_attnb_prep(geo, "abprep")
    im = [{"x": shard_tiles(x, geo, core), "wi": ins["L0_attn_b_Wi"],
           "wp": ins["L0_attn_b_Wp"],
           "ropeC": shard_rope(Cf, core), "ropeS": shard_rope(Sf, core)}
          for core in range(geo.n_cores)]
    resA = _run(kb, im, ["it", "pt", "pn", "inn"])
    B = geo.n_batch
    pt_g = np.zeros((B, d, S), ml_dtypes.bfloat16)
    gtab = np.zeros((B, S + 1, d), ml_dtypes.bfloat16)
    for core in range(geo.n_cores):
        b, cc = divmod(core, geo.cpb)
        pt_g[b].reshape(d, geo.n_tiles, 128)[:, cc::geo.cpb] = \
            resA[core]["pt"].reshape(d, geo.n_slots, 128)
        gtab[b, :S].reshape(geo.n_tiles, 128, d)[cc::geo.cpb] = \
            resA[core]["pn"].reshape(geo.n_slots, 128, d)

    kb2 = build_attnb_topk(geo, "abtopk")
    im2 = []
    keeps = {}
    for core in range(geo.n_cores):
        b, cc = divmod(core, geo.cpb)
        tg = _tg_of(geo, core)
        keep = ((np.arange(16)[None, None, :] <= tg[:, :, None]) &
                (np.arange(16)[None, None, :] < 15)).astype(np.float32)
        keeps[core] = (tg, keep)
        im2.append({"it": resA[core]["it"], "pt": pt_g[b],
                    "masksq": _masks_sq(geo, cc, transpose_q=True), "keep": keep})
    resB = _run(kb2, im2, ["idxg"])

    kb3 = build_attnb_main(geo, "abmain")
    im3 = []
    for core in range(geo.n_cores):
        b, cc = divmod(core, geo.cpb)
        tg, keep = keeps[core]
        kneg = (1.0 - keep) * NEG
        im3.append({"x": shard_tiles(x, geo, core), "gtab": gtab[b],
                    "inn": resA[core]["inn"],
                    "idxg": resB[core]["idxg"].reshape(geo.n_slots, 128, 16),
                    "tg": tg[:, :, None].copy(), "keep": keep, "kneg": kneg,
                    "w1": ins["L0_attn_b_W1"], "w2": ins["L0_attn_b_W2"],
                    "wc": ins["L0_attn_b_Wc"], "wm": ins["L0_attn_b_Wm"],
                    "b1": ins["L0_attn_b_b1"], "b2": ins["L0_attn_b_b2"],
                    "bc": ins["L0_attn_b_bc"], "bm": ins["L0_attn_b_bm"],
                    "palette": np.ascontiguousarray(
                        ins["L0_attn_b_palette"].reshape(geo.d, 256)),
                    "wo": np.ascontiguousarray(ins["L0_attn_b_Wo"][0])})
    resC = _run(kb3, im3, ["y"])
    return unshard_tiles([r["y"] for r in resC], geo)


def _do_prepass(geo, ins):
    kb = build_prepass(geo, "prepass", F=4096, d=geo.d)
    jobs_a = [("W1", e) for e in range(20)] + [("W3", e) for e in range(20)]
    im = []
    for core in range(8):
        l0a = np.stack([ins["L0_ffn_" + t][e] for (t, e) in jobs_a[5 * core:5 * core + 5]])
        w2jobs = [ins["L0_ffn_W2"][e] for e in range(3 * core, min(20, 3 * core + 3))]
        while len(w2jobs) < 3:
            w2jobs.append(np.zeros((2 * geo.d, geo.d), np.float32))
        l0b = np.stack(w2jobs)
        im.append({"w1": ins["L1_ffn_W1"][core], "w2": ins["L1_ffn_W2"][core],
                   "gamma": ins["L1_ffn_gamma"][core], "beta": ins["L1_ffn_beta"][core],
                   "b1": ins["L1_ffn_b1"][core], "l0a": l0a, "l0b": l0b})
    res = _run(kb, im, ["w1t", "w2t", "b1p", "l0a_o", "l0b_o"])
    pre = {}
    pre["l1w1t"] = np.stack([res[c]["w1t"].reshape(geo.d, 4096) for c in range(8)])
    pre["l1w2t"] = np.stack([res[c]["w2t"].reshape(4096, geo.d) for c in range(8)])
    pre["l1b1p"] = np.stack([res[c]["b1p"].reshape(128, 32) for c in range(8)])
    F2 = 2 * geo.d
    l0w1c = np.zeros((20, geo.d, F2), ml_dtypes.bfloat16)
    l0w3c = np.zeros((20, geo.d, F2), ml_dtypes.bfloat16)
    l0w2c = np.zeros((20, F2, geo.d), ml_dtypes.bfloat16)
    for core in range(8):
        aa = res[core]["l0a_o"].reshape(5, geo.d, F2)
        for j, (t, e) in enumerate(jobs_a[5 * core:5 * core + 5]):
            (l0w1c if t == "W1" else l0w3c)[e] = aa[j]
        bb = res[core]["l0b_o"].reshape(3, F2, geo.d)
        for j, e in enumerate(range(3 * core, min(20, 3 * core + 3))):
            l0w2c[e] = bb[j]
    pre["l0w1c"] = l0w1c
    pre["l0w3c"] = l0w3c
    pre["l0w2c"] = l0w2c
    return pre


def kernel(**ins):
    geo = GEO
    x = np.ascontiguousarray(ins["L0_attn_a_x"], dtype=np.float32)
    pre = _do_prepass(geo, ins)

    pos = host_cosine_pos(geo.S, 64).T
    posT2 = np.concatenate([pos, pos], 0)
    cfg_a = AttnCfg(nq=16, nkv=16, dh=64, causal=True, scale=geo.d ** (-0.5),
                    mode="pos")
    x, _ = _do_attn(geo, x, cfg_a,
                    {"w_q": ins["L0_attn_a_Wq"], "w_k": ins["L0_attn_a_Wk"],
                     "w_v": ins["L0_attn_a_Wv"]},
                    {"wo": ins["L0_attn_a_Wo"],
                     "prep": {"posT": lambda c: _shard_cols(posT2, geo, c)}})
    x = _do_l0ffn(geo, x, ins, pre)
    x = _do_attnb(geo, x, ins)
    x = _do_l0ffn(geo, x, ins, pre)

    Ct, St = host_rope_head64(geo.S)
    swapT = host_swap64()
    cfg_1a = AttnCfg(nq=16, nkv=4, dh=64, causal=False, scale=1.0 / 8.0,
                     mode="rope64")
    x, _ = _do_attn(geo, x, cfg_1a,
                    {"w_q": ins["L1_attn_a_w_q"], "w_k": ins["L1_attn_a_w_k"],
                     "w_v": ins["L1_attn_a_w_v"]},
                    {"wo": ins["L1_attn_a_w_o"],
                     "prep": {"ropeC": lambda c: _shard_cols(Ct, geo, c),
                              "ropeS": lambda c: _shard_cols(St, geo, c),
                              "swapT": lambda c: swapT}})
    x = _do_l1ffn(geo, x, ins, pre)
    cfg_1b = AttnCfg(nq=16, nkv=16, dh=64, causal=True, scale=1.0 / 8.0,
                     mode="none", fused_qkv=True)
    x, _ = _do_attn(geo, x, cfg_1b, {"w_qkv": ins["L1_attn_b_w_qkv"]},
                    {"wo": ins["L1_attn_b_w_out"]})
    x = _do_l1ffn(geo, x, ins, pre)
    return x


# version 06
# speedup vs baseline: 1.6015x; 1.0913x over previous
import numpy as np
from contextlib import ExitStack

import concourse.bass as bass
import concourse.mybir as mybir
from concourse.bass import IndirectOffsetOnAxis

F32 = mybir.dt.float32
BF16 = mybir.dt.bfloat16
U32 = mybir.dt.uint32
AF = mybir.ActivationFunctionType
ALU = mybir.AluOpType
AX = mybir.AxisListType

NEG = -1.0e30


class Geo:
    def __init__(self, n_batch=2, cpb=4, n_slots=16, d=1024):
        self.n_batch = n_batch
        self.cpb = cpb
        self.n_cores = n_batch * cpb
        self.n_slots = n_slots
        self.groups = n_slots // 4
        self.T_local = n_slots * 128
        self.n_tiles = cpb * n_slots
        self.S = self.n_tiles * 128
        self.d = d
        self.dchunks = d // 128

    def chunks_of_group(self, m):
        return 4 * self.cpb * m + 4 * self.cpb


class Sem:
    __slots__ = ("h", "n", "name")

    def __init__(self, h, name):
        self.h = h
        self.n = 0
        self.name = name

    def tick(self, k=1):
        self.n += k
        return self.n


class K:
    ENGS = ("sync", "gpsimd", "vector", "scalar", "tensor")

    def __init__(self, name="k"):
        self.nc = bass.Bass("TRN2", target_bir_lowering=False, debug=False)
        self.ctx = ExitStack()
        self.ops = {e: [] for e in self.ENGS}
        self._consts = {}
        self._cpool = None
        self._cpool_n = 0
        self.name = name
        self.vt = self.sem("vt")
        self.st = self.sem("st")
        self.gt = self.sem("gt")
        self.tt = self.sem("tt")

    def sem(self, name):
        return Sem(self.ctx.enter_context(self.nc.semaphore(name)), name)

    def sb(self, name, shape, dt):
        return self.ctx.enter_context(self.nc.sbuf_tensor(name, shape, dt))

    def psum(self, name, shape, dt=F32):
        return self.ctx.enter_context(self.nc.psum_tensor(name, shape, dt))

    def inp(self, name, shape, dt):
        return self.nc.dram_tensor(name, shape, dt, kind="ExternalInput").ap()

    def outp(self, name, shape, dt):
        return self.nc.dram_tensor(name, shape, dt, kind="ExternalOutput").ap()

    def dram(self, name, shape, dt, addr_space="Local"):
        return self.nc.dram_tensor(name, shape, dt, addr_space=addr_space).ap()

    def const(self, val):
        if self._cpool is None:
            self._cpool = self.sb("cpool", [128, 16], F32)
        val = float(val)
        if val not in self._consts:
            i = self._cpool_n
            assert i < 16
            self._cpool_n += 1
            cell = self._cpool[:, i:i + 1]
            self.nc.const_aps.register(F32, val, cell)
            self._consts[val] = cell
        return val

    def emit(self, eng, build, waits=(), inc=None, inck=1):
        wl = []
        for w in waits:
            if isinstance(w, tuple):
                s, v = w
            else:
                s, v = w, w.n
            if v > 0:
                wl.append((s.h, v))
        n = None
        if inc is not None:
            n = inc.tick(inck)
        self.ops[eng].append((wl, build, (inc.h if inc else None), inck))
        return n

    def v(self, build, waits=(), ):
        return self.emit("vector", build, waits=list(waits) + [self.vt], inc=self.vt)

    def s(self, build, waits=()):
        return self.emit("scalar", build, waits=list(waits) + [self.st], inc=self.st)

    def g(self, build, waits=()):
        return self.emit("gpsimd", build, waits=list(waits) + [self.gt], inc=self.gt)

    def t(self, build, waits=(), milestone=False):
        return self.emit("tensor", build, waits=waits,
                         inc=(self.tt if milestone else None))

    def dma(self, eng, out, in_, sem, waits=()):
        return self.emit(eng, lambda e, o=out, i=in_: e.dma_start(out=o, in_=i),
                         waits=waits, inc=sem, inck=16)

    def close(self):
        self.ctx.close()

    def finalize(self):
        nc = self.nc
        if self._cpool is not None:
            cs = self.sem("const_init")
            new = []
            for val, cell in self._consts.items():
                new.append(([], (lambda e, c=cell, v=val: e.memset(c, v)), None, 1))
            if new:
                new[-1] = (new[-1][0], new[-1][1], cs.h, 1)
            self.ops["gpsimd"] = new + self.ops["gpsimd"]
            for eng in ("scalar", "vector", "tensor"):
                if self.ops[eng]:
                    wl, build, inch, inck = self.ops[eng][0]
                    self.ops[eng][0] = (wl + [(cs.h, 1)], build, inch, inck)

        block = self.ctx.enter_context(nc.Block())
        for eng in self.ENGS:
            lst = self.ops[eng]
            if not lst:
                continue

            def runner(e, lst=lst):
                for wl, build, inch, inck in lst:
                    for sh, v in wl:
                        e.wait_ge(sh, v)
                    inst = build(e)
                    if inch is not None and inst is not None:
                        inst.then_inc(inch, inck)
            getattr(block, eng)(runner)
        return nc



def host_causal_masks(cc):
    s = np.arange(128)[:, None]
    q = np.arange(128)[None, :]
    out = np.zeros((4, 128, 128), np.float32)
    for p in range(4):
        out[p] = np.where(128 * (p - cc) + s > q, NEG, 0.0)
    return out


def host_rope_head64(S):
    inv = 1.0 / (10000.0 ** (np.arange(32, dtype=np.float64) * 2.0 / 64.0))
    t = np.arange(S, dtype=np.float64)
    ang = inv[:, None] * t[None, :]
    c = np.cos(ang).astype(np.float32)
    s = np.sin(ang).astype(np.float32)
    return (np.concatenate([c, c, c, c], 0), np.concatenate([s, s, s, s], 0))


def host_rope_d1024(S):
    inv = 1.0 / (10000.0 ** (np.arange(512, dtype=np.float64) / 512.0))
    t = np.arange(S, dtype=np.float64)
    ang = inv[:, None] * t[None, :]
    return (np.cos(ang).astype(np.float32), np.sin(ang).astype(np.float32))


def host_swap64():
    n = 128
    P = np.zeros((n, n), np.float32)
    for b0 in range(0, n, 64):
        P[b0:b0 + 32, b0 + 32:b0 + 64] = -np.eye(32)
        P[b0 + 32:b0 + 64, b0:b0 + 32] = np.eye(32)
    return P.T.copy()


def host_cosine_pos(S, dim=64):
    EPS = float(np.finfo(np.float32).eps)
    idx = np.arange(S, dtype=np.float32)
    sc = 1.0 / (10000.0 ** (np.arange(0, dim, 2, dtype=np.float32) / dim) + EPS)
    ang = idx[:, None] * sc[None, :]
    pos = np.stack([np.sin(ang), np.cos(ang)], axis=-1).reshape(S, dim)
    return pos.astype(np.float32)


def shard_tiles(x, geo: Geo, core):
    b, cc = divmod(core, geo.cpb)
    xb = x[b].reshape(geo.n_tiles, 128, -1)
    return np.ascontiguousarray(xb[cc::geo.cpb])


def unshard_tiles(parts, geo: Geo):
    d = parts[0].size // (geo.n_slots * 128)
    out = np.empty((geo.n_batch, geo.n_tiles, 128, d), parts[0].dtype)
    for core, p in enumerate(parts):
        b, cc = divmod(core, geo.cpb)
        out[b, cc::geo.cpb] = p.reshape(geo.n_slots, 128, d)
    return out.reshape(geo.n_batch, geo.S, d)



class Common:

    def __init__(self, k: K, d=1024):
        self.d = d
        self.ident = k.sb("ident", [128, 128], BF16)
        self.identf = k.sb("identf", [128, 128], F32)
        self.iod = k.sb("iod", [128, 128], F32)
        self.ones1 = k.sb("ones1", [1, 128], F32)
        self.ones1b = k.sb("ones1b", [1, 128], BF16)
        self.lsum = k.sb("ln_sum", [128, 1], F32)
        self.lssq = k.sb("ln_ssq", [128, 1], F32)
        self.lmean = k.sb("ln_mean", [128, 1], F32)
        self.lvar = k.sb("ln_var", [128, 1], F32)
        self.lrstd = k.sb("ln_rstd", [128, 1], F32)
        self.lnmr = k.sb("ln_nmr", [128, 1], F32)
        self.sqd = k.sb("ln_sqd", [128, d], F32)

    def emit_init(self, k: K):
        k.g(lambda e: e.iota(out=self.iod[:], pattern=[[-1, 128]], base=0,
                             channel_multiplier=1, allow_small_or_imprecise_dtypes=True))
        k.g(lambda e: e.memset(self.ones1[:], 1.0))
        k.g(lambda e: e.memset(self.ones1b[:], 1.0))
        k.v(lambda e: e.tensor_scalar(out=self.ident[:], in0=self.iod[:], scalar1=0.0,
                                      scalar2=None, op0=ALU.is_equal), waits=[k.gt])
        k.v(lambda e: e.tensor_scalar(out=self.identf[:], in0=self.iod[:], scalar1=0.0,
                                      scalar2=None, op0=ALU.is_equal), waits=[k.gt])


def emit_ln(k: K, cm: Common, x_ap, xn_ap, d, eps=1e-5, x_waits=(),
            gain_b=None, bias_b=None, out_f32=None):
    xw = list(x_waits)
    k.const(eps)
    k.v(lambda e: e.tensor_reduce(out=cm.lsum[:], in_=x_ap, axis=AX.X, op=ALU.add),
        waits=xw)
    k.s(lambda e: e.activation(out=cm.sqd[:, :d], in_=x_ap, func=AF.Square,
                               accum_out=cm.lssq[:]), waits=xw + [k.vt])
    k.v(lambda e: e.tensor_scalar_mul(cm.lmean[:], cm.lsum[:], 1.0 / d))
    k.v(lambda e: e.tensor_scalar_mul(cm.lvar[:], cm.lssq[:], 1.0 / d), waits=[k.st])
    k.v(lambda e: e.tensor_tensor(out=cm.lnmr[:], in0=cm.lmean[:], in1=cm.lmean[:],
                                  op=ALU.mult))
    k.v(lambda e: e.tensor_sub(cm.lvar[:], cm.lvar[:], cm.lnmr[:]))
    k.s(lambda e: e.activation(out=cm.lrstd[:], in_=cm.lvar[:], func=AF.Sqrt, bias=eps),
        waits=[k.vt])
    k.v(lambda e: e.reciprocal(cm.lrstd[:], cm.lrstd[:]), waits=[k.st])
    k.v(lambda e: e.scalar_tensor_tensor(out=cm.lnmr[:], in0=cm.lmean[:], scalar=-1.0,
                                         in1=cm.lrstd[:], op0=ALU.mult, op1=ALU.mult))
    if gain_b is None:
        tgt = xn_ap if out_f32 is None else out_f32
        k.v(lambda e: e.scalar_tensor_tensor(out=tgt, in0=x_ap, scalar=cm.lmean[:],
                                             in1=cm.lrstd[:].to_broadcast([128, d]), op0=ALU.subtract,
                                             op1=ALU.mult))
        if out_f32 is not None and xn_ap is not None:
            k.v(lambda e: e.tensor_copy(out=xn_ap, in_=out_f32))
        return "vt"
    else:
        k.v(lambda e: e.scalar_tensor_tensor(out=cm.sqd[:, :d], in0=x_ap,
                                             scalar=cm.lmean[:],
                                             in1=cm.lrstd[:].to_broadcast([128, d]),
                                             op0=ALU.subtract, op1=ALU.mult))
        k.v(lambda e: e.tensor_mul(cm.sqd[:, :d], cm.sqd[:, :d], gain_b[:, :]))
        k.v(lambda e: e.tensor_tensor(out=xn_ap, in0=cm.sqd[:, :d], in1=bias_b[:, :],
                                      op=ALU.add))
        if out_f32 is not None:
            k.v(lambda e: e.tensor_copy(out=out_f32, in_=xn_ap))
        return "vt"


def emit_bcast_row(k: K, cm: Common, row_ap, ps_ap, sb_ap, nrows, waits=()):
    k.t(lambda e: e.matmul(ps_ap, cm.ones1[0:1, 0:nrows], row_ap, start=True, stop=True),
        waits=waits, milestone=True)
    k.v(lambda e: e.tensor_copy(out=sb_ap, in_=ps_ap), waits=[k.tt])


def emit_transpose(k: K, cm: Common, src_bf16, dst_bf16, ps_bf16, waits=()):
    k.t(lambda e: e.transpose(out=ps_bf16, in_=src_bf16, identity=cm.ident[:]),
        waits=waits, milestone=True)
    k.v(lambda e: e.tensor_copy(out=dst_bf16, in_=ps_bf16), waits=[k.tt])



class AttnCfg:
    def __init__(self, nq=16, nkv=16, dh=64, causal=True, scale=0.125,
                 mode="none", inner_ln=False, fused_qkv=False):
        self.nq = nq
        self.nkv = nkv
        self.dh = dh
        self.causal = causal
        self.scale = scale
        self.mode = mode
        self.inner_ln = inner_ln
        self.fused_qkv = fused_qkv
        self.dq = nq * dh
        self.dkv = nkv * dh


def build_attn_prep(geo: Geo, cfg: AttnCfg, name="prep"):
    k = K(name)
    d = geo.d
    nch = geo.dchunks
    Tl = geo.T_local

    x_in = k.inp("x", [geo.n_slots, 128, d], F32)
    if cfg.fused_qkv:
        w_qkv = k.inp("w_qkv", [d, 3 * d], F32)
    else:
        w_q = k.inp("w_q", [d, cfg.dq], F32)
        w_k = k.inp("w_k", [d, cfg.dkv], F32)
        w_v = k.inp("w_v", [d, cfg.dkv], F32)
    if cfg.inner_ln:
        ln_g = k.inp("ln_g", [d], F32)
        ln_b = k.inp("ln_b", [d], F32)
    if cfg.mode == "rope64":
        ropeC = k.inp("ropeC", [128, Tl], F32)
        ropeS = k.inp("ropeS", [128, Tl], F32)
        swap_in = k.inp("swapT", [128, 128], F32)
    if cfg.mode == "pos":
        posT = k.inp("posT", [128, Tl], F32)

    qt_out = k.outp("qt", [cfg.dq, Tl], BF16)
    kt_out = k.outp("kt", [cfg.nkv * 64, Tl], BF16)
    v_out = k.outp("v", [cfg.nkv, 128, geo.n_slots * 65], BF16)

    cm = Common(k, d)
    cm.emit_init(k)

    wsem = k.sem("wld")
    wq_sb = k.sb("wq_sb", [128, nch * cfg.dq], BF16)
    wk_sb = k.sb("wk_sb", [128, nch * cfg.dkv], BF16)
    wv_sb = k.sb("wv_sb", [128, nch * cfg.dkv], BF16)
    for c in range(nch):
        rs = slice(128 * c, 128 * (c + 1))
        if cfg.fused_qkv:
            k.dma("gpsimd", wq_sb[:, c * cfg.dq:(c + 1) * cfg.dq], w_qkv[rs, 0:d], wsem)
            k.dma("gpsimd", wk_sb[:, c * cfg.dkv:(c + 1) * cfg.dkv], w_qkv[rs, d:2 * d], wsem)
            k.dma("gpsimd", wv_sb[:, c * cfg.dkv:(c + 1) * cfg.dkv], w_qkv[rs, 2 * d:3 * d], wsem)
        else:
            k.dma("gpsimd", wq_sb[:, c * cfg.dq:(c + 1) * cfg.dq], w_q[rs, :], wsem)
            k.dma("gpsimd", wk_sb[:, c * cfg.dkv:(c + 1) * cfg.dkv], w_k[rs, :], wsem)
            k.dma("gpsimd", wv_sb[:, c * cfg.dkv:(c + 1) * cfg.dkv], w_v[rs, :], wsem)
    wld_total = wsem.n

    gain_b = bias_b = None
    if cfg.inner_ln:
        gsem = k.sem("gld")
        g_row = k.sb("g_row", [1, d], F32)
        b_row = k.sb("b_row", [1, d], F32)
        k.dma("sync", g_row[:], ln_g[None, :], gsem)
        k.dma("sync", b_row[:], ln_b[None, :], gsem)
        gld_total = gsem.n
        gain_b = k.sb("gain_b", [128, d], BF16)
        bias_b = k.sb("bias_b", [128, d], BF16)
        psb = k.psum("ps_bc", [128, 512])
        for half in range((d + 511) // 512):
            hw = min(512, d - half * 512)
            sl = slice(half * 512, half * 512 + hw)
            k.t(lambda e, sl=sl, hw=hw: e.matmul(psb[:, 0:hw], cm.ones1[:], g_row[0:1, sl],
                                                 start=True, stop=True),
                waits=[(gsem, gld_total), k.vt], milestone=True)
            k.v(lambda e, sl=sl, hw=hw: e.tensor_copy(out=gain_b[:, sl], in_=psb[:, 0:hw]),
                waits=[k.tt])
            k.t(lambda e, sl=sl, hw=hw: e.matmul(psb[:, 0:hw], cm.ones1[:], b_row[0:1, sl],
                                                 start=True, stop=True),
                waits=[k.vt], milestone=True)
            k.v(lambda e, sl=sl, hw=hw: e.tensor_copy(out=bias_b[:, sl], in_=psb[:, 0:hw]),
                waits=[k.tt])

    rsem = k.sem("rld")
    rope_total = 0
    if cfg.mode == "rope64":
        ropeC_sb = k.sb("ropeC_sb", [128, Tl], F32)
        ropeS_sb = k.sb("ropeS_sb", [128, Tl], F32)
        swap_sb = k.sb("swap_sb", [128, 128], BF16)
        k.dma("sync", ropeC_sb[:], ropeC[:, :], rsem)
        k.dma("sync", ropeS_sb[:], ropeS[:, :], rsem)
        k.dma("gpsimd", swap_sb[:], swap_in[:, :], wsem)
        rope_total = rsem.n
    if cfg.mode == "pos":
        posT_sb = k.sb("posT_sb", [128, Tl], F32)
        k.dma("sync", posT_sb[:], posT[:, :], rsem)
        rope_total = rsem.n

    xs = [k.sem("xs0"), k.sem("xs1")]
    x_sb = [k.sb(f"x_sb{i}", [128, d], F32) for i in range(2)]
    xin_sb = k.sb("xin_sb", [128, d], F32) if cfg.inner_ln else None
    xn_sb = k.sb("xn_sb", [128, d], BF16)
    xnT = k.sb("xnT", [128, nch * 512], BF16)
    ps_tr = k.psum("ps_tr", [128, 128], BF16)
    ps_p = [k.psum(f"ps_p{i}", [128, 512]) for i in range(2)]
    ps_r = k.psum("ps_r", [128, 512])
    ps_v = k.psum("ps_v", [128, 512])
    stg = [k.sb(f"stg{i}", [128, 512], BF16) for i in range(2)]
    qs = [k.sem("qs0"), k.sem("qs1")]
    t1 = k.sb("rope_t1", [128, 512], BF16)
    t2 = k.sb("rope_t2", [128, 512], BF16)
    v_sb = [k.sb(f"v_sb{i}", [128, cfg.nkv * 65], BF16) for i in range(2)]
    vq = [k.sem("vq0"), k.sem("vq1")]
    stg_i = [0]

    k.dma("sync", x_sb[0][:], x_in[0, :, :], xs[0])

    for m in range(geo.groups):
        for kk in range(4):
            ti = 4 * m + kk
            buf = ti % 2
            if ti + 1 < geo.n_slots:
                k.dma("sync", x_sb[(ti + 1) % 2][:], x_in[ti + 1, :, :], xs[(ti + 1) % 2],
                      waits=[k.vt, k.st])
            ld_waits = [(xs[buf], 16 * (ti // 2 + 1))]
            if cfg.inner_ln:
                emit_ln(k, cm, x_sb[buf][:], None, d, x_waits=ld_waits,
                        out_f32=xin_sb[:])
                ln_done = emit_ln(k, cm, xin_sb[:], xn_sb[:], d, x_waits=[k.vt],
                                  gain_b=gain_b, bias_b=bias_b)
            else:
                ln_done = emit_ln(k, cm, x_sb[buf][:], xn_sb[:], d, x_waits=ld_waits)
            ln_sem = k.vt if ln_done == "vt" else k.st
            for c in range(nch):
                k.t(lambda e, c=c: e.transpose(
                    out=ps_tr[:], in_=xn_sb[:, 128 * c:128 * (c + 1)],
                    identity=cm.ident[:]),
                    waits=[(ln_sem, ln_sem.n), k.vt], milestone=True)
                k.v(lambda e, c=c, kk=kk: e.tensor_copy(
                    out=xnT[:, c * 512 + kk * 128: c * 512 + kk * 128 + 128],
                    in_=ps_tr[:]), waits=[k.tt])
            vbuf = ti % 2
            for vc in range((cfg.dkv + 511) // 512):
                fmv = min(512, cfg.dkv - vc * 512)
                for c in range(nch):
                    k.t(lambda e, c=c, kk=kk, vc=vc, fmv=fmv: e.matmul(
                        ps_v[:, 0:fmv],
                        xnT[:, c * 512 + kk * 128: c * 512 + kk * 128 + 128],
                        wv_sb[:, c * cfg.dkv + vc * 512: c * cfg.dkv + vc * 512 + fmv],
                        start=(c == 0), stop=(c == nch - 1)),
                        waits=([wsem, k.vt,
                                (vq[vbuf], vq[vbuf].n)] if c == 0 else []),
                        milestone=(c == nch - 1))
                nh_v = fmv // 64
                h0 = vc * 8
                k.v(lambda e, vc=vc, nh_v=nh_v, h0=h0, vbuf=vbuf: e.tensor_copy(
                    out=v_sb[vbuf][:, 65 * h0:65 * (h0 + nh_v)].rearrange(
                        "p (h c) -> p h c", c=65)[:, :, 0:64],
                    in_=ps_v[:, 0:nh_v * 64].rearrange("p (h c) -> p h c", c=64)),
                    waits=[k.tt])
            k.v(lambda e, vbuf=vbuf: e.memset(
                v_sb[vbuf][:].rearrange("p (h c) -> p h c", c=65)[:, :, 64:65], 1.0))
            k.dma("sync", v_out[:, :, ti * 65:(ti + 1) * 65].rearrange("h p c -> p h c"),
                  v_sb[vbuf][:].rearrange("p (h c) -> p h c", c=65), vq[vbuf],
                  waits=[k.vt])

        def proj(dst_out, w_sb, wW, oc, kind, m=m):
            pb = ps_p[oc % 2]
            for c in range(nch):
                k.t(lambda e, c=c, oc=oc, pb=pb, w_sb=w_sb, wW=wW: e.matmul(
                    pb[:], w_sb[:, c * wW + 128 * oc: c * wW + 128 * (oc + 1)],
                    xnT[:, c * 512:(c + 1) * 512],
                    start=(c == 0), stop=(c == nch - 1)),
                    waits=([k.vt, wsem] if c == 0 else []),
                    milestone=(c == nch - 1))
            tt_mm = k.tt.n
            si = stg_i[0] % 2
            stg_i[0] += 1
            sg = stg[si]
            sq = qs[si]
            co = slice(m * 512, (m + 1) * 512)
            if cfg.mode == "rope64" and kind in ("q", "k"):
                k.v(lambda e, pb=pb, co=co: e.tensor_tensor(
                    out=t1[:], in0=pb[:], in1=ropeS_sb[:, co], op=ALU.mult),
                    waits=[(k.tt, tt_mm), (rsem, rope_total)])
                k.t(lambda e: e.matmul(ps_r[:], swap_sb[:], t1[:], start=True, stop=True),
                    waits=[k.vt, (wsem, wsem.n)], milestone=True)
                k.v(lambda e, pb=pb, co=co: e.tensor_tensor(
                    out=t2[:], in0=pb[:], in1=ropeC_sb[:, co], op=ALU.mult))
                k.v(lambda e, sg=sg: e.tensor_tensor(out=sg[:], in0=t2[:], in1=ps_r[:],
                                                     op=ALU.add),
                    waits=[k.tt, (sq, sq.n)])
            elif cfg.mode == "pos" and kind in ("q", "k"):
                k.v(lambda e, pb=pb, co=co, sg=sg: e.tensor_tensor(
                    out=sg[:], in0=pb[:], in1=posT_sb[:, co], op=ALU.add),
                    waits=[(k.tt, tt_mm), (rsem, rope_total), (sq, sq.n)])
            else:
                k.v(lambda e, pb=pb, sg=sg: e.tensor_copy(out=sg[:], in_=pb[:]),
                    waits=[(k.tt, tt_mm), (sq, sq.n)])
            k.dma("sync", dst_out[128 * oc:128 * (oc + 1), m * 512:(m + 1) * 512],
                  sg[:], sq, waits=[k.vt])

        for oc in range(cfg.dq // 128):
            proj(qt_out, wq_sb, cfg.dq, oc, "q")
        for oc in range(cfg.dkv // 128):
            proj(kt_out, wk_sb, cfg.dkv, oc, "k")

    k.emit("sync", lambda e: e.nop(),
           waits=[(qs[0], qs[0].n), (qs[1], qs[1].n), (vq[0], vq[0].n), (vq[1], vq[1].n)])
    return k


def build_attn_main(geo: Geo, cfg: AttnCfg, name="attnB", residual_ln=False,
                    gate=False):
    k = K(name)
    d = geo.d
    nch = geo.dchunks
    Tl = geo.T_local
    S = geo.S
    njall = S // 128
    cpb = geo.cpb
    hpkv = cfg.nq // cfg.nkv

    x_in = k.inp("x", [geo.n_slots, 128, d], F32)
    qt_in = k.inp("qt", [cfg.dq, Tl], BF16)
    kt_in = k.inp("kt", [cfg.nkv * 64, S], BF16)
    v_in = k.inp("v", [cfg.nkv, 128, geo.n_tiles * 65], BF16)
    wo_in = k.inp("wo", [d, d], F32)
    mk_in = k.inp("masks", [cpb, 128, 128], F32)
    y_out = k.outp("y", [geo.n_slots, 128, d], F32)
    if gate:
        g2_in = k.inp("ln2_g", [d], F32)
        b2_in = k.inp("ln2_b", [d], F32)
        ag_in = k.inp("agent_gate", [d, 10], F32)
        ag_out = k.outp("aglast", [1, 10], F32)

    cm = Common(k, d)
    cm.emit_init(k)
    k.const(cfg.scale)

    wsem = k.sem("wld")
    wo_sb = k.sb("wo_sb", [128, nch * d], BF16)
    for c in range(nch):
        k.dma("gpsimd", wo_sb[:, c * d:(c + 1) * d], wo_in[128 * c:128 * (c + 1), :], wsem)
    wld_total = wsem.n

    qsem = k.sem("qld")
    qt_sb = k.sb("qt_sb", [64, cfg.nq * Tl], BF16)
    for h in range(cfg.nq):
        k.dma("sync", qt_sb[:, h * Tl:(h + 1) * Tl], qt_in[64 * h:64 * (h + 1), :], qsem)
    msem = k.sem("mld")
    mask_sb = k.sb("mask_sb", [128, cpb * 128], F32)
    if cfg.causal:
        for p in range(cpb):
            k.dma("sync", mask_sb[:, 128 * p:128 * (p + 1)], mk_in[p, :, :], msem)
    mld_total = msem.n
    qld_total = qsem.n

    kvsem = [k.sem("kv0"), k.sem("kv1")]
    kt_sb = [k.sb(f"kt_sb{i}", [64, S], BF16) for i in range(2)]
    v_sb = [k.sb(f"v_sb{i}", [128, geo.n_tiles * 65], BF16) for i in range(2)]

    def load_kv(kv, waits=()):
        b = kv % 2
        k.dma("gpsimd", kt_sb[b][:], kt_in[64 * kv:64 * (kv + 1), :], kvsem[b],
              waits=waits)
        k.dma("gpsimd", v_sb[b][:], v_in[kv, :, :], kvsem[b], waits=waits)

    load_kv(0)
    tt_end_prev = 0

    ps_s = [k.psum(f"ps_s{i}", [128, 512]) for i in range(2)]
    ps_o = k.psum("ps_o", [128, 512])
    ps_b = k.psum("ps_b", [128, 512])
    exp_sb = [k.sb(f"exp_sb{i}", [128, 512], BF16) for i in range(3)]
    rcp_sb = k.sb("rcp_sb", [1, 512], F32)
    bb_sb = k.sb("bb_sb", [64, 512], F32)
    outT = k.sb("outT", [128, max(nch * Tl, 4 * d + 256 + 2 * nch * 128 + 128)], BF16)

    for kv in range(cfg.nkv):
        b = kv % 2
        if kv + 1 < cfg.nkv:
            load_kv(kv + 1, waits=[(k.tt, tt_end_prev)])
        kv_w = (kvsem[b], 32 * (kv // 2 + 1))
        for hq in range(hpkv):
            h = kv * hpkv + hq
            qrow = 64 * (h % 2)
            qcol0 = h * Tl
            for m in range(geo.groups):
                jmax = geo.chunks_of_group(m) if cfg.causal else njall
                qk_t = {}
                exp_t = {}
                av_t = {}
                geom = {}

                def emit_qk(j, b=b, qrow=qrow, qcol0=qcol0, m=m):
                    r = j - 4 * cpb * m if cfg.causal else -1
                    if cfg.causal and r >= 0:
                        kmin = r // cpb
                        pat = r % cpb
                    else:
                        kmin, pat = 0, -1
                    coff = 128 * kmin
                    fm = 512 - coff
                    jb = j % 2
                    eb = j % 3
                    geom[j] = (coff, fm, eb)
                    w = [kv_w, (qsem, qld_total)]
                    if j - 2 >= 0:
                        w.append((k.st, exp_t[j - 2]))
                    else:
                        w.append((k.vt, k.vt.n))
                    qk_t[j] = k.emit(
                        "tensor",
                        lambda e, b=b, j=j, jb=jb, coff=coff, fm=fm, qrow=qrow,
                        qcol0=qcol0, m=m: e.matmul(
                            ps_s[jb][:, coff:coff + fm],
                            kt_sb[b][:, 128 * j:128 * (j + 1)],
                            qt_sb[:, qcol0 + m * 512 + coff: qcol0 + m * 512 + coff + fm],
                            start=True, stop=True),
                        waits=w, inc=k.tt)
                    mvt = None
                    if cfg.causal and r >= 0 and kmin < 4:
                        mvt = k.v(lambda e, jb=jb, kmin=kmin, pat=pat: e.tensor_tensor(
                            out=ps_s[jb][:, 128 * kmin:128 * (kmin + 1)],
                            in0=ps_s[jb][:, 128 * kmin:128 * (kmin + 1)],
                            in1=mask_sb[:, 128 * pat:128 * (pat + 1)], op=ALU.add),
                            waits=[(k.tt, qk_t[j]), (msem, mld_total)])
                    w = [(k.tt, qk_t[j])]
                    if mvt is not None:
                        w.append((k.vt, mvt))
                    if j - 3 in av_t:
                        w.append((k.tt, av_t[j - 3]))
                    exp_t[j] = k.s(lambda e, jb=jb, eb=eb, coff=coff, fm=fm: e.activation(
                        out=exp_sb[eb][:, coff:coff + fm], in_=ps_s[jb][:, coff:coff + fm],
                        func=AF.Exp, scale=cfg.scale), waits=w)

                def emit_av(j, b=b, jmax=jmax):
                    coff, fm, eb = geom[j]
                    w = [(k.st, exp_t[j])]
                    if j == 0:
                        w.append((k.vt, k.vt.n))
                    av_t[j] = k.emit(
                        "tensor",
                        lambda e, b=b, j=j, eb=eb, coff=coff, fm=fm, last=(j == jmax - 1):
                        e.matmul(ps_o[0:65, coff:coff + fm],
                                 v_sb[b][:, 65 * j:65 * (j + 1)],
                                 exp_sb[eb][:, coff:coff + fm],
                                 start=(j == 0), stop=last),
                        waits=w, inc=k.tt)

                LAG = 2
                for j in range(jmax + LAG):
                    if j < jmax:
                        emit_qk(j)
                    if j - LAG >= 0:
                        emit_av(j - LAG)
                k.v(lambda e: e.reciprocal(rcp_sb[:], ps_o[64:65, 0:512]),
                    waits=[(k.tt, av_t[jmax - 1])])
                k.t(lambda e: e.matmul(ps_b[0:64, :], cm.ones1[0:1, 0:64], rcp_sb[:],
                                       start=True, stop=True),
                    waits=[k.vt], milestone=True)
                k.v(lambda e: e.tensor_copy(out=bb_sb[:], in_=ps_b[0:64, :]),
                    waits=[k.tt])
                oc0 = (h // 2) * Tl
                k.v(lambda e, qrow=qrow, oc0=oc0, m=m: e.tensor_tensor(
                    out=outT[qrow:qrow + 64, oc0 + m * 512: oc0 + (m + 1) * 512],
                    in0=ps_o[0:64, 0:512], in1=bb_sb[:], op=ALU.mult))
        tt_end_prev = k.tt.n

    if residual_ln:
        xin_sb = k.sb("xin_sb", [128, d], F32)
    xs = [k.sem("xs0"), k.sem("xs1")]
    x_sb = [k.sb(f"x_sb{i}", [128, d], F32) for i in range(2)]
    yq = [k.sem("yq0"), k.sem("yq1")]
    y_sb = [k.sb(f"y_sb{i}", [128, d], F32) for i in range(2)]
    ps_y = [k.psum(f"ps_y{i}", [128, 512]) for i in range(2)]

    k.dma("sync", x_sb[0][:], x_in[0, :, :], xs[0])
    attn_done = k.vt.n
    for ti in range(geo.n_slots):
        m, kk = divmod(ti, 4)
        buf = ti % 2
        if ti + 1 < geo.n_slots:
            k.dma("sync", x_sb[(ti + 1) % 2][:], x_in[ti + 1, :, :], xs[(ti + 1) % 2],
                  waits=[k.vt, k.st])
        if residual_ln:
            rl_sem = k.vt if emit_ln(k, cm, x_sb[buf][:], None, d,
                                     x_waits=[(xs[buf], 16 * (ti // 2 + 1))],
                                     out_f32=xin_sb[:]) == "vt" else k.st
        nhalf = (d + 511) // 512
        for half in range(nhalf):
            hw = min(512, d - half * 512)
            pb = ps_y[half % 2]
            for c in range(nch):
                k.t(lambda e, c=c, m=m, kk=kk, half=half, pb=pb, hw=hw: e.matmul(
                    pb[:, 0:hw], outT[:, c * Tl + m * 512 + kk * 128: c * Tl + m * 512 + kk * 128 + 128],
                    wo_sb[:, c * d + half * 512: c * d + half * 512 + hw],
                    start=(c == 0), stop=(c == nch - 1)),
                    waits=([(k.vt, attn_done), (wsem, wld_total),
                            (k.vt, k.vt.n)] if c == 0 else []),
                    milestone=(c == nch - 1))
            tmm = k.tt.n
            res_src = xin_sb if residual_ln else x_sb[buf]
            k.v(lambda e, half=half, buf=buf, pb=pb, hw=hw, res_src=res_src:
                e.tensor_tensor(
                out=y_sb[buf][:, half * 512: half * 512 + hw],
                in0=res_src[:, half * 512: half * 512 + hw],
                in1=pb[:, 0:hw], op=ALU.add),
                waits=[(k.tt, tmm), (xs[buf], 16 * (ti // 2 + 1)),
                       (yq[buf], yq[buf].n)] + ([(rl_sem, rl_sem.n)] if residual_ln else []))
        k.dma("sync", y_out[ti, :, :], y_sb[buf][:], yq[buf], waits=[k.vt])

    if gate:
        gsem = k.sem("ggld")
        g_row = k.sb("gl_g", [1, d], F32)
        b_row = k.sb("gl_b", [1, d], F32)
        k.dma("sync", g_row[:], g2_in[None, :], gsem)
        k.dma("sync", b_row[:], b2_in[None, :], gsem)
        gld = gsem.n
        ag_all = k.sb("ag_all", [128, 2 * nch * 10], BF16)
        ag2sem = k.sem("ag2ld")
        agf2sem = k.sem("agf2ld")
        agf2 = k.sb("agf2", [128, 10], F32)
        for c in range(nch):
            k.dma("gpsimd", ag_all[:, c * 10:(c + 1) * 10], ag_in[128 * c:128 * (c + 1), :], ag2sem)
        ag2 = ag2sem.n
        for c in range(nch):
            k.dma("sync", agf2[:], ag_in[128 * c:128 * (c + 1), :], agf2sem, waits=[k.vt])
            k.v(lambda e, c=c: e.tensor_tensor(
                out=ag_all[:, (nch + c) * 10:(nch + c + 1) * 10], in0=agf2[:],
                in1=ag_all[:, c * 10:(c + 1) * 10], op=ALU.subtract),
                waits=[(agf2sem, agf2sem.n), (ag2sem, ag2)])
        ag2v = k.vt.n
        gain_g = outT[:, 0:d]
        bias_g = outT[:, d:2 * d]
        ps_g = ps_b
        for half in range((d + 511) // 512):
            hw = min(512, d - half * 512)
            sl = slice(half * 512, half * 512 + hw)
            k.t(lambda e, sl=sl, hw=hw: e.matmul(ps_g[:, 0:hw], cm.ones1[:],
                                                 g_row[0:1, sl], start=True, stop=True),
                waits=[(gsem, gld), k.vt], milestone=True)
            k.v(lambda e, sl=sl, hw=hw: e.tensor_copy(out=gain_g[:, sl.start:sl.stop], in_=ps_g[:, 0:hw]),
                waits=[k.tt])
            k.t(lambda e, sl=sl, hw=hw: e.matmul(ps_g[:, 0:hw], cm.ones1[:],
                                                 b_row[0:1, sl], start=True, stop=True),
                waits=[k.vt], milestone=True)
            k.v(lambda e, sl=sl, hw=hw: e.tensor_copy(out=bias_g[:, sl.start:sl.stop], in_=ps_g[:, 0:hw]),
                waits=[k.tt])
        lb = (geo.n_slots - 1) % 2
        xn2g = outT[:, 2 * d:3 * d]
        xn2glo = outT[:, 3 * d + 256:4 * d + 256]
        emit_ln(k, cm, y_sb[lb][:], xn2g, d, x_waits=[k.vt],
                gain_b=None, bias_b=None)
        k.v(lambda e: e.tensor_mul(cm.sqd[:, :d], xn2g, gain_g))
        k.v(lambda e: e.tensor_tensor(out=cm.sqd[:, :d], in0=cm.sqd[:, :d],
                                      in1=bias_g, op=ALU.add))
        k.v(lambda e: e.tensor_copy(out=xn2g, in_=cm.sqd[:, :d]))
        k.v(lambda e: e.tensor_tensor(out=xn2glo, in0=cm.sqd[:, :d], in1=xn2g,
                                      op=ALU.subtract))
        ps_ag = k.psum("ps_ag", [128, 10])
        ps_tr_g = k.psum("ps_tr_g", [128, 128], BF16)
        gstg = outT[:, 3 * d:3 * d + 128]
        gstg2 = outT[:, 3 * d + 128:3 * d + 256]
        hiT0 = 4 * d + 256
        loT0 = 4 * d + 256 + nch * 128
        for c in range(nch):
            k.t(lambda e, c=c: e.transpose(out=ps_tr_g[:],
                                           in_=outT[:, 2 * d + 128 * c: 2 * d + 128 * (c + 1)],
                                           identity=cm.ident[:]), waits=[k.vt],
                milestone=True)
            k.v(lambda e, c=c: e.tensor_copy(out=outT[:, hiT0 + 128 * c: hiT0 + 128 * (c + 1)],
                                             in_=ps_tr_g[:]), waits=[k.tt])
            k.t(lambda e, c=c: e.transpose(out=ps_tr_g[:],
                                           in_=outT[:, 3 * d + 256 + 128 * c: 3 * d + 256 + 128 * (c + 1)],
                                           identity=cm.ident[:]), waits=[k.vt],
                milestone=True)
            k.v(lambda e, c=c: e.tensor_copy(out=outT[:, loT0 + 128 * c: loT0 + 128 * (c + 1)],
                                             in_=ps_tr_g[:]), waits=[k.tt])
        nmm_g = 0
        for c in range(nch):
            for (hi_side, tblo) in ((True, 0), (True, 1), (False, 0)):
                nmm_g += 1
                k.t(lambda e, c=c, hi_side=hi_side, tblo=tblo,
                    st_=(nmm_g == 1), sp_=(nmm_g == 3 * nch): e.matmul(
                    ps_ag[:], outT[:, (hiT0 if hi_side else loT0) + 128 * c:
                                   (hiT0 if hi_side else loT0) + 128 * (c + 1)],
                    ag_all[:, (tblo * nch + c) * 10:(tblo * nch + c + 1) * 10],
                    start=st_, stop=sp_),
                    waits=[k.vt, (k.vt, ag2v)], milestone=(nmm_g == 3 * nch))
        agfull = k.sb("agfull", [128, 10], F32)
        k.v(lambda e: e.tensor_copy(out=agfull[:], in_=ps_ag[:]), waits=[k.tt])
        gq2 = k.sem("gq2")
        k.dma("sync", ag_out[:, :], agfull[127:128, :], gq2, waits=[k.vt])
        k.emit("sync", lambda e: e.nop(), waits=[(gq2, gq2.n)])

    k.emit("sync", lambda e: e.nop(), waits=[(yq[0], yq[0].n), (yq[1], yq[1].n)])
    return k



def build_l1ffn(geo: Geo, name="l1ffn", n_exp=8, F=4096):
    k = K(name)
    d = geo.d
    nch = geo.dchunks
    Tl = geo.T_local
    nslots = geo.n_slots
    nfc = F // 128

    x_in = k.inp("x", [nslots, 128, d], F32)
    cen_in = k.inp("cen", [n_exp, d], F32)
    w1t_in = k.inp("w1t", [n_exp, d, F], BF16)
    w2t_in = k.inp("w2t", [n_exp, F, d], BF16)
    b1p_in = k.inp("b1p", [n_exp, 128, nfc], F32)
    b2_in = k.inp("b2", [n_exp, d], F32)
    y_out = k.outp("y", [nslots, 128, d], F32)

    cm = Common(k, d)
    cm.emit_init(k)
    csem = k.sem("cld")
    cen_f = k.sb("cen_f", [n_exp, d], F32)
    cen_hi = k.sb("cen_hi", [n_exp, d], BF16)
    cen_lo = k.sb("cen_lo", [n_exp, d], BF16)
    cen_sb = k.sb("cen_sb", [128, 2 * nch * n_exp], BF16)
    k.dma("sync", cen_f[:], cen_in[:, :], csem)
    cld0 = csem.n
    ps_tr2 = k.psum("ps_tr", [128, 128], BF16)
    k.v(lambda e: e.tensor_copy(out=cen_hi[:], in_=cen_f[:]), waits=[(csem, cld0)])
    k.v(lambda e: e.tensor_tensor(out=cen_lo[:], in0=cen_f[:], in1=cen_hi[:],
                                  op=ALU.subtract))
    for part in range(2):
        src = cen_hi if part == 0 else cen_lo
        for c in range(nch):
            k.t(lambda e, c=c, src=src: e.transpose(
                out=ps_tr2[:, 0:n_exp], in_=src[:, 128 * c:128 * (c + 1)],
                identity=cm.ident[0:n_exp, 0:n_exp]),
                waits=[k.vt], milestone=True)
            k.v(lambda e, c=c, part=part: e.tensor_copy(
                out=cen_sb[:, (part * nch + c) * n_exp:(part * nch + c + 1) * n_exp],
                in_=ps_tr2[:, 0:n_exp]), waits=[k.tt])
    cen_ready = k.vt.n
    b2sem = k.sem("b2ld")
    b2_sb = k.sb("b2_sb", [1, d], BF16)
    k.dma("gpsimd", b2_sb[:], b2_in[0:1, :], b2sem)

    feats_hi = k.sb("feats_hi", [128, d], BF16)
    nrmT = k.sb("nrmT", [128, nch * Tl], BF16)
    acc = k.sb("acc", [128, nslots * d], BF16)
    masks = k.sb("masks", [128, nslots * n_exp], F32)
    alpha = k.sb("alpha", [128, nslots], F32)
    mrow = k.sb("mrow", [128, 1], F32)
    meq = k.sb("meq", [128, n_exp], F32)
    mcum = k.sb("mcum", [128, n_exp], F32)
    nrm_sb = k.sb("nrm_sb", [128, d], BF16)
    ps_tr = ps_tr2
    ps_sc = k.psum("ps_sc", [128, n_exp])
    ftr_sb = k.sb("ftr_sb", [128, 128], BF16)
    flo_sb = k.sb("flo_sb", [128, 128], BF16)
    feats_f = k.sb("feats_f", [128, d], F32)
    feats_lo = k.sb("feats_lo", [128, d], BF16)
    zeros8 = k.sb("zeros8", [128, 8], F32)

    xs = [k.sem("xs0"), k.sem("xs1")]
    x_sb = [k.sb(f"x_sb{i}", [128, d], F32) for i in range(2)]

    k.v(lambda e: e.memset(zeros8[:], 0.0))
    k.dma("sync", x_sb[0][:], x_in[0, :, :], xs[0])
    for ti in range(nslots):
        buf = ti % 2
        if ti + 1 < nslots:
            k.dma("sync", x_sb[(ti + 1) % 2][:], x_in[ti + 1, :, :], xs[(ti + 1) % 2],
                  waits=[k.vt, k.st])
        ldw = [(xs[buf], 16 * (ti // 2 + 1))]
        emit_ln(k, cm, x_sb[buf][:], None, d, x_waits=ldw, out_f32=feats_f[:])
        k.v(lambda e: e.tensor_copy(out=feats_hi[:], in_=feats_f[:]),
            waits=[(k.tt, k.tt.n), (k.st, k.st.n)])
        k.v(lambda e: e.tensor_tensor(out=feats_lo[:], in0=feats_f[:],
                                      in1=feats_hi[:], op=ALU.subtract))
        fvt = k.vt.n
        emit_ln(k, cm, feats_hi[:], nrm_sb[:], d, x_waits=[k.vt])
        m, kk = divmod(ti, 4)
        for c in range(nch):
            k.t(lambda e, c=c: e.transpose(out=ps_tr[:], in_=nrm_sb[:, 128 * c:128 * (c + 1)],
                                           identity=cm.ident[:]),
                waits=[k.vt], milestone=True)
            k.v(lambda e, c=c, ti=ti: e.tensor_copy(
                out=nrmT[:, c * Tl + ti * 128: c * Tl + ti * 128 + 128],
                in_=ps_tr[:]), waits=[k.tt])
        nmm = 3 * nch
        imm = 0
        for c in range(nch):
            k.t(lambda e, c=c: e.transpose(
                out=ps_tr[:], in_=feats_hi[:, 128 * c: 128 * (c + 1)],
                identity=cm.ident[:]), waits=[(k.vt, fvt), k.vt], milestone=True)
            k.v(lambda e: e.tensor_copy(out=ftr_sb[:], in_=ps_tr[:]),
                waits=[k.tt])
            k.t(lambda e, c=c: e.transpose(
                out=ps_tr[:], in_=feats_lo[:, 128 * c: 128 * (c + 1)],
                identity=cm.ident[:]), waits=[k.vt], milestone=True)
            k.v(lambda e: e.tensor_copy(out=flo_sb[:], in_=ps_tr[:]),
                waits=[k.tt])
            for (aa, bb) in ((ftr_sb, 0), (ftr_sb, 1), (flo_sb, 0)):
                imm += 1
                k.t(lambda e, c=c, aa=aa, bb=bb: e.matmul(
                    ps_sc[:], aa[:],
                    cen_sb[:, (bb * nch + c) * n_exp:(bb * nch + c + 1) * n_exp],
                    start=(c == 0 and aa is ftr_sb and bb == 0),
                    stop=(c == nch - 1 and aa is flo_sb)),
                    waits=[k.vt, (k.vt, cen_ready)],
                    milestone=(c == nch - 1 and aa is flo_sb))
        k.v(lambda e: e.tensor_reduce(out=mrow[:], in_=ps_sc[:], axis=AX.X, op=ALU.max),
            waits=[k.tt])
        k.s(lambda e, ti=ti: e.activation(out=alpha[:, ti:ti + 1], in_=mrow[:],
                                          func=AF.Sigmoid), waits=[k.vt])
        k.v(lambda e: e.tensor_scalar(out=meq[:], in0=ps_sc[:], scalar1=mrow[:],
                                      scalar2=None, op0=ALU.is_ge))
        k.v(lambda e: e.tensor_tensor_scan(out=mcum[:], data0=meq[:], data1=zeros8[:],
                                           initial=0.0, op0=ALU.add, op1=ALU.add))
        k.v(lambda e: e.tensor_scalar(out=mcum[:], in0=mcum[:], scalar1=1.0,
                                      scalar2=None, op0=ALU.is_le))
        k.v(lambda e, ti=ti: e.tensor_tensor(out=masks[:, ti * n_exp:(ti + 1) * n_exp],
                                             in0=meq[:], in1=mcum[:], op=ALU.mult))
        k.v(lambda e, ti=ti: e.memset(acc[:, ti * d:(ti + 1) * d], 0.0))

    state_done = k.vt.n

    w1sem = k.sem("w1s")
    w1r = k.sb("w1r", [128, nch * F], BF16)
    NW2 = 4
    w2sem = [k.sem(f"w2s{i}") for i in range(NW2)]
    w2_sb = [k.sb(f"w2_sb{i}", [128, 512], BF16) for i in range(NW2)]
    b1sem = k.sem("b1ld")
    b1_sb = k.sb("b1_sb", [128, n_exp * nfc], F32)
    for e in range(n_exp):
        k.dma("sync", b1_sb[:, e * nfc:(e + 1) * nfc], b1p_in[e, :, :], b1sem)
    b1_total = b1sem.n
    relu_sb = k.sb("relu_sb", [128, nfc * 512], BF16)
    ps_h = [k.psum("ps_h0", [128, 512]), k.psum("ps_h1", [128, 512])]
    ps_y2 = [k.psum(f"ps_y2{i}", [128, 512]) for i in range(4)]
    ymask = k.sb("ymask", [128, 512], BF16)

    nfp = F // 256

    def load_w1(e, waits):
        for c in range(nch):
            k.dma("sync", w1r[:, c * F:(c + 1) * F],
                  w1t_in[e, 128 * c:128 * (c + 1), :],
                  w1sem, waits=(list(waits) if c == 0 else []))

    nhalf = (d + 511) // 512
    cons2 = [(e, g, half, fc) for e in range(n_exp) for g in range(geo.groups)
             for half in range(nhalf) for fc in range(nfc)]

    def load_w2(i, waits):
        e, g, half, fc = cons2[i]
        hw = min(512, d - half * 512)
        b = i % NW2
        k.dma("gpsimd", w2_sb[b][:, 0:hw],
              w2t_in[e, 128 * fc:128 * (fc + 1), half * 512: half * 512 + hw],
              w2sem[b], waits=waits)

    load_w1(0, [])
    for i in range(NW2):
        load_w2(i, [])
    i2 = 0
    for e in range(n_exp):
        w1_tgt = 16 * nch * (e + 1)
        if e > 0:
            k.dma("gpsimd", b2_sb[:], b2_in[e:e + 1, :], b2sem,
                  waits=[(k.tt, k.tt.n)])
        b2_tgt = 16 * (e + 1)
        for g in range(geo.groups):
            rel_st = {}
            for fp in range(nfp):
                for sub in range(2):
                    fc = fp * 2 + sub
                    hb = fc % 2
                    for c in range(nch):
                        w = []
                        if c == 0:
                            w = [(k.vt, state_done),
                                 (k.st, rel_st.get(fc - 2, k.st.n))]
                            if g == 0 and fc == 0:
                                w.append((w1sem, w1_tgt))
                        k.t(lambda e2, c=c, g=g, fp=fp, sub=sub, hb=hb: e2.matmul(
                            ps_h[hb][:], w1r[:, c * F + fp * 256 + sub * 128:
                                             c * F + fp * 256 + (sub + 1) * 128],
                            nrmT[:, c * Tl + g * 512: c * Tl + (g + 1) * 512],
                            start=(c == 0), stop=(c == nch - 1)),
                            waits=w, milestone=(c == nch - 1))
                    rel_st[fc] = k.s(lambda e2, fc=fc, e=e, hb=hb: e2.activation(
                        out=relu_sb[:, fc * 512:(fc + 1) * 512], in_=ps_h[hb][:],
                        func=AF.Relu, bias=b1_sb[:, e * nfc + fc: e * nfc + fc + 1]),
                        waits=[(k.tt, k.tt.n), (b1sem, b1_total)])
            if g == geo.groups - 1 and e + 1 < n_exp:
                load_w1(e + 1, [(k.tt, k.tt.n)])
            relu_done_st = k.st.n
            for half in range(nhalf):
                hw = min(512, d - half * 512)
                for fc in range(nfc):
                    b = i2 % NW2
                    w_w = (w2sem[b], 16 * (i2 // NW2 + 1))
                    for kk in range(4):
                        k.t(lambda e2, fc=fc, kk=kk, b=b, hw=hw: e2.matmul(
                            ps_y2[kk][:, 0:hw],
                            relu_sb[:, fc * 512 + kk * 128: fc * 512 + (kk + 1) * 128],
                            w2_sb[b][:, 0:hw],
                            start=(fc == 0), stop=False),
                            waits=([w_w, (k.st, relu_done_st), (k.vt, k.vt.n)]
                                   if (kk == 0 and fc == 0) else
                                   ([w_w] if kk == 0 else [])),
                            milestone=(kk == 3))
                    if i2 + NW2 < len(cons2):
                        load_w2(i2 + NW2, [(k.tt, k.tt.n)])
                    i2 += 1
                for kk in range(4):
                    ti = 4 * g + kk
                    k.t(lambda e2, e=e, half=half, kk=kk, hw=hw: e2.matmul(
                        ps_y2[kk][:, 0:hw], cm.ones1b[0:1, 0:128],
                        b2_sb[:, half * 512: half * 512 + hw],
                        start=False, stop=True),
                        waits=[(b2sem, b2_tgt)], milestone=True)
                    tmm = k.tt.n
                    k.s(lambda e2, ti=ti, e=e, kk=kk, hw=hw: e2.mul(
                        ymask[:, 0:hw], ps_y2[kk][:, 0:hw],
                        masks[:, ti * n_exp + e: ti * n_exp + e + 1]),
                        waits=[(k.tt, tmm), (k.vt, k.vt.n)])
                    k.v(lambda e2, ti=ti, half=half, hw=hw: e2.tensor_tensor(
                        out=acc[:, ti * d + half * 512: ti * d + half * 512 + hw],
                        in0=acc[:, ti * d + half * 512: ti * d + half * 512 + hw],
                        in1=ymask[:, 0:hw], op=ALU.add), waits=[k.st])

    xs2 = [k.sem("xt0"), k.sem("xt1")]
    x2_sb = x_sb
    yq = k.sem("yq")
    y_sb = k.sb("y_sb", [128, d], F32)
    k.dma("sync", x2_sb[0][:], x_in[0, :, :], xs2[0], waits=[k.vt, k.st])
    moe_done = k.vt.n
    for ti in range(nslots):
        buf = ti % 2
        if ti + 1 < nslots:
            k.dma("sync", x2_sb[(ti + 1) % 2][:], x_in[ti + 1, :, :], xs2[(ti + 1) % 2],
                  waits=[k.vt, k.st])
        emit_ln(k, cm, x2_sb[buf][:], None, d,
                x_waits=[(xs2[buf], 16 * (ti // 2 + 1)), (k.vt, moe_done)],
                out_f32=feats_f[:])
        k.v(lambda e, buf=buf: e.tensor_tensor(
            out=y_sb[:], in0=x2_sb[buf][:], in1=feats_f[:],
            op=ALU.add),
            waits=[(yq, 16 * ti), k.vt])
        k.v(lambda e, ti=ti: e.scalar_tensor_tensor(
            out=y_sb[:], in0=acc[:, ti * d:(ti + 1) * d],
            scalar=alpha[:, ti:ti + 1], in1=y_sb[:], op0=ALU.mult, op1=ALU.add))
        k.dma("sync", y_out[ti, :, :], y_sb[:], yq, waits=[k.vt])

    k.emit("sync", lambda e: e.nop(), waits=[(yq, yq.n)])
    return k



def build_prepass(geo: Geo, name="prepass", F=4096, d=None):
    k = K(name)
    d = d or geo.d
    nch = d // 128
    nfc = F // 128
    F2 = 2 * d

    w1_in = k.inp("w1", [F, d], F32)
    w2_in = k.inp("w2", [d, F], F32)
    g_in = k.inp("gamma", [d], F32)
    b_in = k.inp("beta", [d], F32)
    b1_in = k.inp("b1", [F], F32)
    l0a_in = k.inp("l0a", [5, d, F2], F32)
    l0b_in = k.inp("l0b", [3, F2, d], F32)

    w1t_out = k.outp("w1t", [d, F], BF16)
    w2t_out = k.outp("w2t", [F, d], BF16)
    b1p_out = k.outp("b1p", [128, nfc], F32)
    l0a_out = k.outp("l0a_o", [5, d, F2], BF16)
    l0b_out = k.outp("l0b_o", [3, F2, d], BF16)

    cm = Common(k, d)
    cm.emit_init(k)

    ssem = k.sem("sld")
    g8 = k.sb("g8", [8, 128], BF16)
    b1_32 = k.sb("b1_32", [F // 128, 128], F32)
    beta_row = k.sb("beta_row", [1, d], F32)
    gsem2 = k.sem("gld2")
    k.dma("gpsimd", g8[:], g_in.rearrange("(a b) -> a b", b=128), gsem2)
    k.dma("sync", b1_32[:], b1_in.rearrange("(a b) -> a b", b=128), ssem)
    k.dma("sync", beta_row[:], b_in[None, :], ssem)
    sld = ssem.n
    gld2 = gsem2.n
    gscale = k.sb("gscale", [128, 8], F32)
    b1T = k.sb("b1T", [128, F // 128], F32)
    beta_b = k.sb("beta_b", [128, d], F32)
    ps_small = k.psum("ps_small", [128, 512])
    ps_smallb = k.psum("ps_smallb", [128, 128], BF16)
    k.t(lambda e: e.transpose(out=ps_smallb[:, 0:8], in_=g8[:],
                              identity=cm.ident[0:8, 0:8]),
        waits=[(gsem2, gld2), k.vt], milestone=True)
    k.v(lambda e: e.tensor_copy(out=gscale[:], in_=ps_smallb[:, 0:8]), waits=[k.tt])
    ps_f = k.psum("ps_f", [128, 128])
    k.t(lambda e: e.transpose(out=ps_f[:, 0:F // 128], in_=b1_32[:],
                              identity=cm.identf[0:F // 128, 0:F // 128]),
        waits=[(ssem, sld), k.vt], milestone=True)
    k.v(lambda e: e.tensor_copy(out=b1T[:], in_=ps_f[:, 0:F // 128]), waits=[k.tt])
    for half in range((d + 511) // 512):
        hw = min(512, d - half * 512)
        k.t(lambda e, half=half, hw=hw: e.matmul(
            ps_small[:, 0:hw], cm.ones1[:], beta_row[0:1, half * 512:half * 512 + hw],
            start=True, stop=True), waits=[(ssem, sld), k.vt], milestone=True)
        k.v(lambda e, half=half, hw=hw: e.tensor_copy(
            out=beta_b[:, half * 512:half * 512 + hw], in_=ps_small[:, 0:hw]),
            waits=[k.tt])

    wsem = [k.sem("wl0"), k.sem("wl1")]
    row_sb = [k.sb(f"row_sb{i}", [128, max(F2, F)], BF16) for i in range(2)]
    junk = k.sb("junk", [128, max(F2, F)], BF16)
    b1pc = k.sb("b1pc", [128, 1], F32)
    b1pa = k.sb("b1pa", [128, nfc], F32)
    stage = k.sb("stage", [128, nch * F], BF16)
    ps_t = k.psum("ps_t", [128, 128], BF16)
    oq = k.sem("oq")

    k.dma("gpsimd", row_sb[0][:, 0:d], w1_in[0:128, :], wsem[0])
    for ft in range(F // 128):
        bb = ft % 2
        if ft + 1 < F // 128:
            k.dma("gpsimd", row_sb[(ft + 1) % 2][:, 0:d], w1_in[128 * (ft + 1):128 * (ft + 2), :],
                  wsem[(ft + 1) % 2], waits=[k.vt, (k.tt, k.tt.n)])
        ww = (wsem[bb], 16 * (ft // 2 + 1))
        k.v(lambda e, bb=bb, ft=ft: e.scalar_tensor_tensor(
            out=junk[:, 0:d], in0=row_sb[bb][:, 0:d], scalar=1.0, in1=beta_b[:],
            op0=ALU.mult, op1=ALU.mult, accum_out=b1pc[:]), waits=[ww])
        k.v(lambda e, ft=ft: e.tensor_copy(out=b1pa[:, ft:ft + 1], in_=b1pc[:]))
        for c in range(nch):
            k.t(lambda e, bb=bb, c=c: e.transpose(
                out=ps_t[:], in_=row_sb[bb][:, 128 * c:128 * (c + 1)],
                identity=cm.ident[:]), waits=[ww, k.vt, (k.st, k.st.n)], milestone=True)
            k.s(lambda e, c=c, ft=ft: e.mul(
                stage[:, c * F + 128 * ft: c * F + 128 * (ft + 1)], ps_t[:],
                gscale[:, c:c + 1]), waits=[k.tt, (ssem, sld), k.vt])
    k.v(lambda e: e.tensor_tensor(out=b1pa[:], in0=b1pa[:], in1=b1T[:], op=ALU.add),
        waits=[k.st])
    k.dma("sync", b1p_out[:, :], b1pa[:], oq, waits=[k.vt])
    for c in range(nch):
        k.dma("sync", w1t_out[128 * c:128 * (c + 1), :], stage[:, c * F:(c + 1) * F], oq,
              waits=[k.st, k.vt])
    w1_stores = oq.n

    k.dma("gpsimd", row_sb[0][:, 0:F], w2_in[0:128, :], wsem[0],
          waits=[(oq, w1_stores), (k.tt, k.tt.n), k.vt])
    for dt_ in range(nch):
        bb = dt_ % 2
        if dt_ + 1 < nch:
            k.dma("gpsimd", row_sb[(dt_ + 1) % 2][:, 0:F], w2_in[128 * (dt_ + 1):128 * (dt_ + 2), :],
                  wsem[(dt_ + 1) % 2], waits=[(k.tt, k.tt.n)])
        ww = (wsem[bb], 16 * (F // 128 // 2) + 16 * (dt_ // 2 + 1))
        for fc in range(nfc):
            k.t(lambda e, bb=bb, fc=fc: e.transpose(
                out=ps_t[:], in_=row_sb[bb][:, 128 * fc:128 * (fc + 1)],
                identity=cm.ident[:]), waits=[ww, k.vt], milestone=True)
            k.v(lambda e, fc=fc, dt_=dt_: e.tensor_copy(
                out=stage[:, fc * d + 128 * dt_: fc * d + 128 * (dt_ + 1)], in_=ps_t[:]),
                waits=[k.tt])
    for fc in range(nfc):
        k.dma("sync", w2t_out[128 * fc:128 * (fc + 1), :], stage[:, fc * d:(fc + 1) * d], oq,
              waits=[k.vt])
    w2_stores = oq.n

    csem = [k.sem("cl0"), k.sem("cl1")]
    cq = [k.sem("cq0"), k.sem("cq1")]
    cast_sb = [k.sb(f"cast_sb{i}", [128, F2], BF16) for i in range(2)]
    jobs = []
    for j in range(5):
        for rt in range(d // 128):
            jobs.append((l0a_in[j, 128 * rt:128 * (rt + 1), :],
                         l0a_out[j, 128 * rt:128 * (rt + 1), :], F2))
    for j in range(3):
        for rt in range(F2 // 128):
            jobs.append((l0b_in[j, 128 * rt:128 * (rt + 1), :],
                         l0b_out[j, 128 * rt:128 * (rt + 1), :], d))
    for i, (src, dst, w) in enumerate(jobs):
        bb = i % 2
        k.dma("gpsimd", cast_sb[bb][:, 0:w], src, csem[bb],
              waits=[(cq[bb], cq[bb].n)])
        k.dma("sync", dst, cast_sb[bb][:, 0:w], cq[bb],
              waits=[(csem[bb], 16 * (i // 2 + 1))])

    k.emit("sync", lambda e: e.nop(),
           waits=[(oq, oq.n), (cq[0], cq[0].n), (cq[1], cq[1].n)])
    return k



def build_l0moe(geo: Geo, name="l0moe", n_ag=10, n_exp=20, Fm=2048):
    k = K(name)
    d = geo.d
    nch = geo.dchunks
    Tl = geo.T_local
    nslots = geo.n_slots
    nfc = Fm // 128

    xm_in = k.inp("xmid", [nslots, 128, d], F32)
    xo_in = k.inp("xorig", [nslots, 128, d], F32)
    gr_in = k.inp("gate_row", [1, n_ag], F32)
    g2_in = k.inp("ln2_g", [d], F32)
    b2_in = k.inp("ln2_b", [d], F32)
    ag_in = k.inp("agent_gate", [d, n_ag], F32)
    eg_in = k.inp("expert_gate", [d, n_exp], F32)
    re_in = k.inp("role_emb", [n_ag, d], F32)
    w1c_in = k.inp("w1c", [n_exp, d, Fm], BF16)
    w3c_in = k.inp("w3c", [n_exp, d, Fm], BF16)
    w2c_in = k.inp("w2c", [n_exp, Fm, d], BF16)
    y_out = k.outp("y", [nslots, 128, d], F32)
    dbg_awk = k.outp("dbg_awk", [128, 2 * nslots], F32)
    dbg_aew = k.outp("dbg_aew", [128, 2 * nslots], F32)

    cm = Common(k, d)
    cm.emit_init(k)
    k.const(1e-6)

    ssem = k.sem("sld")
    gr_sb = k.sb("gr_sb", [1, n_ag], F32)
    g_row = k.sb("g_row", [1, d], F32)
    b_row = k.sb("b_row", [1, d], F32)
    k.dma("sync", gr_sb[:], gr_in[:, :], ssem)
    k.dma("sync", g_row[:], g2_in[None, :], ssem)
    k.dma("sync", b_row[:], b2_in[None, :], ssem)
    sld = ssem.n
    gsem = k.sem("gld")
    ag_all = k.sb("ag_all", [128, 2 * nch * n_ag], BF16)
    ag_f = k.sb("ag_f", [128, n_ag], F32)
    agsem = k.sem("agld")
    eg_all = k.sb("eg_all", [128, nch * n_exp], BF16)
    re_sb = k.sb("re_sb", [n_ag, d], BF16)
    for c in range(nch):
        k.dma("gpsimd", ag_all[:, c * n_ag:(c + 1) * n_ag],
              ag_in[128 * c:128 * (c + 1), :], gsem)
        k.dma("gpsimd", eg_all[:, c * n_exp:(c + 1) * n_exp],
              eg_in[128 * c:128 * (c + 1), :], gsem)
    k.dma("gpsimd", re_sb[:], re_in[:, :], gsem)
    gld = gsem.n
    for c in range(nch):
        k.dma("sync", ag_f[:], ag_in[128 * c:128 * (c + 1), :], agsem,
              waits=[k.vt])
        k.v(lambda e, c=c: e.tensor_tensor(
            out=ag_all[:, (nch + c) * n_ag:(nch + c + 1) * n_ag], in0=ag_f[:],
            in1=ag_all[:, c * n_ag:(c + 1) * n_ag], op=ALU.subtract),
            waits=[(agsem, agsem.n), (gsem, gld)])

    gain_g = k.sb("gain_g", [128, d], F32)
    bias_g = k.sb("bias_g", [128, d], F32)
    ps_all = k.psum("ps_all", [128, 512])
    for half in range((d + 511) // 512):
        hw = min(512, d - half * 512)
        sl = slice(half * 512, half * 512 + hw)
        k.t(lambda e, sl=sl, hw=hw: e.matmul(ps_all[:, 0:hw], cm.ones1[:], g_row[0:1, sl],
                                             start=True, stop=True),
            waits=[(ssem, sld), k.vt], milestone=True)
        k.v(lambda e, sl=sl, hw=hw: e.tensor_copy(out=gain_g[:, sl], in_=ps_all[:, 0:hw]),
            waits=[k.tt])
        k.t(lambda e, sl=sl, hw=hw: e.matmul(ps_all[:, 0:hw], cm.ones1[:], b_row[0:1, sl],
                                             start=True, stop=True),
            waits=[k.vt], milestone=True)
        k.v(lambda e, sl=sl, hw=hw: e.tensor_copy(out=bias_g[:, sl], in_=ps_all[:, 0:hw]),
            waits=[k.tt])

    m1 = k.sb("m1", [1, 1], F32)
    m2 = k.sb("m2", [1, 1], F32)
    oh1 = k.sb("oh1", [1, n_ag], F32)
    oh2 = k.sb("oh2", [1, n_ag], F32)
    tmp10 = k.sb("tmp10", [1, n_ag], F32)
    e1s = k.sb("e1s", [1, 1], F32)
    e2s = k.sb("e2s", [1, 1], F32)
    iota10 = k.sb("iota10", [1, n_ag], F32)
    k.g(lambda e: e.iota(out=iota10[:], pattern=[[1, n_ag]], base=0,
                         channel_multiplier=0, allow_small_or_imprecise_dtypes=True))
    k.v(lambda e: e.tensor_reduce(out=m1[:], in_=gr_sb[:], axis=AX.X, op=ALU.max),
        waits=[(ssem, sld)])
    k.v(lambda e: e.tensor_scalar(out=oh1[:], in0=gr_sb[:], scalar1=m1[:],
                                  scalar2=None, op0=ALU.is_ge))
    zz = k.sb("zz10", [1, n_ag], F32)
    k.v(lambda e: e.memset(zz[:], 0.0))
    k.v(lambda e: e.tensor_tensor_scan(out=tmp10[:], data0=oh1[:], data1=zz[:],
                                       initial=0.0, op0=ALU.add, op1=ALU.add))
    k.v(lambda e: e.tensor_scalar(out=tmp10[:], in0=tmp10[:], scalar1=1.0,
                                  scalar2=None, op0=ALU.is_le))
    k.v(lambda e: e.tensor_mul(oh1[:], oh1[:], tmp10[:]))
    k.v(lambda e: e.scalar_tensor_tensor(out=tmp10[:], in0=oh1[:], scalar=NEG,
                                         in1=gr_sb[:], op0=ALU.mult, op1=ALU.add))
    k.v(lambda e: e.tensor_reduce(out=m2[:], in_=tmp10[:], axis=AX.X, op=ALU.max))
    k.v(lambda e: e.tensor_scalar(out=oh2[:], in0=tmp10[:], scalar1=m2[:],
                                  scalar2=None, op0=ALU.is_ge))
    k.v(lambda e: e.tensor_tensor_scan(out=tmp10[:], data0=oh2[:], data1=zz[:],
                                       initial=0.0, op0=ALU.add, op1=ALU.add))
    k.v(lambda e: e.tensor_scalar(out=tmp10[:], in0=tmp10[:], scalar1=1.0,
                                  scalar2=None, op0=ALU.is_le))
    k.v(lambda e: e.tensor_mul(oh2[:], oh2[:], tmp10[:]))
    k.v(lambda e: e.tensor_mul(tmp10[:], iota10[:], oh1[:]), waits=[k.gt])
    k.v(lambda e: e.tensor_reduce(out=e1s[:], in_=tmp10[:], axis=AX.X, op=ALU.add))
    k.v(lambda e: e.tensor_mul(tmp10[:], iota10[:], oh2[:]))
    k.v(lambda e: e.tensor_reduce(out=e2s[:], in_=tmp10[:], axis=AX.X, op=ALU.add))
    gate_vt = k.vt.n

    ebc = k.sb("ebc", [128, 2], F32)
    k.t(lambda e: e.matmul(ps_all[:, 0:1], cm.ones1[:], e1s[:], start=True, stop=True),
        waits=[(k.vt, gate_vt)], milestone=True)
    k.t(lambda e: e.matmul(ps_all[:, 1:2], cm.ones1[:], e2s[:], start=True, stop=True),
        milestone=True)
    k.v(lambda e: e.tensor_copy(out=ebc[:], in_=ps_all[:, 0:2]), waits=[k.tt])
    iop = k.sb("iop", [128, 1], F32)
    k.g(lambda e: e.iota(out=iop[:], pattern=[[0, 1]], base=0, channel_multiplier=1,
                         allow_small_or_imprecise_dtypes=True))
    idx13 = k.sb("idx13", [128, 4 * nch], U32)
    idx2 = k.sb("idx2", [128, 4 * nfc], U32)
    idxf = k.sb("idxf", [128, 1], F32)
    emb_vt = k.vt.n
    for bm in range(4):
        br, mm = divmod(bm, 2)
        for c in range(nch):
            k.v(lambda e, bm=bm, br=br, mm=mm, c=c: e.scalar_tensor_tensor(
                out=idxf[:], in0=ebc[:, br:br + 1], scalar=2.0 * d, in1=iop[:],
                op0=ALU.mult, op1=ALU.add), waits=[k.gt])
            k.v(lambda e, bm=bm, mm=mm, c=c: e.tensor_scalar_add(
                idxf[:], idxf[:], float(mm * d + 128 * c)))
            k.v(lambda e, bm=bm, c=c: e.tensor_copy(
                out=idx13[:, bm * nch + c: bm * nch + c + 1], in_=idxf[:]))
        for fc in range(nfc):
            k.v(lambda e, bm=bm, br=br, mm=mm, fc=fc: e.scalar_tensor_tensor(
                out=idxf[:], in0=ebc[:, br:br + 1], scalar=2.0 * Fm, in1=iop[:],
                op0=ALU.mult, op1=ALU.add))
            k.v(lambda e, bm=bm, mm=mm, fc=fc: e.tensor_scalar_add(
                idxf[:], idxf[:], float(mm * Fm + 128 * fc)))
            k.v(lambda e, bm=bm, fc=fc: e.tensor_copy(
                out=idx2[:, bm * nfc + fc: bm * nfc + fc + 1], in_=idxf[:]))
    idx_vt = k.vt.n

    sel = k.sb("sel", [1, 4 * n_exp], F32)
    iota20 = k.sb("iota20", [1, n_exp], F32)
    k.g(lambda e: e.iota(out=iota20[:], pattern=[[1, n_exp]], base=0,
                         channel_multiplier=0, allow_small_or_imprecise_dtypes=True))
    self_idx = k.sb("selidx", [1, 1], F32)
    for bm in range(4):
        br, mm = divmod(bm, 2)
        src = e1s if br == 0 else e2s
        k.v(lambda e, src=src, mm=mm: e.scalar_tensor_tensor(
            out=self_idx[:], in0=src[:], scalar=2.0, in1=src[:],
            op0=ALU.mult, op1=ALU.bypass), waits=[(k.vt, gate_vt), k.gt])
        k.v(lambda e, mm=mm: e.tensor_scalar_add(self_idx[:], self_idx[:], float(mm)))
        k.v(lambda e, bm=bm: e.tensor_scalar(
            out=sel[:, bm * n_exp:(bm + 1) * n_exp], in0=iota20[:],
            scalar1=self_idx[:], scalar2=None, op0=ALU.is_equal))
    sel_vt = k.vt.n

    xs = [k.sem("xs0"), k.sem("xs1")]
    x_sb = [k.sb(f"x_sb{i}", [128, d], F32) for i in range(2)]
    xn2 = k.sb("xn2", [128, d], BF16)
    xn2lo = k.sb("xn2lo", [128, d], BF16)
    xn2loT = k.sb("xn2loT", [128, None], BF16) if False else k.sb("xn2loT", [128, nch * 128], BF16)
    xn2T = k.sb("xn2T", [128, nch * Tl], BF16)
    ps_tr = k.psum("ps_tr", [128, 128], BF16)
    aw = k.sb("aw", [128, n_ag], F32)
    awt = k.sb("awt", [128, n_ag], F32)
    awm = k.sb("awm", [128, 1], F32)
    ohb = k.sb("ohb", [128, 2 * nslots * n_ag], BF16)
    awk = k.sb("awk", [128, 2 * nslots], F32)
    aew = k.sb("aew", [128, 2 * nslots], F32)
    awk_aew = k.sb("awk_aew", [128, 2 * nslots], F32)
    awk_aew1 = k.sb("awk_aew1", [128, 2 * nslots], F32)
    elog = k.sb("elog", [128, n_exp], F32)
    selb = k.sb("selb", [128, n_exp], F32)
    l0l1 = k.sb("l0l1", [128, 2], F32)
    zz20 = k.sb("zz20", [128, n_ag], F32)
    sc20 = k.sb("sc20", [128, n_ag], F32)
    k.v(lambda e: e.memset(zz20[:], 0.0))

    selb_all = k.sb("selb_all", [128, 4 * n_exp], F32)
    for bm in range(4):
        k.t(lambda e, bm=bm: e.matmul(ps_all[:, 32:32 + n_exp], cm.ones1[:],
                                      sel[0:1, bm * n_exp:(bm + 1) * n_exp],
                                      start=True, stop=True),
            waits=[(k.vt, sel_vt), k.vt], milestone=True)
        k.v(lambda e, bm=bm: e.tensor_copy(out=selb_all[:, bm * n_exp:(bm + 1) * n_exp],
                                           in_=ps_all[:, 32:32 + n_exp]), waits=[k.tt])

    k.dma("sync", x_sb[0][:], xm_in[0, :, :], xs[0])
    for ti in range(nslots):
        buf = ti % 2
        if ti + 1 < nslots:
            k.dma("sync", x_sb[(ti + 1) % 2][:], xm_in[ti + 1, :, :], xs[(ti + 1) % 2],
                  waits=[k.vt, k.st])
        ldw = [(xs[buf], 16 * (ti // 2 + 1))]
        emit_ln(k, cm, x_sb[buf][:], xn2[:], d, x_waits=ldw)
        k.v(lambda e: e.tensor_mul(cm.sqd[:, :d], xn2[:], gain_g[:]))
        k.v(lambda e: e.tensor_tensor(out=cm.sqd[:, :d], in0=cm.sqd[:, :d],
                                      in1=bias_g[:], op=ALU.add))
        k.v(lambda e: e.tensor_copy(out=xn2[:], in_=cm.sqd[:, :d]))
        k.v(lambda e: e.tensor_tensor(out=xn2lo[:], in0=cm.sqd[:, :d], in1=xn2[:],
                                      op=ALU.subtract))
        for c in range(nch):
            k.t(lambda e, c=c: e.transpose(out=ps_tr[:], in_=xn2[:, 128 * c:128 * (c + 1)],
                                           identity=cm.ident[:]), waits=[k.vt],
                milestone=True)
            k.v(lambda e, c=c, ti=ti: e.tensor_copy(
                out=xn2T[:, c * Tl + ti * 128: c * Tl + ti * 128 + 128], in_=ps_tr[:]),
                waits=[k.tt])
        for c in range(nch):
            k.t(lambda e, c=c: e.transpose(out=ps_tr[:], in_=xn2lo[:, 128 * c:128 * (c + 1)],
                                           identity=cm.ident[:]), waits=[k.vt],
                milestone=True)
            k.v(lambda e, c=c: e.tensor_copy(out=xn2loT[:, 128 * c:128 * (c + 1)],
                                             in_=ps_tr[:]), waits=[k.tt])
        nmm_ag = 0
        for c in range(nch):
            for (hi_side, tblo) in ((True, 0), (True, 1), (False, 0)):
                nmm_ag += 1
                k.t(lambda e, c=c, ti=ti, hi_side=hi_side, tblo=tblo,
                    st_=(nmm_ag == 1), sp_=(nmm_ag == 3 * nch): e.matmul(
                    ps_all[:, 0:n_ag],
                    (xn2T[:, c * Tl + ti * 128: c * Tl + ti * 128 + 128]
                     if hi_side else xn2loT[:, 128 * c:128 * (c + 1)]),
                    ag_all[:, (tblo * nch + c) * n_ag:(tblo * nch + c + 1) * n_ag],
                    start=st_, stop=sp_),
                    waits=[k.vt, (gsem, gld)], milestone=(nmm_ag == 3 * nch))
        k.s(lambda e: e.activation(out=aw[:], in_=ps_all[:, 0:n_ag], func=AF.Exp),
            waits=[(k.tt, k.tt.n)])
        k.v(lambda e: e.tensor_reduce(out=awm[:], in_=aw[:], axis=AX.X, op=ALU.add),
            waits=[k.st])
        k.v(lambda e: e.reciprocal(awm[:], awm[:]))
        k.v(lambda e: e.tensor_scalar_mul(aw[:], aw[:], awm[:]))
        for br in range(2):
            k.v(lambda e: e.tensor_reduce(out=awm[:], in_=aw[:], axis=AX.X, op=ALU.max))
            k.v(lambda e: e.tensor_scalar(out=awt[:], in0=aw[:], scalar1=awm[:],
                                          scalar2=None, op0=ALU.is_ge))
            k.v(lambda e: e.tensor_tensor_scan(out=sc20[:], data0=awt[:], data1=zz20[:],
                                               initial=0.0, op0=ALU.add, op1=ALU.add))
            k.v(lambda e: e.tensor_scalar(out=sc20[:], in0=sc20[:], scalar1=1.0,
                                          scalar2=None, op0=ALU.is_le))
            k.v(lambda e: e.tensor_mul(awt[:], awt[:], sc20[:]))
            k.v(lambda e, ti=ti, br=br: e.tensor_copy(
                out=ohb[:, (ti * 2 + br) * n_ag:(ti * 2 + br + 1) * n_ag], in_=awt[:]))
            k.v(lambda e, ti=ti, br=br: e.tensor_copy(out=awk[:, ti * 2 + br: ti * 2 + br + 1],
                                                      in_=awm[:]))
            k.v(lambda e: e.scalar_tensor_tensor(out=aw[:], in0=awt[:], scalar=NEG,
                                                 in1=aw[:], op0=ALU.mult, op1=ALU.add))
        k.v(lambda e, ti=ti: e.tensor_tensor(out=awm[:], in0=awk[:, ti * 2:ti * 2 + 1],
                                             in1=awk[:, ti * 2 + 1:ti * 2 + 2], op=ALU.add))
        k.v(lambda e: e.tensor_scalar_add(awm[:], awm[:], 1e-6))
        k.v(lambda e: e.reciprocal(awm[:], awm[:]))
        k.v(lambda e, ti=ti: e.tensor_scalar_mul(awk[:, ti * 2:ti * 2 + 2],
                                                 awk[:, ti * 2:ti * 2 + 2], awm[:]))
        for c in range(nch):
            k.t(lambda e, c=c, ti=ti: e.matmul(
                ps_all[:, 0:n_exp], xn2T[:, c * Tl + ti * 128: c * Tl + ti * 128 + 128],
                eg_all[:, c * n_exp:(c + 1) * n_exp],
                start=(c == 0), stop=(c == nch - 1)),
                waits=[k.vt, (gsem, gld)], milestone=(c == nch - 1))
        k.v(lambda e: e.tensor_copy(out=elog[:], in_=ps_all[:, 0:n_exp]),
            waits=[(k.tt, k.tt.n)])
        for br in range(2):
            for mm in range(2):
                bm = br * 2 + mm
                k.v(lambda e, bm=bm: e.tensor_mul(selb[:], elog[:],
                                                  selb_all[:, bm * n_exp:(bm + 1) * n_exp]))
                k.v(lambda e, mm=mm: e.tensor_reduce(out=l0l1[:, mm:mm + 1], in_=selb[:],
                                                     axis=AX.X, op=ALU.add))
            k.v(lambda e: e.tensor_sub(l0l1[:, 0:1], l0l1[:, 0:1], l0l1[:, 1:2]))
            k.s(lambda e, ti=ti, br=br: e.activation(out=aew[:, ti * 2 + br: ti * 2 + br + 1],
                                                     in_=l0l1[:, 0:1], func=AF.Sigmoid),
                waits=[k.vt])
            co = ti * 2 + br
            k.v(lambda e, co=co: e.tensor_tensor(
                out=awk_aew[:, co:co + 1], in0=awk[:, co:co + 1],
                in1=aew[:, co:co + 1], op=ALU.mult), waits=[k.st])
            k.v(lambda e, co=co: e.tensor_sub(
                awk_aew1[:, co:co + 1], awk[:, co:co + 1], awk_aew[:, co:co + 1]))
    state_vt = k.vt.n
    state_st = k.st.n

    w13sem = k.sem("w13s")
    w13_sb = k.sb("w13_sb", [128, nch * Fm], BF16)
    w33_sb = k.sb("w33_sb", [128, nch * Fm], BF16)
    NW2 = 4
    w2sem = [k.sem(f"w2s{i}") for i in range(NW2)]
    w2_sb = [k.sb(f"w2m_sb{i}", [128, 512], BF16) for i in range(NW2)]
    hbuf = k.sb("hbuf", [128, nfc * 512], BF16)
    silu_sb = k.sb("silu_sb", [128, 512], BF16)
    acc = k.sb("acc", [128, nslots * d], BF16)
    ps_h1 = k.psum("ps_h1", [128, 512])
    ps_h3 = k.psum("ps_h3", [128, 512])
    ps_eo = [k.psum(f"ps_eo{i}", [128, 512]) for i in range(4)]
    ps_ae = ps_all
    ps_trh = ps_tr
    ohT_sb = k.sb("ohT_sb", [n_ag, 128], BF16)
    t_ev = [k.sb(f"t_ev{i}", [128, 512], F32) for i in range(2)]
    k.v(lambda e: e.memset(acc[:], 0.0))

    nfp = Fm // 256
    IndA = IndirectOffsetOnAxis

    def load_w13(bm, waits):
        for c in range(nch):
            k.emit("gpsimd", lambda e, c=c, bm=bm: e.indirect_dma_start(
                out=w13_sb[:, c * Fm:(c + 1) * Fm], out_offset=None,
                in_=w1c_in.rearrange("e d f -> (e d) f"),
                in_offset=IndA(ap=idx13[:, bm * nch + c: bm * nch + c + 1], axis=0),
                element_offset=0),
                waits=(list(waits) if c == 0 else []) + [(k.vt, idx_vt)],
                inc=w13sem, inck=16)
            k.emit("gpsimd", lambda e, c=c, bm=bm: e.indirect_dma_start(
                out=w33_sb[:, c * Fm:(c + 1) * Fm], out_offset=None,
                in_=w3c_in.rearrange("e d f -> (e d) f"),
                in_offset=IndA(ap=idx13[:, bm * nch + c: bm * nch + c + 1], axis=0),
                element_offset=0),
                waits=[], inc=w13sem, inck=16)

    nhalf = (d + 511) // 512

    def load_w2(i, waits):
        bm = i // (geo.groups * nhalf * nfc)
        rem = i % (geo.groups * nhalf * nfc)
        fc = rem % nfc
        half = (rem // nfc) % nhalf
        b = i % NW2
        hw = min(512, d - half * 512)
        k.emit("gpsimd",
               lambda e, b=b, fc=fc, bm=bm, half=half, hw=hw: e.indirect_dma_start(
                   out=w2_sb[b][:, 0:hw], out_offset=None,
                   in_=w2c_in.rearrange("e f dd -> (e f) dd"),
                   in_offset=IndA(ap=idx2[:, bm * nfc + fc: bm * nfc + fc + 1], axis=0),
                   element_offset=half * 512),
               waits=waits + [(k.vt, idx_vt)], inc=w2sem[b], inck=16)

    n2 = 4 * geo.groups * nhalf * nfc
    load_w13(0, [])
    for i in range(NW2):
        load_w2(i, [])
    i2 = 0
    for bm in range(4):
        br, mm = divmod(bm, 2)
        w13_tgt = 16 * 2 * nch * (bm + 1)
        for g in range(geo.groups):
            for fp in range(nfp):
                for sub in range(2):
                    fc = fp * 2 + sub
                    for c in range(nch):
                        w = []
                        if c == 0:
                            w = [(k.vt, state_vt), (k.st, k.st.n), (k.vt, k.vt.n)]
                            if g == 0 and fp == 0 and sub == 0:
                                w.append((w13sem, w13_tgt))
                        k.t(lambda e2, c=c, g=g, fp=fp, sub=sub: e2.matmul(
                            ps_h1[:], w13_sb[:, c * Fm + fp * 256 + sub * 128:
                                             c * Fm + fp * 256 + (sub + 1) * 128],
                            xn2T[:, c * Tl + g * 512: c * Tl + (g + 1) * 512],
                            start=(c == 0), stop=(c == nch - 1)),
                            waits=w, milestone=(c == nch - 1))
                    for c in range(nch):
                        k.t(lambda e2, c=c, g=g, fp=fp, sub=sub: e2.matmul(
                            ps_h3[:], w33_sb[:, c * Fm + fp * 256 + sub * 128:
                                             c * Fm + fp * 256 + (sub + 1) * 128],
                            xn2T[:, c * Tl + g * 512: c * Tl + (g + 1) * 512],
                            start=(c == 0), stop=(c == nch - 1)),
                            waits=[], milestone=(c == nch - 1))
                    tmm = k.tt.n
                    k.s(lambda e2: e2.activation(out=silu_sb[:], in_=ps_h1[:],
                                                 func=AF.Silu), waits=[(k.tt, tmm)])
                    k.v(lambda e2, fc=fc: e2.tensor_tensor(
                        out=hbuf[:, fc * 512:(fc + 1) * 512], in0=silu_sb[:],
                        in1=ps_h3[:], op=ALU.mult), waits=[k.st])
            if g == geo.groups - 1 and bm + 1 < 4:
                load_w13(bm + 1, [(k.tt, k.tt.n)])
            h_done = k.vt.n
            for half in range(nhalf):
                hw = min(512, d - half * 512)
                for fc in range(nfc):
                    b = i2 % NW2
                    w_w = (w2sem[b], 16 * (i2 // NW2 + 1))
                    for kk in range(4):
                        k.t(lambda e2, fc=fc, kk=kk, b=b, hw=hw: e2.matmul(
                            ps_eo[kk][:, 0:hw],
                            hbuf[:, fc * 512 + kk * 128: fc * 512 + (kk + 1) * 128],
                            w2_sb[b][:, 0:hw],
                            start=(fc == 0), stop=(fc == nfc - 1)),
                            waits=([w_w, (k.vt, h_done), (k.st, k.st.n)]
                                   if (kk == 0 and fc == 0)
                                   else ([w_w] if kk == 0 else [])),
                            milestone=(kk == 3))
                    if i2 + NW2 < n2:
                        load_w2(i2 + NW2, [(k.tt, k.tt.n)])
                    i2 += 1
                eo_tt = k.tt.n
                wcol = awk_aew if mm == 0 else awk_aew1
                for kk in range(4):
                    ti = 4 * g + kk
                    sl = slice(ti * d + half * 512, ti * d + half * 512 + hw)
                    k.t(lambda e2, ti=ti, br=br: e2.transpose(
                        out=ps_trh[0:n_ag, :],
                        in_=ohb[:, (ti * 2 + br) * n_ag:(ti * 2 + br + 1) * n_ag],
                        identity=cm.ident[:]),
                        waits=[(k.vt, state_vt), (k.vt, k.vt.n)], milestone=True)
                    k.v(lambda e2: e2.tensor_copy(out=ohT_sb[0:n_ag, :],
                                                  in_=ps_trh[0:n_ag, :]),
                        waits=[k.tt])
                    k.t(lambda e2, half=half, hw=hw: e2.matmul(
                        ps_ae[:, 0:hw], ohT_sb[0:n_ag, :],
                        re_sb[:, half * 512: half * 512 + hw],
                        start=True, stop=True),
                        waits=[k.vt, (gsem, gld)], milestone=True)
                    k.s(lambda e2, kk=kk, ti=ti, hw=hw, br=br, wcol=wcol: e2.mul(
                        t_ev[0][:, 0:hw], ps_eo[kk][:, 0:hw],
                        wcol[:, ti * 2 + br: ti * 2 + br + 1]),
                        waits=[(k.tt, eo_tt), (k.vt, k.vt.n)])
                    k.v(lambda e2, hw=hw: e2.tensor_tensor(
                        out=t_ev[1][:, 0:hw], in0=t_ev[0][:, 0:hw],
                        in1=ps_ae[:, 0:hw], op=ALU.mult), waits=[k.st, k.tt])
                    k.v(lambda e2, hw=hw: e2.scalar_tensor_tensor(
                        out=t_ev[1][:, 0:hw], in0=t_ev[1][:, 0:hw], scalar=0.1,
                        in1=t_ev[0][:, 0:hw], op0=ALU.mult, op1=ALU.add))
                    k.v(lambda e2, sl=sl, hw=hw: e2.tensor_tensor(
                        out=acc[:, sl], in0=acc[:, sl], in1=t_ev[1][:, 0:hw],
                        op=ALU.add))

    dbq = k.sem("dbq")
    k.dma("sync", dbg_awk[:, :], awk[:], dbq, waits=[k.vt, k.st])
    k.dma("sync", dbg_aew[:, :], aew[:], dbq, waits=[k.vt, k.st])
    xs2 = [k.sem("xt0"), k.sem("xt1")]
    xo2 = [k.sem("xo0"), k.sem("xo1")]
    x2_sb = x_sb
    xo_sb = [cm.sqd[:, 0:d], k.sb("xo_sb1", [128, d], F32)]
    yq = [k.sem("yq0"), k.sem("yq1")]
    y_sb = [gain_g, bias_g]
    k.dma("sync", x2_sb[0][:], xm_in[0, :, :], xs2[0], waits=[k.vt, k.st])
    k.dma("sync", xo_sb[0][:], xo_in[0, :, :], xo2[0], waits=[k.vt, k.st])
    moe_done = k.vt.n
    for ti in range(nslots):
        buf = ti % 2
        if ti + 1 < nslots:
            k.dma("sync", x2_sb[(ti + 1) % 2][:], xm_in[ti + 1, :, :], xs2[(ti + 1) % 2],
                  waits=[k.vt])
            k.dma("sync", xo_sb[(ti + 1) % 2][:], xo_in[ti + 1, :, :], xo2[(ti + 1) % 2],
                  waits=[k.vt])
        k.v(lambda e, ti=ti, buf=buf: e.tensor_tensor(
            out=y_sb[buf][:], in0=x2_sb[buf][:], in1=acc[:, ti * d:(ti + 1) * d],
            op=ALU.add),
            waits=[(xs2[buf], 16 * (ti // 2 + 1)), (k.vt, moe_done),
                   (yq[buf], yq[buf].n), k.vt])
        k.v(lambda e, buf=buf: e.tensor_tensor(
            out=y_sb[buf][:], in0=y_sb[buf][:], in1=xo_sb[buf][:], op=ALU.add),
            waits=[(xo2[buf], 16 * (ti // 2 + 1))])
        k.dma("sync", y_out[ti, :, :], y_sb[buf][:], yq[buf], waits=[k.vt])
    k.emit("sync", lambda e: e.nop(), waits=[(yq[0], yq[0].n), (yq[1], yq[1].n)])
    return k



def build_attnb_prep(geo: Geo, name="abprep"):
    k = K(name)
    d = geo.d
    nch = geo.dchunks
    half = nch // 2
    Tl = geo.T_local

    x_in = k.inp("x", [geo.n_slots, 128, d], F32)
    wi_in = k.inp("wi", [d, d], F32)
    wp_in = k.inp("wp", [d, d], F32)
    rc_in = k.inp("ropeC", [128, half * Tl], F32)
    rs_in = k.inp("ropeS", [128, half * Tl], F32)
    it_out = k.outp("it", [d, Tl], BF16)
    pt_out = k.outp("pt", [d, Tl], BF16)
    pn_out = k.outp("pn", [geo.n_slots, 128, d], BF16)
    in_out = k.outp("inn", [geo.n_slots, 128, d], BF16)

    cm = Common(k, d)
    cm.emit_init(k)

    wsem = k.sem("wld")
    wi_sb = k.sb("wi_sb", [128, nch * d], BF16)
    wp_sb = k.sb("wp_sb", [128, nch * d], BF16)
    for c in range(nch):
        k.dma("gpsimd", wi_sb[:, c * d:(c + 1) * d], wi_in[128 * c:128 * (c + 1), :], wsem)
        k.dma("gpsimd", wp_sb[:, c * d:(c + 1) * d], wp_in[128 * c:128 * (c + 1), :], wsem)
    rsem = k.sem("rld")
    rc_sb = k.sb("rc_sb", [128, half * Tl], F32)
    rs_sb = k.sb("rs_sb", [128, half * Tl], F32)
    k.dma("sync", rc_sb[:], rc_in[:, :], rsem)
    k.dma("sync", rs_sb[:], rs_in[:, :], rsem)
    rld = rsem.n

    xs = [k.sem("xs0"), k.sem("xs1")]
    x_sb = [k.sb(f"x_sb{i}", [128, d], F32) for i in range(2)]
    xn_sb = k.sb("xn_sb", [128, d], BF16)
    xnT = k.sb("xnT", [128, nch * 512], BF16)
    raw = k.sb("raw", [128, nch * 512], BF16)
    rop = k.sb("rop", [128, nch * 512], BF16)
    tok = k.sb("tok", [128, d], BF16)
    ssq = k.sb("ssq", [128, 1], F32)
    rn = k.sb("rn", [128, 1], F32)
    sqd2 = k.sb("sqd2", [128, max(d, 512)], F32)
    ps_tr = k.psum("ps_tr", [128, 128], BF16)
    ps_p = [k.psum(f"ps_p{i}", [128, 512]) for i in range(2)]
    oq = k.sem("oq")
    oq2 = k.sem("oq2")

    k.dma("sync", x_sb[0][:], x_in[0, :, :], xs[0])
    for m in range(geo.groups):
        for kk in range(4):
            ti = 4 * m + kk
            buf = ti % 2
            if ti + 1 < geo.n_slots:
                k.dma("sync", x_sb[(ti + 1) % 2][:], x_in[ti + 1, :, :], xs[(ti + 1) % 2],
                      waits=[k.vt, k.st])
            emit_ln(k, cm, x_sb[buf][:], xn_sb[:], d,
                    x_waits=[(xs[buf], 16 * (ti // 2 + 1))])
            for c in range(nch):
                k.t(lambda e, c=c: e.transpose(out=ps_tr[:], in_=xn_sb[:, 128 * c:128 * (c + 1)],
                                               identity=cm.ident[:]),
                    waits=[k.vt], milestone=True)
                k.v(lambda e, c=c, kk=kk: e.tensor_copy(
                    out=xnT[:, c * 512 + kk * 128: c * 512 + kk * 128 + 128],
                    in_=ps_tr[:]), waits=[k.tt])
        for wname, w_sb, t_dst, tm_dst in (("i", wi_sb, it_out, in_out),
                                           ("p", wp_sb, pt_out, pn_out)):
            for oc in range(nch):
                pb = ps_p[oc % 2]
                for c in range(nch):
                    k.t(lambda e, c=c, oc=oc, pb=pb, w_sb=w_sb: e.matmul(
                        pb[:], w_sb[:, c * d + 128 * oc: c * d + 128 * (oc + 1)],
                        xnT[:, c * 512:(c + 1) * 512],
                        start=(c == 0), stop=(c == nch - 1)),
                        waits=([k.vt, wsem] if c == 0 else []),
                        milestone=(c == nch - 1))
                k.v(lambda e, oc=oc, pb=pb: e.tensor_copy(
                    out=raw[:, oc * 512:(oc + 1) * 512], in_=pb[:]),
                    waits=[(k.tt, k.tt.n)])
            for c in range(nch):
                cf = c if c < half else c - half
                partner = c + half if c < half else c - half
                sgn = ALU.subtract if c < half else ALU.add
                co = slice(cf * Tl + m * 512, cf * Tl + (m + 1) * 512) if False else \
                    slice(cf * Tl // 1 + m * 512, cf * Tl // 1 + (m + 1) * 512)
                k.v(lambda e, c=c, cf=cf: e.tensor_tensor(
                    out=rop[:, c * 512:(c + 1) * 512],
                    in0=raw[:, c * 512:(c + 1) * 512],
                    in1=rc_sb[:, cf * Tl + m * 512: cf * Tl + (m + 1) * 512],
                    op=ALU.mult), waits=[(rsem, rld), (oq, oq.n)])
                k.v(lambda e, c=c, cf=cf, partner=partner: e.tensor_tensor(
                    out=sqd2[:, 0:512],
                    in0=raw[:, partner * 512:(partner + 1) * 512],
                    in1=rs_sb[:, cf * Tl + m * 512: cf * Tl + (m + 1) * 512],
                    op=ALU.mult))
                k.v(lambda e, c=c, sgn=sgn: e.tensor_tensor(
                    out=rop[:, c * 512:(c + 1) * 512],
                    in0=rop[:, c * 512:(c + 1) * 512], in1=sqd2[:, 0:512], op=sgn))
            rvt = k.vt.n
            for c in range(nch):
                k.dma("sync", t_dst[128 * c:128 * (c + 1), m * 512:(m + 1) * 512],
                      rop[:, c * 512:(c + 1) * 512], oq, waits=[(k.vt, rvt)])
            for kk in range(4):
                ti = 4 * m + kk
                for c in range(nch):
                    k.t(lambda e, c=c, kk=kk: e.transpose(
                        out=ps_tr[:], in_=rop[:, c * 512 + kk * 128: c * 512 + kk * 128 + 128],
                        identity=cm.ident[:]),
                        waits=[(k.vt, k.vt.n), (k.st, k.st.n)], milestone=True)
                    k.v(lambda e, c=c: e.tensor_copy(out=tok[:, 128 * c:128 * (c + 1)],
                                                     in_=ps_tr[:]),
                        waits=[k.tt, (oq2, oq2.n)])
                k.s(lambda e: e.activation(out=sqd2[:, 0:d], in_=tok[:], func=AF.Square,
                                           accum_out=ssq[:]), waits=[k.vt])
                k.s(lambda e: e.activation(out=rn[:], in_=ssq[:], func=AF.Sqrt))
                k.v(lambda e: e.tensor_scalar_max(rn[:], rn[:], 1e-12), waits=[k.st])
                k.v(lambda e: e.reciprocal(rn[:], rn[:]))
                k.v(lambda e: e.tensor_scalar_mul(tok[:], tok[:], rn[:]))
                k.dma("sync", tm_dst[ti, :, :], tok[:], oq2, waits=[k.vt])

    k.emit("sync", lambda e: e.nop(), waits=[(oq, oq.n), (oq2, oq2.n)])
    return k


def build_attnb_topk(geo: Geo, name="abtopk", K_top=15):
    k = K(name)
    d = geo.d
    nch = geo.dchunks
    Tl = geo.T_local
    S = geo.S
    cpb = geo.cpb

    it_in = k.inp("it", [d, Tl], BF16)
    pt_in = k.inp("pt", [d, S], BF16)
    mk_in = k.inp("masksq", [cpb, 128, 128], F32)
    keep_in = k.inp("keep", [geo.n_slots, 128, 16], F32)
    idx_out = k.outp("idxg", [geo.n_slots, 128, 16], U32)

    cm = Common(k, d)
    cm.emit_init(k)

    qsem = k.sem("qld")
    it_sb = k.sb("it_sb", [128, nch * Tl], BF16)
    for c in range(nch):
        k.dma("sync", it_sb[:, c * Tl:(c + 1) * Tl], it_in[128 * c:128 * (c + 1), :], qsem)
    qld = qsem.n
    msem = k.sem("mld")
    mask_sb = k.sb("mask_sb", [128, cpb * 128], F32)
    for p in range(cpb):
        k.dma("sync", mask_sb[:, 128 * p:128 * (p + 1)], mk_in[p, :, :], msem)
    mld = msem.n
    ksem = k.sem("kld")
    keep_sb = k.sb("keep_sb", [128, 16], F32)

    ptsem = [k.sem("pt0"), k.sem("pt1")]
    pt_sb = [k.sb(f"pt_sb{i}", [128, nch * 512], BF16) for i in range(2)]
    logit = k.sb("logit", [128, S], F32)
    m8 = k.sb("m8", [128, 8], F32)
    i8 = k.sb("i8", [128, 16], U32)
    idxf = k.sb("idxf", [128, 16], F32)
    rep = k.sb("rep", [128, S], F32)
    ps_s = [k.psum(f"ps_s{i}", [128, 512]) for i in range(2)]
    oq = k.sem("oq")

    sched = []
    for ti in range(geo.n_slots):
        nj = (cpb * (ti + 1) * 128 + 511) // 512
        for j5 in range(nj):
            sched.append((ti, j5))

    def load_pt(i, waits):
        ti, j5 = sched[i]
        b = i % 2
        for c in range(nch):
            k.dma("gpsimd", pt_sb[b][:, c * 512:(c + 1) * 512],
                  pt_in[128 * c:128 * (c + 1), 512 * j5:512 * (j5 + 1)], ptsem[b],
                  waits=waits)

    load_pt(0, [])
    load_pt(1, [])
    ii = 0
    for ti in range(geo.n_slots):
        Wt = cpb * (ti + 1) * 128
        nj = Wt // 512
        k.dma("sync", keep_sb[:], keep_in[ti, :, :], ksem, waits=[k.vt])
        kld_ti = ksem.n
        for j5 in range(nj):
            b = ii % 2
            pb = ps_s[j5 % 2]
            w_w = (ptsem[b], 16 * nch * (ii // 2 + 1))
            for c in range(nch):
                k.t(lambda e, c=c, ti=ti, j5=j5, b=b, pb=pb: e.matmul(
                    pb[:], it_sb[:, c * Tl + ti * 128: c * Tl + ti * 128 + 128],
                    pt_sb[b][:, c * 512:(c + 1) * 512],
                    start=(c == 0), stop=(c == nch - 1)),
                    waits=([w_w, (qsem, qld), (k.vt, k.vt.n)] if c == 0 else []),
                    milestone=(c == nch - 1))
            if ii + 2 < len(sched):
                load_pt(ii + 2, [(k.tt, k.tt.n)])
            ii += 1
            k.v(lambda e, j5=j5, pb=pb: e.tensor_copy(
                out=logit[:, 512 * j5:512 * (j5 + 1)], in_=pb[:]),
                waits=[(k.tt, k.tt.n)])
        for p in range(cpb):
            j = cpb * ti + p
            k.v(lambda e, j=j, p=p: e.tensor_tensor(
                out=logit[:, 128 * j:128 * (j + 1)], in0=logit[:, 128 * j:128 * (j + 1)],
                in1=mask_sb[:, 128 * p:128 * (p + 1)], op=ALU.add),
                waits=[(msem, mld)])
        k.v(lambda e, Wt=Wt: e.max(out=m8[:], in_=logit[:, 0:Wt]),
            waits=[(oq, oq.n)])
        k.v(lambda e, Wt=Wt: e.max_index(out=i8[:, 0:8], in_max=m8[:],
                                         in_values=logit[:, 0:Wt]))
        k.v(lambda e, Wt=Wt: e.match_replace(out=rep[:, 0:Wt], in_to_replace=m8[:],
                                             in_values=logit[:, 0:Wt], imm_value=NEG))
        k.v(lambda e, Wt=Wt: e.max(out=m8[:], in_=rep[:, 0:Wt]))
        k.v(lambda e, Wt=Wt: e.max_index(out=i8[:, 8:16], in_max=m8[:],
                                         in_values=rep[:, 0:Wt]))
        k.v(lambda e: e.tensor_copy(out=idxf[:], in_=i8[:]))
        k.v(lambda e: e.tensor_mul(idxf[:], idxf[:], keep_sb[:]),
            waits=[(ksem, kld_ti)])
        k.v(lambda e: e.scalar_tensor_tensor(out=idxf[:], in0=keep_sb[:],
                                             scalar=-float(S), in1=idxf[:],
                                             op0=ALU.mult, op1=ALU.add))
        k.v(lambda e: e.tensor_scalar_add(idxf[:], idxf[:], float(S)))
        k.v(lambda e: e.tensor_copy(out=i8[:], in_=idxf[:]))
        k.dma("sync", idx_out[ti, :, :], i8[:], oq, waits=[k.vt])

    k.emit("sync", lambda e: e.nop(), waits=[(oq, oq.n)])
    return k


def build_attnb_main(geo: Geo, name="abmain", K_top=15):
    k = K(name)
    d = geo.d
    nch = geo.dchunks
    Tl = geo.T_local
    S = geo.S

    x_in = k.inp("x", [geo.n_slots, 128, d], F32)
    gt_in = k.inp("gtab", [S + 1, d], BF16)
    in_in = k.inp("inn", [geo.n_slots, 128, d], BF16)
    idx_in = k.inp("idxg", [geo.n_slots, 128, 16], U32)
    tg_in = k.inp("tg", [geo.n_slots, 128, 1], F32)
    keep_in = k.inp("keep", [geo.n_slots, 128, 16], F32)
    kneg_in = k.inp("kneg", [geo.n_slots, 128, 16], F32)
    w1_in = k.inp("w1", [17, 64], F32)
    w2_in = k.inp("w2", [64, 64], F32)
    wc_in = k.inp("wc", [64, 2], F32)
    wm_in = k.inp("wm", [64, 1], F32)
    b1_in = k.inp("b1", [64], F32)
    b2_in = k.inp("b2", [64], F32)
    bc_in = k.inp("bc", [2], F32)
    bm_in = k.inp("bm", [1], F32)
    pal_in = k.inp("palette", [d, 256], F32)
    wo_in = k.inp("wo", [d, d], F32)
    y_out = k.outp("y", [geo.n_slots, 128, d], F32)

    cm = Common(k, d)
    cm.emit_init(k)
    k.const(1.0)
    k.const(-1.0)

    ssem = k.sem("sld")
    w1_sb = k.sb("w1_sb", [17, 64], BF16)
    w2_sb = k.sb("w2_sb", [64, 64], BF16)
    wcm_sb = k.sb("wcm_sb", [64, 3], BF16)
    k.dma("gpsimd", w1_sb[:], w1_in[:, :], ssem)
    k.dma("gpsimd", w2_sb[:], w2_in[:, :], ssem)
    k.dma("gpsimd", wcm_sb[:, 0:2], wc_in[:, :], ssem)
    k.dma("gpsimd", wcm_sb[:, 2:3], wm_in[:, :], ssem)
    b_sem = k.sem("bld")
    b1c = k.sb("b1c", [64, 1], F32)
    b2c = k.sb("b2c", [64, 1], F32)
    bcm = k.sb("bcm", [3, 1], F32)
    k.dma("sync", b1c[:], b1_in[:, None], b_sem)
    k.dma("sync", b2c[:], b2_in[:, None], b_sem)
    k.dma("sync", bcm[0:2, :], bc_in[:, None], b_sem)
    k.dma("sync", bcm[2:3, :], bm_in[:, None], b_sem)
    sld = ssem.n
    bld = b_sem.n

    wsem = k.sem("wld")
    pal_sb = k.sb("pal_sb", [128, nch * 256], BF16)
    wo_sb = k.sb("wo_sb", [128, nch * d], BF16)
    for c in range(nch):
        k.dma("gpsimd", pal_sb[:, c * 256:(c + 1) * 256], pal_in[128 * c:128 * (c + 1), :], wsem)
        k.dma("gpsimd", wo_sb[:, c * d:(c + 1) * d], wo_in[128 * c:128 * (c + 1), :], wsem)
    wld = wsem.n
    wp_sb = k.sb("wp_sb", [128, 2 * d], BF16)
    ps_w = k.psum("ps_w", [128, 512])
    for r in range(2):
        for fh in range((d + 511) // 512):
            hw = min(512, d - fh * 512)
            for c in range(nch):
                k.t(lambda e, r=r, fh=fh, c=c, hw=hw: e.matmul(
                    ps_w[:, 0:hw], pal_sb[:, c * 256 + 128 * r: c * 256 + 128 * (r + 1)],
                    wo_sb[:, c * d + fh * 512: c * d + fh * 512 + hw],
                    start=(c == 0), stop=(c == nch - 1)),
                    waits=([(wsem, wld), k.vt] if c == 0 else []),
                    milestone=(c == nch - 1))
            k.v(lambda e, r=r, fh=fh, hw=hw: e.tensor_copy(
                out=wp_sb[:, r * d + fh * 512: r * d + fh * 512 + hw], in_=ps_w[:, 0:hw]),
                waits=[k.tt])
    wp_vt = k.vt.n

    gsem = [k.sem("g0"), k.sem("g1")]
    gdq = k.sem("gdq")
    isem = k.sem("ild")
    xsem = [k.sem("xs0"), k.sem("xs1")]
    idq = k.sem("idq")
    A_sb = k.sb("A_sb", [128, 17 * d], BF16)
    idx_sb = k.sb("idx_sb", [128, 16], U32)
    idxf = k.sb("idxf", [128, 16], F32)
    tg_sb = k.sb("tg_sb", [128, 1], F32)
    keep_sb = k.sb("keep_sb", [128, 16], F32)
    kneg_sb = k.sb("kneg_sb", [128, 16], F32)
    ATall = k.sb("ATall", [128, nch * 2048], BF16)
    relT = k.sb("relT", [17, 2048], BF16)
    dl = k.sb("dl", [128, 16], F32)
    dlrow = k.sb("dlrow", [1, 2048], F32)
    h1 = k.sb("h1", [64, 2048], BF16)
    h2 = k.sb("h2", [64, 2048], BF16)
    zmix = k.sb("zmix", [3, 2048], F32)
    gx = k.sb("gx", [128, 16], F32)
    gy = k.sb("gy", [128, 16], F32)
    wsm = k.sb("wsm", [128, 16], F32)
    wrow = k.sb("wrow", [128, 1], F32)
    fxr = k.sb("fxr", [1, 2048], F32)
    fyr = k.sb("fyr", [1, 2048], F32)
    wxr = k.sb("wxr", [1, 2048], F32)
    bc16 = k.sb("bc16", [16, 2048], F32)
    wxT = k.sb("wxT", [16, 2048], BF16)
    wyT = k.sb("wyT", [16, 2048], BF16)
    iot16 = k.sb("iot16", [16, 1], F32)
    cmbT = k.sb("cmbT", [128, 2 * 2048], BF16)
    cmb_t = k.sb("cmb_t", [128, 512], BF16)
    cmb = k.sb("cmb", [128, 2 * 128], BF16)
    cmb_f = k.sb("cmb_f", [128, 128], F32)
    x_sb = [k.sb(f"x_sb{i}", [128, d], F32) for i in range(2)]
    y_sb = [k.sb(f"y_sb{i}", [128, d], F32) for i in range(2)]
    yq = [k.sem("yq0"), k.sem("yq1")]
    ps_tr = k.psum("ps_tr", [128, 128], BF16)
    ps_tr_b = k.psum("ps_tr_b", [128, 128], BF16)
    ps_rel = [k.psum(f"ps_rel{i}", [128, 512]) for i in range(2)]
    ps_mlp = k.psum("ps_mlp", [128, 512])
    ps_c = k.psum("ps_c", [128, 512])
    ps_y = k.psum("ps_y", [128, 512])
    ytmp = k.sb("ytmp", [128, 128], BF16)
    flat_d = k.dram("flat_d", [4, 128, 16], F32)
    zrow_d = k.dram("zrow_d", [3, 2048], F32)
    repy_sb = k.sb("repy_sb", [16, 2 * 128], BF16)
    repx_sb = k.sb("repx_sb", [16, 128], BF16)
    repi = k.sb("repi", [16, 128], F32)

    k.g(lambda e: e.iota(out=iot16[:], pattern=[[0, 1]], base=0, channel_multiplier=1,
                         allow_small_or_imprecise_dtypes=True))
    k.g(lambda e: e.iota(out=repi[:], pattern=[[0, 8], [1, 16]], base=0,
                         channel_multiplier=-1, allow_small_or_imprecise_dtypes=True))
    k.v(lambda e: e.tensor_scalar(out=repx_sb[:], in0=repi[:], scalar1=0.0,
                                  scalar2=None, op0=ALU.is_equal), waits=[k.gt])
    for hf in range(2):
        k.g(lambda e, hf=hf: e.iota(out=repi[:], pattern=[[1, 8], [0, 16]], base=8 * hf,
                                    channel_multiplier=-1,
                                    allow_small_or_imprecise_dtypes=True), waits=[k.vt])
        k.v(lambda e, hf=hf: e.tensor_scalar(out=repy_sb[:, hf * 128:(hf + 1) * 128],
                                             in0=repi[:], scalar1=0.0, scalar2=None,
                                             op0=ALU.is_equal), waits=[k.gt])

    for ti in range(geo.n_slots):
        k.dma("sync", idx_sb[:], idx_in[ti, :, :], idq, waits=[k.vt, (k.gt, k.gt.n)])
        k.dma("sync", tg_sb[:], tg_in[ti, :, :], idq)
        k.dma("sync", keep_sb[:], keep_in[ti, :, :], idq)
        k.dma("sync", kneg_sb[:], kneg_in[ti, :, :], idq)
        k.dma("sync", x_sb[ti % 2][:], x_in[ti, :, :], xsem[ti % 2], waits=[k.vt])
        idq_ti = idq.n
        gvt = k.gt.n
        for kslot in range(16):
            b = kslot % 2
            k.emit("gpsimd", lambda e, kslot=kslot: e.indirect_dma_start(
                out=A_sb[:, kslot * d:(kslot + 1) * d], out_offset=None,
                in_=gt_in[:, :],
                in_offset=IndirectOffsetOnAxis(ap=idx_sb[:, kslot:kslot + 1], axis=0)),
                waits=[(idq, idq_ti), (k.tt, k.tt.n)], inc=gsem[b], inck=16)
        k.dma("sync", A_sb[:, 16 * d:17 * d], in_in[ti, :, :], isem,
              waits=[(k.tt, k.tt.n)])
        g_t = (gsem[0], gsem[0].n)
        g_t2 = (gsem[1], gsem[1].n)
        i_t = (isem, isem.n)
        tb_banks = [ps_tr, ps_tr_b]
        tb_tok = [None, None]
        tb_i = 0
        for kslot in range(17):
            if kslot == 15:
                continue
            srcw = [g_t, g_t2] if kslot < 16 else [i_t]
            slotcol = kslot if kslot < 15 else 15
            for c in range(nch):
                bk = tb_i % 2
                pb = tb_banks[bk]
                w = list(srcw)
                if tb_tok[bk] is not None:
                    w.append(tb_tok[bk])
                else:
                    w.append((k.vt, k.vt.n))
                k.t(lambda e, kslot=kslot, c=c, pb=pb: e.transpose(
                    out=pb[:], in_=A_sb[:, kslot * d + 128 * c: kslot * d + 128 * (c + 1)],
                    identity=cm.ident[:]),
                    waits=w, milestone=True)
                dst = (ATall[:].rearrange("p (ch tok sl) -> p ch tok sl", ch=nch, sl=16)
                       [:, c, :, slotcol])
                if tb_i % 2 == 0:
                    t = k.v(lambda e, dst=dst, pb=pb: e.tensor_copy(out=dst, in_=pb[:]),
                            waits=[k.tt])
                    tb_tok[bk] = (k.vt, t)
                else:
                    t = k.s(lambda e, dst=dst, pb=pb: e.activation(
                        out=dst, in_=pb[:], func=AF.Copy), waits=[(k.tt, k.tt.n)])
                    tb_tok[bk] = (k.st, t)
                tb_i += 1
        at_vt = k.vt.n
        for half32 in range(4):
            pb = ps_rel[half32 % 2]
            for t32 in range(32):
                tok = half32 * 32 + t32
                for c in range(nch):
                    k.t(lambda e, tok=tok, t32=t32, c=c, pb=pb: e.matmul(
                        pb[0:16, t32 * 16:(t32 + 1) * 16],
                        ATall[:, c * 2048 + 16 * tok: c * 2048 + 16 * (tok + 1)],
                        ATall[:, c * 2048 + 16 * tok: c * 2048 + 16 * (tok + 1)],
                        start=(c == 0), stop=(c == nch - 1)),
                        waits=([(k.vt, k.vt.n), (k.st, k.st.n)]
                               if (t32 == 0 and c == 0) else []),
                        milestone=(tok % 8 == 7 and c == nch - 1))
            k.v(lambda e, half32=half32, pb=pb: e.tensor_scalar(
                out=relT[0:16, half32 * 512:(half32 + 1) * 512], in0=pb[0:16, :],
                scalar1=1.0, scalar2=-1.0, op0=ALU.min, op1=ALU.max),
                waits=[(k.tt, k.tt.n)])
        k.v(lambda e: e.tensor_copy(out=idxf[:], in_=idx_sb[:]), waits=[(idq, idq_ti)])
        k.v(lambda e: e.scalar_tensor_tensor(out=dl[:], in0=idxf[:], scalar=-1.0,
                                             in1=tg_sb[:].to_broadcast([128, 16]),
                                             op0=ALU.mult, op1=ALU.add))
        k.v(lambda e: e.tensor_relu(dl[:], dl[:]))
        k.v(lambda e: e.tensor_scalar_mul(dl[:], dl[:], 1.0 / S))
        k.v(lambda e: e.tensor_mul(dl[:], dl[:], keep_sb[:]))
        k.dma("sync", flat_d[3, :, :], dl[:], idq, waits=[k.vt])
        dls = idq.n
        k.dma("sync", dlrow[:], flat_d[3].rearrange("p c -> () (p c)"), idq,
              waits=[(idq, dls)])
        k.emit("gpsimd", lambda e: e.dma_start(out=relT[16:17, :], in_=dlrow[:]),
               waits=[(idq, idq.n)], inc=gdq, inck=16)
        dl_t = gdq.n
        rel_vt = k.vt.n
        for fm in range(4):
            k.t(lambda e, fm=fm: e.matmul(ps_mlp[0:64, :], w1_sb[:],
                                          relT[:, fm * 512:(fm + 1) * 512],
                                          start=True, stop=True),
                waits=[(k.vt, rel_vt), (gdq, dl_t), (ssem, sld), (k.st, k.st.n)],
                milestone=True)
            k.s(lambda e, fm=fm: e.activation(out=h1[:, fm * 512:(fm + 1) * 512],
                                              in_=ps_mlp[0:64, :], func=AF.Gelu,
                                              bias=b1c[:]),
                waits=[(k.tt, k.tt.n), (b_sem, bld)])
        for fm in range(4):
            k.t(lambda e, fm=fm: e.matmul(ps_mlp[0:64, :], w2_sb[:],
                                          h1[:, fm * 512:(fm + 1) * 512],
                                          start=True, stop=True),
                waits=[(k.st, k.st.n)], milestone=True)
            k.s(lambda e, fm=fm: e.activation(out=h2[:, fm * 512:(fm + 1) * 512],
                                              in_=ps_mlp[0:64, :], func=AF.Gelu,
                                              bias=b2c[:]),
                waits=[(k.tt, k.tt.n)])
        for fm in range(4):
            k.t(lambda e, fm=fm: e.matmul(ps_mlp[0:3, :], wcm_sb[:],
                                          h2[:, fm * 512:(fm + 1) * 512],
                                          start=True, stop=True),
                waits=[(k.st, k.st.n), (k.vt, k.vt.n)], milestone=True)
            k.v(lambda e, fm=fm: e.tensor_copy(out=zmix[:, fm * 512:(fm + 1) * 512],
                                               in_=ps_mlp[0:3, :]), waits=[k.tt])
        k.v(lambda e: e.tensor_scalar(out=zmix[:], in0=zmix[:], scalar1=bcm[:],
                                      scalar2=None, op0=ALU.add))
        z_vt = k.vt.n
        k.dma("sync", zrow_d[0, :], zmix[0:1, :], idq, waits=[(k.vt, z_vt)])
        k.dma("sync", zrow_d[1, :], zmix[1:2, :], idq)
        k.dma("sync", zrow_d[2, :], zmix[2:3, :], idq)
        zst = idq.n
        k.dma("sync", gx[:], zrow_d[0].rearrange("(p c) -> p c", c=16), idq,
              waits=[(idq, zst)])
        k.dma("sync", gy[:], zrow_d[1].rearrange("(p c) -> p c", c=16), idq)
        k.dma("sync", wsm[:], zrow_d[2].rearrange("(p c) -> p c", c=16), idq)
        mix_t = idq.n
        k.v(lambda e: e.tensor_tensor(out=wsm[:], in0=wsm[:], in1=kneg_sb[:], op=ALU.add),
            waits=[(idq, mix_t)])
        k.s(lambda e: e.activation(out=wsm[:], in_=wsm[:], func=AF.Exp), waits=[k.vt])
        k.v(lambda e: e.tensor_reduce(out=wrow[:], in_=wsm[:], axis=AX.X, op=ALU.add),
            waits=[k.st])
        k.v(lambda e: e.reciprocal(wrow[:], wrow[:]))
        k.v(lambda e: e.tensor_scalar_mul(wsm[:], wsm[:], wrow[:]))
        k.s(lambda e: e.activation(out=gx[:], in_=gx[:], func=AF.Tanh), waits=[k.vt])
        k.s(lambda e: e.activation(out=gy[:], in_=gy[:], func=AF.Tanh))
        k.v(lambda e: e.tensor_scalar(out=gx[:], in0=gx[:], scalar1=1.0, scalar2=7.5,
                                      op0=ALU.add, op1=ALU.mult), waits=[k.st])
        k.v(lambda e: e.tensor_scalar(out=gx[:], in0=gx[:], scalar1=0.0, scalar2=15.0,
                                      op0=ALU.max, op1=ALU.min))
        k.v(lambda e: e.tensor_scalar(out=gy[:], in0=gy[:], scalar1=1.0, scalar2=7.5,
                                      op0=ALU.add, op1=ALU.mult))
        k.v(lambda e: e.tensor_scalar(out=gy[:], in0=gy[:], scalar1=0.0, scalar2=15.0,
                                      op0=ALU.max, op1=ALU.min))
        k.dma("sync", flat_d[0, :, :], gx[:], idq, waits=[k.vt])
        k.dma("sync", flat_d[1, :, :], gy[:], idq)
        k.dma("sync", flat_d[2, :, :], wsm[:], idq)
        fst = idq.n
        k.dma("sync", fxr[:], flat_d[0].rearrange("p c -> () (p c)"), idq,
              waits=[(idq, fst), (k.tt, k.tt.n)])
        k.dma("sync", fyr[:], flat_d[1].rearrange("p c -> () (p c)"), idq)
        k.dma("sync", wxr[:], flat_d[2].rearrange("p c -> () (p c)"), idq)
        f_t = idq.n

        for fm in range(4):
            sl = slice(fm * 512, (fm + 1) * 512)
            k.t(lambda e, sl=sl: e.matmul(ps_c[0:16, :], cm.ones1[0:1, 0:16],
                                          fxr[0:1, sl], start=True, stop=True),
                waits=[(idq, f_t), (k.vt, k.vt.n), (k.st, k.st.n)], milestone=True)
            k.v(lambda e: e.tensor_scalar(out=bc16[:, 0:512], in0=ps_c[0:16, :],
                                          scalar1=iot16[:, 0:1], scalar2=None,
                                          op0=ALU.subtract), waits=[k.tt])
            k.s(lambda e: e.activation(out=bc16[:, 0:512], in_=bc16[0:16, 0:512],
                                       func=AF.Abs), waits=[k.vt])
            k.s(lambda e, sl=sl: e.activation(out=wxT[0:16, sl], in_=bc16[0:16, 0:512],
                                              func=AF.Relu, bias=1.0, scale=-1.0))
            k.t(lambda e, sl=sl: e.matmul(ps_c[0:16, :], cm.ones1[0:1, 0:16],
                                          fyr[0:1, sl], start=True, stop=True),
                waits=[(k.st, k.st.n)], milestone=True)
            k.v(lambda e: e.tensor_scalar(out=bc16[:, 0:512], in0=ps_c[0:16, :],
                                          scalar1=iot16[:, 0:1], scalar2=None,
                                          op0=ALU.subtract), waits=[k.tt])
            k.s(lambda e: e.activation(out=bc16[:, 0:512], in_=bc16[0:16, 0:512],
                                       func=AF.Abs), waits=[k.vt])
            k.s(lambda e, sl=sl: e.activation(out=wyT[0:16, sl], in_=bc16[0:16, 0:512],
                                              func=AF.Relu, bias=1.0, scale=-1.0))
            k.t(lambda e, sl=sl: e.matmul(ps_c[0:16, :], cm.ones1[0:1, 0:16],
                                          wxr[0:1, sl], start=True, stop=True),
                waits=[(k.st, k.st.n)], milestone=True)
            k.v(lambda e, sl=sl: e.tensor_tensor(out=wyT[0:16, sl], in0=wyT[0:16, sl],
                                                 in1=ps_c[0:16, :], op=ALU.mult),
                waits=[k.tt, (k.st, k.st.n)])
        wxy_vt = k.vt.n

        for hf in range(2):
            for fm in range(4):
                sl = slice(fm * 512, (fm + 1) * 512)
                k.t(lambda e, hf=hf, sl=sl: e.matmul(
                    ps_c[:], repy_sb[:, hf * 128:(hf + 1) * 128], wyT[0:16, sl],
                    start=True, stop=True),
                    waits=[(k.vt, k.vt.n), (k.gt, k.gt.n)], milestone=True)
                k.v(lambda e: e.tensor_copy(out=cmb_t[:], in_=ps_c[:]),
                    waits=[k.tt])
                k.t(lambda e, sl=sl: e.matmul(
                    ps_c[:], repx_sb[:], wxT[0:16, sl], start=True, stop=True),
                    waits=[(k.vt, k.vt.n)], milestone=True)
                k.v(lambda e, hf=hf, sl=sl: e.tensor_tensor(
                    out=cmbT[:, hf * 2048 + sl.start: hf * 2048 + sl.stop],
                    in0=cmb_t[:], in1=ps_c[:], op=ALU.mult), waits=[k.tt])
            k.v(lambda e, hf=hf: e.tensor_reduce(
                out=cmb_f[:],
                in_=cmbT[:, hf * 2048:(hf + 1) * 2048].rearrange(
                    "p (t k) -> p t k", k=16),
                axis=AX.X, op=ALU.add))
            k.v(lambda e, hf=hf: e.tensor_copy(out=cmb[:, hf * 128:(hf + 1) * 128],
                                               in_=cmb_f[:]))
        cmb_vt = k.vt.n

        buf = ti % 2
        for oc in range(nch):
            for r in range(2):
                k.t(lambda e, oc=oc, r=r: e.matmul(
                    ps_y[:, 0:128], wp_sb[:, r * d + 128 * oc: r * d + 128 * (oc + 1)],
                    cmb[:, r * 128:(r + 1) * 128],
                    start=(r == 0), stop=(r == 1)),
                    waits=([(k.vt, cmb_vt), (k.vt, wp_vt), (k.st, k.st.n)]
                           if r == 0 else []),
                    milestone=(r == 1))
            k.v(lambda e: e.tensor_copy(out=ytmp[:], in_=ps_y[:, 0:128]),
                waits=[(k.tt, k.tt.n)])
            k.t(lambda e: e.transpose(out=ps_tr[:], in_=ytmp[:], identity=cm.ident[:]),
                waits=[k.vt], milestone=True)
            k.v(lambda e, oc=oc, buf=buf: e.tensor_tensor(
                out=y_sb[buf][:, 128 * oc:128 * (oc + 1)],
                in0=x_sb[buf][:, 128 * oc:128 * (oc + 1)], in1=ps_tr[:], op=ALU.add),
                waits=[k.tt, (xsem[buf], 16 * (ti // 2 + 1)), (yq[buf], yq[buf].n)])
        k.dma("sync", y_out[ti, :, :], y_sb[buf][:], yq[buf], waits=[k.vt])

    k.emit("sync", lambda e: e.nop(), waits=[(yq[0], yq[0].n), (yq[1], yq[1].n)])
    return k


from concourse.bass_utils import run_bass_kernel_spmd
import ml_dtypes

GEO = Geo(n_batch=2, cpb=4, n_slots=16, d=1024)
NC = list(range(8))


def _run(kb, in_maps, out_names):
    nc = kb.finalize()
    res = run_bass_kernel_spmd(nc, in_maps, core_ids=NC)
    out = [{nm: np.asarray(r[nm]) for nm in out_names} for r in res.results]
    kb.close()
    return out


def _shard_cols(tab, geo, core):
    b, cc = divmod(core, geo.cpb)
    R = tab.shape[0]
    t = tab.reshape(R, geo.n_tiles, 128)
    return np.ascontiguousarray(t[:, cc::geo.cpb].reshape(R, geo.T_local))


def _tg_of(geo, core):
    b, cc = divmod(core, geo.cpb)
    g = np.arange(geo.n_slots) * geo.cpb + cc
    return (g[:, None] * 128 + np.arange(128)[None, :]).astype(np.float32)


def _masks_sq(geo, cc, transpose_q):
    a = np.arange(128)[:, None]
    b = np.arange(128)[None, :]
    out = np.zeros((geo.cpb, 128, 128), np.float32)
    for p in range(geo.cpb):
        if transpose_q:
            out[p] = np.where(128 * (p - cc) + b > a, NEG, 0.0)
        else:
            out[p] = np.where(128 * (p - cc) + a > b, NEG, 0.0)
    return out


def _assemble_ktv(geo, resA, nkv):
    S = geo.S
    kt_g = np.zeros((geo.n_batch, nkv * 64, S), ml_dtypes.bfloat16)
    v_g = np.zeros((geo.n_batch, nkv, 128, geo.n_tiles * 65), ml_dtypes.bfloat16)
    for core in range(geo.n_cores):
        b, cc = divmod(core, geo.cpb)
        kt_g[b].reshape(nkv * 64, geo.n_tiles, 128)[:, cc::geo.cpb] = \
            resA[core]["kt"].reshape(nkv * 64, geo.n_slots, 128)
        v_g[b].reshape(nkv, 128, geo.n_tiles, 65)[:, :, cc::geo.cpb] = \
            resA[core]["v"].reshape(nkv, 128, geo.n_slots, 65)
    return kt_g, v_g


def _do_attn(geo, x, cfg, wmap, extras, gate=False, residual_ln=False):
    kb = build_attn_prep(geo, cfg, "prep")
    im = []
    for core in range(geo.n_cores):
        d = {"x": shard_tiles(x, geo, core)}
        d.update(wmap)
        for nm, fn in extras.get("prep", {}).items():
            d[nm] = fn(core)
        im.append(d)
    resA = _run(kb, im, ["qt", "kt", "v"])
    kt_g, v_g = _assemble_ktv(geo, resA, cfg.nkv)

    kb2 = build_attn_main(geo, cfg, "attnB", residual_ln=residual_ln, gate=gate)
    im2 = []
    for core in range(geo.n_cores):
        b, cc = divmod(core, geo.cpb)
        d = {"x": shard_tiles(x, geo, core), "qt": resA[core]["qt"],
             "kt": kt_g[b], "v": v_g[b], "wo": extras["wo"],
             "masks": _masks_sq(geo, cc, transpose_q=False)}
        if gate:
            d["ln2_g"] = extras["ln2_g"]
            d["ln2_b"] = extras["ln2_b"]
            d["agent_gate"] = extras["agent_gate"]
        im2.append(d)
    outs = ["y"] + (["aglast"] if gate else [])
    resB = _run(kb2, im2, outs)
    y = unshard_tiles([r["y"] for r in resB], geo)
    ag = resB[geo.cpb - 1]["aglast"].reshape(1, 10) if gate else None
    return y, ag


def _do_l0ffn(geo, x, ins, pre):
    cfg = AttnCfg(nq=16, nkv=4, dh=64, causal=True, scale=1.0 / 8.0,
                  mode="rope64", inner_ln=True)
    Ct, St = host_rope_head64(geo.S)
    swapT = host_swap64()
    wmap = {"w_q": ins["L0_ffn_wq"], "w_k": ins["L0_ffn_wk"], "w_v": ins["L0_ffn_wv"],
            "ln_g": ins["L0_ffn_ln1_g"], "ln_b": ins["L0_ffn_ln1_b"]}
    extras = {"wo": ins["L0_ffn_wo"],
              "ln2_g": ins["L0_ffn_ln2_g"], "ln2_b": ins["L0_ffn_ln2_b"],
              "agent_gate": ins["L0_ffn_agent_gate"],
              "prep": {"ropeC": lambda c: _shard_cols(Ct, geo, c),
                       "ropeS": lambda c: _shard_cols(St, geo, c),
                       "swapT": lambda c: swapT}}
    xmid, gate_row = _do_attn(geo, x, cfg, wmap, extras, gate=True, residual_ln=True)
    kb3 = build_l0moe(geo, "l0moe", n_ag=10, n_exp=20, Fm=2048)
    im3 = []
    for core in range(geo.n_cores):
        im3.append({"xmid": shard_tiles(xmid, geo, core),
                    "xorig": shard_tiles(x, geo, core),
                    "gate_row": gate_row,
                    "ln2_g": ins["L0_ffn_ln2_g"], "ln2_b": ins["L0_ffn_ln2_b"],
                    "agent_gate": ins["L0_ffn_agent_gate"],
                    "expert_gate": ins["L0_ffn_expert_gate"],
                    "role_emb": ins["L0_ffn_role_emb"],
                    "w1c": pre["l0w1c"], "w3c": pre["l0w3c"], "w2c": pre["l0w2c"]})
    resC = _run(kb3, im3, ["y"])
    return unshard_tiles([r["y"] for r in resC], geo)


def _do_l1ffn(geo, x, ins, pre):
    kb = build_l1ffn(geo, "l1ffn", n_exp=8, F=4096)
    im = []
    for core in range(geo.n_cores):
        im.append({"x": shard_tiles(x, geo, core),
                   "cen": ins["L1_ffn_centroids"],
                   "w1t": pre["l1w1t"], "w2t": pre["l1w2t"],
                   "b1p": pre["l1b1p"], "b2": ins["L1_ffn_b2"]})
    res = _run(kb, im, ["y"])
    return unshard_tiles([r["y"] for r in res], geo)


def _do_attnb(geo, x, ins):
    S = geo.S
    d = geo.d
    half = d // 2
    inv = 1.0 / 10000.0 ** (np.arange(half, dtype=np.float64) / half)
    ang = inv[:, None] * np.arange(S, dtype=np.float64)[None, :]
    Cf = np.cos(ang).astype(np.float32)
    Sf = np.sin(ang).astype(np.float32)

    def shard_rope(tab, core):
        b, cc = divmod(core, geo.cpb)
        nhc = half // 128
        t = tab.reshape(nhc, 128, geo.n_tiles, 128)[:, :, cc::geo.cpb]
        return np.ascontiguousarray(
            t.transpose(1, 0, 2, 3).reshape(128, nhc * geo.T_local))

    kb = build_attnb_prep(geo, "abprep")
    im = [{"x": shard_tiles(x, geo, core), "wi": ins["L0_attn_b_Wi"],
           "wp": ins["L0_attn_b_Wp"],
           "ropeC": shard_rope(Cf, core), "ropeS": shard_rope(Sf, core)}
          for core in range(geo.n_cores)]
    resA = _run(kb, im, ["it", "pt", "pn", "inn"])
    B = geo.n_batch
    pt_g = np.zeros((B, d, S), ml_dtypes.bfloat16)
    gtab = np.zeros((B, S + 1, d), ml_dtypes.bfloat16)
    for core in range(geo.n_cores):
        b, cc = divmod(core, geo.cpb)
        pt_g[b].reshape(d, geo.n_tiles, 128)[:, cc::geo.cpb] = \
            resA[core]["pt"].reshape(d, geo.n_slots, 128)
        gtab[b, :S].reshape(geo.n_tiles, 128, d)[cc::geo.cpb] = \
            resA[core]["pn"].reshape(geo.n_slots, 128, d)

    kb2 = build_attnb_topk(geo, "abtopk")
    im2 = []
    keeps = {}
    for core in range(geo.n_cores):
        b, cc = divmod(core, geo.cpb)
        tg = _tg_of(geo, core)
        keep = ((np.arange(16)[None, None, :] <= tg[:, :, None]) &
                (np.arange(16)[None, None, :] < 15)).astype(np.float32)
        keeps[core] = (tg, keep)
        im2.append({"it": resA[core]["it"], "pt": pt_g[b],
                    "masksq": _masks_sq(geo, cc, transpose_q=True), "keep": keep})
    resB = _run(kb2, im2, ["idxg"])

    kb3 = build_attnb_main(geo, "abmain")
    im3 = []
    for core in range(geo.n_cores):
        b, cc = divmod(core, geo.cpb)
        tg, keep = keeps[core]
        kneg = (1.0 - keep) * NEG
        im3.append({"x": shard_tiles(x, geo, core), "gtab": gtab[b],
                    "inn": resA[core]["inn"],
                    "idxg": resB[core]["idxg"].reshape(geo.n_slots, 128, 16),
                    "tg": tg[:, :, None].copy(), "keep": keep, "kneg": kneg,
                    "w1": ins["L0_attn_b_W1"], "w2": ins["L0_attn_b_W2"],
                    "wc": ins["L0_attn_b_Wc"], "wm": ins["L0_attn_b_Wm"],
                    "b1": ins["L0_attn_b_b1"], "b2": ins["L0_attn_b_b2"],
                    "bc": ins["L0_attn_b_bc"], "bm": ins["L0_attn_b_bm"],
                    "palette": np.ascontiguousarray(
                        ins["L0_attn_b_palette"].reshape(geo.d, 256)),
                    "wo": np.ascontiguousarray(ins["L0_attn_b_Wo"][0])})
    resC = _run(kb3, im3, ["y"])
    return unshard_tiles([r["y"] for r in resC], geo)


def _do_prepass(geo, ins):
    kb = build_prepass(geo, "prepass", F=4096, d=geo.d)
    jobs_a = [("W1", e) for e in range(20)] + [("W3", e) for e in range(20)]
    im = []
    for core in range(8):
        l0a = np.stack([ins["L0_ffn_" + t][e] for (t, e) in jobs_a[5 * core:5 * core + 5]])
        w2jobs = [ins["L0_ffn_W2"][e] for e in range(3 * core, min(20, 3 * core + 3))]
        while len(w2jobs) < 3:
            w2jobs.append(np.zeros((2 * geo.d, geo.d), np.float32))
        l0b = np.stack(w2jobs)
        im.append({"w1": ins["L1_ffn_W1"][core], "w2": ins["L1_ffn_W2"][core],
                   "gamma": ins["L1_ffn_gamma"][core], "beta": ins["L1_ffn_beta"][core],
                   "b1": ins["L1_ffn_b1"][core], "l0a": l0a, "l0b": l0b})
    res = _run(kb, im, ["w1t", "w2t", "b1p", "l0a_o", "l0b_o"])
    pre = {}
    pre["l1w1t"] = np.stack([res[c]["w1t"].reshape(geo.d, 4096) for c in range(8)])
    pre["l1w2t"] = np.stack([res[c]["w2t"].reshape(4096, geo.d) for c in range(8)])
    pre["l1b1p"] = np.stack([res[c]["b1p"].reshape(128, 32) for c in range(8)])
    F2 = 2 * geo.d
    l0w1c = np.zeros((20, geo.d, F2), ml_dtypes.bfloat16)
    l0w3c = np.zeros((20, geo.d, F2), ml_dtypes.bfloat16)
    l0w2c = np.zeros((20, F2, geo.d), ml_dtypes.bfloat16)
    for core in range(8):
        aa = res[core]["l0a_o"].reshape(5, geo.d, F2)
        for j, (t, e) in enumerate(jobs_a[5 * core:5 * core + 5]):
            (l0w1c if t == "W1" else l0w3c)[e] = aa[j]
        bb = res[core]["l0b_o"].reshape(3, F2, geo.d)
        for j, e in enumerate(range(3 * core, min(20, 3 * core + 3))):
            l0w2c[e] = bb[j]
    pre["l0w1c"] = l0w1c
    pre["l0w3c"] = l0w3c
    pre["l0w2c"] = l0w2c
    return pre


def kernel(**ins):
    geo = GEO
    x = np.ascontiguousarray(ins["L0_attn_a_x"], dtype=np.float32)
    pre = _do_prepass(geo, ins)

    pos = host_cosine_pos(geo.S, 64).T
    posT2 = np.concatenate([pos, pos], 0)
    cfg_a = AttnCfg(nq=16, nkv=16, dh=64, causal=True, scale=geo.d ** (-0.5),
                    mode="pos")
    x, _ = _do_attn(geo, x, cfg_a,
                    {"w_q": ins["L0_attn_a_Wq"], "w_k": ins["L0_attn_a_Wk"],
                     "w_v": ins["L0_attn_a_Wv"]},
                    {"wo": ins["L0_attn_a_Wo"],
                     "prep": {"posT": lambda c: _shard_cols(posT2, geo, c)}})
    x = _do_l0ffn(geo, x, ins, pre)
    x = _do_attnb(geo, x, ins)
    x = _do_l0ffn(geo, x, ins, pre)

    Ct, St = host_rope_head64(geo.S)
    swapT = host_swap64()
    cfg_1a = AttnCfg(nq=16, nkv=4, dh=64, causal=False, scale=1.0 / 8.0,
                     mode="rope64")
    x, _ = _do_attn(geo, x, cfg_1a,
                    {"w_q": ins["L1_attn_a_w_q"], "w_k": ins["L1_attn_a_w_k"],
                     "w_v": ins["L1_attn_a_w_v"]},
                    {"wo": ins["L1_attn_a_w_o"],
                     "prep": {"ropeC": lambda c: _shard_cols(Ct, geo, c),
                              "ropeS": lambda c: _shard_cols(St, geo, c),
                              "swapT": lambda c: swapT}})
    x = _do_l1ffn(geo, x, ins, pre)
    cfg_1b = AttnCfg(nq=16, nkv=16, dh=64, causal=True, scale=1.0 / 8.0,
                     mode="none", fused_qkv=True)
    x, _ = _do_attn(geo, x, cfg_1b, {"w_qkv": ins["L1_attn_b_w_qkv"]},
                    {"wo": ins["L1_attn_b_w_out"]})
    x = _do_l1ffn(geo, x, ins, pre)
    return x
